# version 20
# baseline (speedup 1.0000x reference)
import math
from contextlib import ExitStack

import numpy as np
import ml_dtypes

import concourse.bass as bass
import concourse.mybir as mybir
from concourse.bass_utils import run_bass_kernel_spmd

F32 = mybir.dt.float32
BF16 = mybir.dt.bfloat16
I32 = mybir.dt.int32
U8 = mybir.dt.uint8
AF = mybir.ActivationFunctionType
ALU = mybir.AluOpType
AX = mybir.AxisListType

NCORES = 8
D = 2048
TOK = 1024
HID = 5632
NHC = 44
NQ = 4
HCQ = NHC // NQ
NH = 32
NDH = 16
NITEMS = 17
LAMBDA_INIT = 0.8 - 0.6 * math.exp(-0.3 * 12)
EPS = 1e-5
NEGBIG = -1e9
BF = ml_dtypes.bfloat16
DEBUG = False


class Prog:
    ENGINES = ("sync", "gpsimd", "vector", "scalar", "tensor")

    def __init__(self):
        self.ops = {e: [] for e in self.ENGINES}
        self.cnt = {}

    QUEUES = ("qS", "qG", "qA")

    def emit(self, engine, fn, waits=(), inc=None, inc_by=1, fifo=True):
        ev = None
        w = [x for x in waits if x]
        if inc is not None:
            if fifo and inc in self.QUEUES and self.cnt.get(inc, 0) > 0:
                w.append((inc, self.cnt[inc]))
            self.cnt[inc] = self.cnt.get(inc, 0) + inc_by
            ev = (inc, self.cnt[inc])
        self.ops[engine].append((fn, tuple(w), inc, inc_by))
        return ev

    def bump(self, sem):
        self.cnt[sem] = self.cnt.get(sem, 0) + 1
        return (sem, self.cnt[sem])

    def replay(self, engine, eng, sems):
        for fn, waits, inc, inc_by in self.ops[engine]:
            for (s, c) in waits:
                eng.wait_ge(sems[s], c)
            ins = fn(eng, sems)
            if inc is not None:
                ins.then_inc(sems[inc], inc_by)


def build_nc():
    nc = bass.Bass("TRN2", target_bir_lowering=False, debug=False)

    dx = nc.dram_tensor("x_own", [TOK, D], F32, kind="ExternalInput").ap()
    dwqT = nc.dram_tensor("wqT", [D, D], BF16, kind="ExternalInput").ap()
    dwkT = nc.dram_tensor("wkT", [D, 512], BF16, kind="ExternalInput").ap()
    dwvT = nc.dram_tensor("wvT", [D, 512], BF16, kind="ExternalInput").ap()
    dwoT = nc.dram_tensor("woT", [D, D], BF16, kind="ExternalInput").ap()
    dwg8 = nc.dram_tensor("wg8", [NHC, 128, D], U8, kind="ExternalInput").ap()
    dwu8 = nc.dram_tensor("wu8", [NHC, 128, D], U8, kind="ExternalInput").ap()
    dwd8 = nc.dram_tensor("wd8", [HID, D], U8, kind="ExternalInput").ap()
    dsg = nc.dram_tensor("sgpp", [128, 48], F32, kind="ExternalInput").ap()
    dsu = nc.dram_tensor("supp", [128, 48], F32, kind="ExternalInput").ap()
    dsd = nc.dram_tensor("sdb", [128, D], F32, kind="ExternalInput").ap()
    dlnw = nc.dram_tensor("lnw", [128, 8], F32, kind="ExternalInput").ap()
    dlam = nc.dram_tensor("lam4", [1, 256], F32, kind="ExternalInput").ap()
    dident_b = nc.dram_tensor("ident_b", [128, 128], BF16, kind="ExternalInput").ap()
    dones_b = nc.dram_tensor("ones_b", [128, 128], BF16, kind="ExternalInput").ap()
    dzrow_b = nc.dram_tensor("zrow_b", [1, 512], BF16, kind="ExternalInput").ap()
    donesrow_b = nc.dram_tensor("onesrow_b", [1, 512], BF16, kind="ExternalInput").ap()
    donescol_f = nc.dram_tensor("onescol_f", [1, 128], F32, kind="ExternalInput").ap()
    donescol128 = nc.dram_tensor("onescol128", [128, 1], F32, kind="ExternalInput").ap()
    dmasks = nc.dram_tensor("masks", [128, 2048], BF16, kind="ExternalInput").ap()
    dcbias = nc.dram_tensor("cbias", [128, 8], F32, kind="ExternalInput").ap()
    ditab = nc.dram_tensor("itab", [1, 3 * NITEMS], I32, kind="ExternalInput").ap()
    dy = nc.dram_tensor("y", [TOK, D], F32, kind="ExternalOutput").ap()
    if DEBUG:
        dbg_xn = nc.dram_tensor("dbg_xn", [128, 16 * TOK], BF16, kind="ExternalOutput").ap()
        dbg_qfm = nc.dram_tensor("dbg_qfm", [128, 16 * TOK], BF16, kind="ExternalOutput").ap()
        dbg_kall = nc.dram_tensor("dbg_kall", [NCORES * 512, TOK], BF16, kind="ExternalOutput").ap()
        dbg_vall = nc.dram_tensor("dbg_vall", [NCORES * 4 * TOK, 128], BF16, kind="ExternalOutput").ap()
        dbg_acc = nc.dram_tensor("dbg_acc", [128, 8 * D], BF16, kind="ExternalOutput").ap()
        dbg_negm = nc.dram_tensor("dbg_negm", [128, 18], F32, kind="ExternalOutput").ap()
        dbg_ofm = nc.dram_tensor("dbg_ofm", [128, NDH * TOK], BF16, kind="ExternalOutput").ap()
        dbg_l = nc.dram_tensor("dbg_l", [2, 1024], F32, kind="ExternalOutput").ap()
        dbg_h0 = nc.dram_tensor("dbg_h0", [128, HCQ * TOK], BF16, kind="ExternalOutput").ap()
        dbg_acc0 = nc.dram_tensor("dbg_acc0", [128, 8 * D], BF16, kind="ExternalOutput").ap()
        dbg_kv = nc.dram_tensor("dbg_kv", [128, 2048], BF16, kind="ExternalOutput").ap()
        dbg_kv16 = nc.dram_tensor("dbg_kv16", [128, 2048], BF16, kind="ExternalOutput").ap()
        dbg_wo = nc.dram_tensor("dbg_wo", [128, 16 * D], BF16, kind="ExternalOutput").ap()
        dbg_accpre = nc.dram_tensor("dbg_accpre", [128, 8 * D], BF16, kind="ExternalOutput").ap()
        dbg_accw = nc.dram_tensor("dbg_accw", [128, 8 * D], BF16, kind="ExternalOutput").ap()
        dbg_o0 = nc.dram_tensor("dbg_o0", [128, 2048], BF16, kind="ExternalOutput").ap()
        dbg_l0 = nc.dram_tensor("dbg_l0", [2, 1024], F32, kind="ExternalOutput").ap()
        dbg_gate = nc.dram_tensor("dbg_gate", [128, 512], F32, kind="ExternalOutput").ap()
        dbg_up = nc.dram_tensor("dbg_up", [128, 512], F32, kind="ExternalOutput").ap()
        dbg_sil = nc.dram_tensor("dbg_sil", [128, 512], BF16, kind="ExternalOutput").ap()
        dbg_wgd = nc.dram_tensor("dbg_wgd", [128, D], BF16, kind="ExternalOutput").ap()

    kst_d = nc.dram_tensor("k_stage", [512, TOK], BF16, kind="Internal")
    vst_d = nc.dram_tensor("v_stage", [4 * TOK, 128], BF16, kind="Internal")
    knst_d = nc.dram_tensor("kn_stage", [1, 8], F32, kind="Internal")
    kall_d = nc.dram_tensor("k_all", [NCORES * 512, TOK], BF16, kind="Internal")
    vall_d = nc.dram_tensor("v_all", [NCORES * 4 * TOK, 128], BF16, kind="Internal")
    knall_d = nc.dram_tensor("kn_all", [1, 8], F32, kind="Internal")
    vcur_d = nc.dram_tensor("v_cur", [8 * TOK + 16, 128], BF16, kind="Internal")

    off = [16384]

    def sb(name, shape, dtype, at=None):
        nbytes = shape[1] * {F32: 4, BF16: 2, I32: 4, U8: 1}[dtype]
        if at is None:
            at = off[0]
            off[0] = at + ((nbytes + 31) // 32) * 32
        return nc.alloc_sbuf_tensor_at(name, shape, dtype, offset=at)

    ident_b = sb("ident_b", [128, 128], BF16)
    ones_b = sb("ones_b", [128, 128], BF16)
    zrow_b = sb("zrow_b", [1, 512], BF16)
    onesrow_b = sb("onesrow_b", [1, 512], BF16)
    onescol_f = sb("onescol_f", [1, 128], F32)
    onescol128 = sb("onescol128", [128, 1], F32)
    cbias = sb("cbias", [128, 8], F32)
    sdb = sb("sdb", [128, D], F32)
    sgpp = sb("sgpp", [128, 48], F32)
    supp = sb("supp", [128, 48], F32)
    lnw = sb("lnw", [128, 8], F32)
    lnw_s = sb("lnw_s", [128, 8], F32)
    lam4 = sb("lam4", [1, 256], F32)
    lscr = sb("lscr", [1, 128], F32)
    stats = sb("stats", [128, 16], F32)
    nmq = sb("nmq", [1, 32], F32)
    nmk = sb("nmk", [1, 8], F32)
    mrow = sb("mrow", [1, 32], F32)
    bcrow = sb("bcrow", [1, 18], F32)
    negmpp = sb("negmpp", [128, 24], F32)
    itab = sb("itab", [1, 3 * NITEMS], I32)
    rpp = sb("rpp", [128, 16], F32)
    rbn = sb("rbn", [128, 8], F32)
    ssq8 = sb("ssq8", [128, 8], F32)
    rst8 = sb("rst8", [128, 8], F32)
    acc = sb("acc", [128, 8 * D], BF16)
    C0 = off[0]
    ofm = sb("ofm", [128, NDH * TOK], BF16)
    wdr2 = sb("wdr2", [128, 512], BF16, at=C0 + 24576)
    wdr3 = sb("wdr3", [128, 512], BF16, at=C0 + 25600)
    wdd2 = sb("wdd2", [128, 512], BF16, at=C0 + 26624)
    wdd3 = sb("wdd3", [128, 512], BF16, at=C0 + 27648)
    D0 = off[0]
    xnfm = sb("xnfm", [128, 16 * TOK], BF16)
    ksb = sb("ksb", [64, 8704], BF16, at=D0)
    vsb = sb("vsb", [128, 8192], BF16, at=D0 + 16384)
    E0 = off[0]
    qfm = sb("qfm", [128, 16 * TOK], BF16)
    wo_all = sb("wo_all", [128, 16 * D], BF16, at=D0)
    assert E0 - D0 == 32 * 1024
    F0 = off[0]
    hfm = sb("hfm", [128, HCQ * TOK], BF16)
    pt = sb("pt", [128, 4096], BF16, at=F0)
    o1s = sb("o1s", [128, 1024], BF16, at=F0 + 8192)
    o2s = sb("o2s", [128, 1024], BF16, at=F0 + 10240)
    kstat = sb("kstat", [128, 1024], BF16, at=F0 + 12288)
    vstat = sb("vstat", [128, 1024], BF16, at=F0 + 14336)
    onb = sb("onb", [128, 1024], BF16, at=F0 + 16384)
    vstat2 = sb("vstat2", [128, 1024], BF16, at=F0 + 18432)
    kfm = sb("kfm", [128, 4096], BF16, at=F0)
    assert 22 * 1024 >= 20480
    G0 = off[0]
    xs0 = sb("xs0", [128, D], F32, at=G0)
    xs1 = sb("xs1", [128, D], F32, at=G0 + 8192)
    la_sb = sb("la_sb", [1, 1024], F32, at=G0)
    lb_sb = sb("lb_sb", [1, 1024], F32, at=G0 + 4096)
    ra_sb = sb("ra_sb", [1, 1024], F32, at=G0 + 8192)
    rb_sb = sb("rb_sb", [1, 1024], F32, at=G0 + 12288)
    wvsb = sb("wvsb", [128, 8192], BF16, at=G0 + 16384)
    out0 = sb("out0", [128, D], F32, at=G0 + 16384)
    out1 = sb("out1", [128, D], F32, at=G0 + 24576)
    masks = sb("masks", [128, 2048], BF16, at=G0 + 16384)
    odb = sb("odb", [128, 1024], F32, at=G0 + 24576)
    scr1 = sb("scr1", [128, 1024], F32, at=G0 + 28672)
    wqs0 = sb("wqs0", [128, D], BF16, at=G0 + 32768)
    wqs1 = sb("wqs1", [128, D], BF16, at=G0 + 36864)
    wgr0 = sb("wgr0", [128, D], BF16, at=G0)
    wgr1 = sb("wgr1", [128, D], BF16, at=G0 + 4096)
    wur0 = sb("wur0", [128, D], BF16, at=G0 + 8192)
    wur1 = sb("wur1", [128, D], BF16, at=G0 + 12288)
    wgd0 = sb("wgd0", [128, D], BF16, at=G0 + 16384)
    wgd1 = sb("wgd1", [128, D], BF16, at=G0 + 20480)
    wud0 = sb("wud0", [128, D], BF16, at=G0 + 24576)
    wud1 = sb("wud1", [128, D], BF16, at=G0 + 28672)
    wdr0 = sb("wdr0", [128, 512], BF16, at=G0 + 32768)
    wdr1 = sb("wdr1", [128, 512], BF16, at=G0 + 33792)
    wdd0 = sb("wdd0", [128, 512], BF16, at=G0 + 34816)
    wdd1 = sb("wdd1", [128, 512], BF16, at=G0 + 35840)
    ds_scr = sb("ds_scr", [128, 512], F32, at=G0 + 36864)
    sil_scr = sb("sil_scr", [128, 1024], BF16, at=G0 + 38912)
    G_end = G0 + 40960
    xc = sb("xc", [128, D], F32, at=F0)
    sqs = sb("sqs", [128, 2048], F32, at=F0 + 8192)
    sqsb = sb("sqsb", [128, 2048], BF16, at=F0 + 8192)
    total = max(G_end, off[0])
    assert total <= 229312, f"SBUF over budget: {total}"

    xcb = xc.bitcast(BF16)
    vtmb = sqs.bitcast(BF16)

    PST = nc.alloc_psum_tensor("PST", [128, 2048], F32)
    PCt = nc.alloc_psum_tensor("PCt", [128, 1024], F32)
    PDt = nc.alloc_psum_tensor("PDt", [128, 1024], F32)
    PSTb = PST.bitcast(BF16)
    PCb = PCt.bitcast(BF16)
    PDb = PDt.bitcast(BF16)

    class PView:
        def __init__(self, t, off, width):
            self.t, self.off, self.width = t, off, width

        def __getitem__(self, idx):
            p, f = idx
            a = 0 if f.start is None else f.start
            b = self.width if f.stop is None else f.stop
            return self.t[p, a + self.off:b + self.off]

    PA = PView(PST, 0, 1024)
    PB = PView(PST, 1024, 1024)
    PC = PView(PCt, 0, 1024)
    PD = PView(PDt, 0, 1024)
    PAb = PView(PSTb, 0, 2048)
    PBb = PView(PSTb, 2048, 2048)

    sem_names = ["qS", "qG", "qA", "sPE", "sDVE", "sACT", "cc"]
    p = Prog()

    def dma(q, out, in_, waits=(), fifo=True):
        eng = {"qS": "sync", "qG": "gpsimd", "qA": "scalar"}[q]
        return p.emit(eng, lambda e, s, o=out, i=in_: e.dma_start(out=o, in_=i),
                      waits=waits, inc=q, inc_by=16, fifo=fifo)

    def op(engine, fn, waits=(), ev=False):
        sem = {"vector": "sDVE", "scalar": "sACT", "tensor": "sPE"}[engine]
        return p.emit(engine, fn, waits=waits, inc=(sem if ev else None))

    ev_ident = dma("qA", ident_b[:, :], dident_b)
    ev_cbias = dma("qA", cbias[:, :], dcbias)
    for (dst, src) in [(ones_b, dones_b), (zrow_b, dzrow_b),
                       (onesrow_b, donesrow_b), (onescol_f, donescol_f),
                       (onescol128, donescol128),
                       (sdb, dsd), (sgpp, dsg), (supp, dsu),
                       (lnw, dlnw), (itab, ditab)]:
        dma("qA", dst[:, :], src)
    ev_consts = dma("qA", lam4[0:1, :], dlam)

    xs = [xs0, xs1]
    ev_xn_act = [None] * 8
    ev_trans = [None] * 8
    ev_xn_evac = [None] * 8
    for tc in range(8):
        sc = 8 * (tc % 2)
        t = xs[tc % 2]
        ev_xld = dma("qS", t[:, :], dx[128 * tc:128 * (tc + 1), :],
                     waits=([ev_xn_act[tc - 2]] if tc >= 2 else []))
        e1 = op("vector", lambda e, s, t=t, sc=sc: e.reduce_sum(
            stats[:, sc:sc + 1], t[:, :], AX.X, negate=True),
            waits=[ev_xld] + ([ev_xn_act[tc - 2]] if tc >= 2 else []), ev=True)
        e2 = op("vector", lambda e, s, t=t, sc=sc: e.scalar_tensor_tensor(
            sqs[:, 0:D], t[:, :], 1.0, t[:, :], ALU.bypass, ALU.mult,
            accum_out=stats[:, sc + 2:sc + 3]),
            waits=[ev_trans[tc - 1]] if tc >= 1 else [], ev=True)
        e3 = op("scalar", lambda e, s, sc=sc: e.activation(
            stats[:, sc + 1:sc + 2], stats[:, sc:sc + 1], AF.Copy,
            scale=1.0 / D), waits=[e1], ev=True)
        e4 = op("scalar", lambda e, s, sc=sc: e.activation(
            stats[:, sc + 3:sc + 4], stats[:, sc + 2:sc + 3], AF.Copy,
            scale=1.0 / D), waits=[e2], ev=True)
        e5 = op("vector", lambda e, s, sc=sc: e.scalar_tensor_tensor(
            stats[:, sc + 4:sc + 5], stats[:, sc + 1:sc + 2], 1.0,
            stats[:, sc + 1:sc + 2], ALU.bypass, ALU.mult),
            waits=[e3], ev=True)
        e6 = op("vector", lambda e, s, sc=sc: e.tensor_sub(
            stats[:, sc + 5:sc + 6], stats[:, sc + 3:sc + 4],
            stats[:, sc + 4:sc + 5]), waits=[e4, e5], ev=True)
        e7 = op("scalar", lambda e, s, sc=sc: e.activation(
            stats[:, sc + 6:sc + 7], stats[:, sc + 5:sc + 6], AF.Sqrt,
            bias=cbias[:, 0:1]),
            waits=[e6] + ([ev_cbias] if tc == 0 else []), ev=True)
        e8 = op("vector", lambda e, s, sc=sc: e.reciprocal(
            stats[:, sc + 7:sc + 8], stats[:, sc + 6:sc + 7]),
            waits=[e7], ev=True)
        e9 = op("vector", lambda e, s, sc=sc: e.scalar_tensor_tensor(
            stats[:, sc + 4:sc + 5], stats[:, sc + 1:sc + 2], 1.0,
            stats[:, sc + 7:sc + 8], ALU.bypass, ALU.mult),
            waits=[e8], ev=True)
        ev_xn_act[tc] = op("vector", lambda e, s, t=t, sc=sc: e.tensor_scalar(
            xcb[:, 0:D], t[:, :], stats[:, sc + 7:sc + 8],
            stats[:, sc + 4:sc + 5], ALU.mult, ALU.add),
            waits=[e9] + ([ev_trans[tc - 1]] if tc >= 1 else []), ev=True)
        for i in range(16):
            ev_trans[tc] = op("tensor", lambda e, s, i=i: e.transpose(
                PAb[:, 128 * i:128 * (i + 1)], xcb[:, 128 * i:128 * (i + 1)],
                ident_b[:, :]),
                waits=([ev_ident, ev_xn_act[tc]] if i == 0 else []) +
                      ([ev_xn_evac[tc - 1]] if (i == 0 and tc >= 1) else []),
                ev=(i == 15))
        ev_xn_evac[tc] = op("vector", lambda e, s, tc=tc: e.tensor_copy(
            xnfm[:, :].rearrange("p (i t) -> p i t", i=16)[:, :, 128 * tc:128 * (tc + 1)],
            PAb[:, :].rearrange("p (i t) -> p i t", i=16)),
            waits=[ev_trans[tc]], ev=True)
    last_xn_evac = ev_xn_evac[7]

    if DEBUG:
        dma("qA", dbg_xn, xnfm[:, :], waits=[last_xn_evac])

    wqs = [wqs0, wqs1]
    ev_qev = [None] * 16
    for oc in range(16):
        ev_wq = dma("qG", wqs[oc % 2][:, :].rearrange("p (i o) -> p i o", i=16),
                    dwqT[:, 128 * oc:128 * (oc + 1)]
                    .rearrange("(i p) o -> p i o", p=128),
                    waits=[ev_qev[oc - 2]] if oc >= 2 else [])
        for th in range(2):
            mm_ev = None
            for ic in range(16):
                mm_ev = op("tensor", lambda e, s, oc=oc, th=th, ic=ic: e.matmul(
                    PB[:, 512 * th:512 * (th + 1)],
                    wqs[oc % 2][:, 128 * ic:128 * (ic + 1)],
                    xnfm[:, 1024 * ic + 512 * th:1024 * ic + 512 * (th + 1)],
                    start=(ic == 0), stop=(ic == 15), skip_group_check=True),
                    waits=([ev_wq, last_xn_evac] if (th == 0 and ic == 0) else []) +
                          ([ev_qev[oc - 1]] if (oc >= 1 and th == 0 and ic == 0) else []),
                    ev=(ic == 15))
            ev_qev[oc] = op("scalar", lambda e, s, oc=oc, th=th: e.activation(
                qfm[:, 1024 * oc + 512 * th:1024 * oc + 512 * (th + 1)],
                PB[:, 512 * th:512 * (th + 1)], AF.Copy, scale=0.125),
                waits=[mm_ev], ev=True)
    ev_kev = [None] * 4
    ev_kst = [None] * 4
    for oc in range(4):
        ev_wk = dma("qG", wqs[oc % 2][:, :].rearrange("p (i o) -> p i o", i=16),
                    dwkT[:, 128 * oc:128 * (oc + 1)]
                    .rearrange("(i p) o -> p i o", p=128),
                    waits=[ev_qev[14 + oc]] if oc < 2 else [ev_kev[oc - 2]])
        for th in range(2):
            mm_ev = None
            for ic in range(16):
                mm_ev = op("tensor", lambda e, s, oc=oc, th=th, ic=ic: e.matmul(
                    PB[:, 512 * th:512 * (th + 1)],
                    wqs[oc % 2][:, 128 * ic:128 * (ic + 1)],
                    xnfm[:, 1024 * ic + 512 * th:1024 * ic + 512 * (th + 1)],
                    start=(ic == 0), stop=(ic == 15), skip_group_check=True),
                    waits=([ev_wk] if (th == 0 and ic == 0) else []) +
                          ([ev_qev[15] if oc == 0 else ev_kev[oc - 1]]
                           if (th == 0 and ic == 0) else []),
                    ev=(ic == 15))
            ev_kev[oc] = op("scalar", lambda e, s, oc=oc, th=th: e.activation(
                kfm[:, 1024 * oc + 512 * th:1024 * oc + 512 * (th + 1)],
                PB[:, 512 * th:512 * (th + 1)], AF.Copy), waits=[mm_ev], ev=True)
        ev_kst[oc] = dma("qA", kst_d[128 * oc:128 * (oc + 1), :],
                         kfm[:, 1024 * oc:1024 * (oc + 1)], waits=[ev_kev[oc]])
    ev_wv = dma("qG", wvsb[:, :].rearrange("p (i v) -> p i v", i=16),
                dwvT.rearrange("(i p) v -> p i v", p=128),
                waits=[ev_qev[15]])
    ev_vst = {}
    ev_vev = [None] * 8
    for tc in range(8):
        mm_ev = None
        for ic in range(16):
            mm_ev = op("tensor", lambda e, s, tc=tc, ic=ic: e.matmul(
                PC[:, 0:512],
                xnfm[:, 1024 * ic + 128 * tc:1024 * ic + 128 * (tc + 1)],
                wvsb[:, 512 * ic:512 * (ic + 1)],
                start=(ic == 0), stop=(ic == 15), skip_group_check=True),
                waits=([ev_wv] if (tc == 0 and ic == 0) else []) +
                      ([ev_vev[tc - 1]] if (tc >= 1 and ic == 0) else []),
                ev=(ic == 15))
        ev_vev[tc] = op("vector", lambda e, s, tc=tc: e.tensor_copy(
            vtmb[:, 512 * (tc % 2):512 * (tc % 2 + 1)], PC[:, 0:512]),
            waits=[mm_ev] + ([ev_vst[(tc - 2, 3)]] if tc >= 2 else []), ev=True)
        for vh in range(4):
            ev_vst[(tc, vh)] = dma(
                "qA", vst_d[1024 * vh + 128 * tc:1024 * vh + 128 * (tc + 1), :],
                vtmb[:, 512 * (tc % 2) + 128 * vh:512 * (tc % 2) + 128 * (vh + 1)],
                waits=[ev_vev[tc]] if vh == 0 else [])
    ev_vst_last = ev_vst[(7, 3)]


    ev_knorm = None
    for oc in range(4):
        esq = op("scalar", lambda e, s, oc=oc: e.activation(
            sqsb[:, 0:1024], kfm[:, 1024 * oc:1024 * (oc + 1)], AF.Square),
            waits=([ev_kev[oc], ev_vst_last] if oc == 0 else [ev_kev[oc]]),
            ev=True)
        for half in range(2):
            kh = 2 * oc + half
            op("tensor", lambda e, s, half=half: e.matmul(
                PD[0:1, 0:512], ones_b[64 * half:64 * half + 64, 0:1],
                sqsb[64 * half:64 * half + 64, 0:512],
                start=True, stop=True, skip_group_check=True),
                waits=[esq] + ([ev_knorm] if ev_knorm else []))
            m2 = op("tensor", lambda e, s, half=half: e.matmul(
                PD[0:1, 512:1024], ones_b[64 * half:64 * half + 64, 0:1],
                sqsb[64 * half:64 * half + 64, 512:1024],
                start=True, stop=True, skip_group_check=True), ev=True)
            ev_knorm = op("vector", lambda e, s, kh=kh: e.reduce_max(
                nmk[0:1, kh:kh + 1], PD[0:1, 0:1024], AX.X),
                waits=[m2], ev=True)
    ev_knst = dma("qA", knst_d.ap(), nmk[0:1, :], waits=[ev_knorm])
    ev_qnorm = None
    for oc in range(16):
        esq = op("scalar", lambda e, s, oc=oc: e.activation(
            sqsb[:, 0:1024], qfm[:, 1024 * oc:1024 * (oc + 1)], AF.Square),
            waits=([ev_knorm] if oc == 0 else []), ev=True)
        for half in range(2):
            h = 2 * oc + half
            op("tensor", lambda e, s, half=half: e.matmul(
                PD[0:1, 0:512], ones_b[64 * half:64 * half + 64, 0:1],
                sqsb[64 * half:64 * half + 64, 0:512],
                start=True, stop=True, skip_group_check=True),
                waits=[esq, ev_qnorm if ev_qnorm else ev_knorm])
            m2 = op("tensor", lambda e, s, half=half: e.matmul(
                PD[0:1, 512:1024], ones_b[64 * half:64 * half + 64, 0:1],
                sqsb[64 * half:64 * half + 64, 512:1024],
                start=True, stop=True, skip_group_check=True), ev=True)
            ev_qnorm = op("vector", lambda e, s, h=h: e.reduce_max(
                nmq[0:1, h:h + 1], PD[0:1, 0:1024], AX.X),
                waits=[m2], ev=True)

    rg = [list(range(NCORES))]
    p.emit("gpsimd", lambda e, s: e.collective_compute(
        "AllGather", ALU.bypass, replica_groups=rg,
        ins=[kst_d.ap().opt()], outs=[kall_d.ap().opt()]),
        waits=[ev_kst[3], ev_vst_last], inc="cc")
    p.emit("gpsimd", lambda e, s: e.collective_compute(
        "AllGather", ALU.bypass, replica_groups=rg,
        ins=[vst_d.ap().opt()], outs=[vall_d.ap().opt()]),
        waits=[ev_vst_last], inc="cc")
    p.emit("gpsimd", lambda e, s: e.collective_compute(
        "AllReduce", ALU.max, replica_groups=rg,
        ins=[knst_d.ap().opt()], outs=[knall_d.ap().opt()]),
        waits=[ev_knst], inc="cc")

    if DEBUG:
        dma("qA", dbg_qfm, qfm[:, :], waits=[ev_qev[15]])
        dma("qA", dbg_kall, kall_d.ap(), waits=[("cc", 3)])
        dma("qA", dbg_vall, vall_d.ap(), waits=[("cc", 3)])

    def emit_mlambda():
        ev_knall = dma("qA", nmk[0:1, :], knall_d.ap(), waits=[("cc", 3)])
        e_sq1 = op("scalar", lambda e, s: e.activation(
            nmk[0:1, :], nmk[0:1, :], AF.Sqrt), waits=[ev_knall], ev=True)
        e_sq2 = op("scalar", lambda e, s: e.activation(
            nmq[0:1, :], nmq[0:1, :], AF.Sqrt), waits=[ev_qnorm], ev=True)
        e_m = None
        for h in range(32):
            e_m = op("vector", lambda e, s, h=h: e.tensor_mul(
                mrow[0:1, h:h + 1], nmq[0:1, h:h + 1],
                nmk[0:1, h // 4:h // 4 + 1]),
                waits=([e_sq1, e_sq2] if h == 0 else []), ev=(h == 31))
        e_mp = op("vector", lambda e, s: e.tensor_tensor(
            bcrow[0:1, 0:16],
            mrow[0:1, :].rearrange("a (h two) -> a h two", two=2)[:, :, 0],
            mrow[0:1, :].rearrange("a (h two) -> a h two", two=2)[:, :, 1],
            ALU.max), waits=[e_m], ev=True)
        e_neg = op("vector", lambda e, s: e.tensor_scalar(
            bcrow[0:1, 0:16], bcrow[0:1, 0:16], -1.0, -1.0,
            ALU.mult, ALU.add), waits=[e_mp], ev=True)
        e_l1 = op("vector", lambda e, s: e.tensor_mul(
            lscr[0:1, 0:64], lam4[0:1, 0:64], lam4[0:1, 64:128]),
            waits=[ev_consts], ev=True)
        e_s1 = op("vector", lambda e, s: e.reduce_sum(
            lscr[0:1, 64:65], lscr[0:1, 0:64], AX.X), waits=[e_l1], ev=True)
        e_e1 = op("scalar", lambda e, s: e.activation(
            lscr[0:1, 66:67], lscr[0:1, 64:65], AF.Exp), waits=[e_s1], ev=True)
        e_l2 = op("vector", lambda e, s: e.tensor_mul(
            lscr[0:1, 0:64], lam4[0:1, 128:192], lam4[0:1, 192:256]),
            waits=[e_e1], ev=True)
        e_s2 = op("vector", lambda e, s: e.reduce_sum(
            lscr[0:1, 65:66], lscr[0:1, 0:64], AX.X), waits=[e_l2], ev=True)
        e_e2 = op("scalar", lambda e, s: e.activation(
            lscr[0:1, 68:69], lscr[0:1, 65:66], AF.Exp), waits=[e_s2], ev=True)
        e_lf = op("vector", lambda e, s: e.tensor_sub(
            lscr[0:1, 70:71], lscr[0:1, 66:67], lscr[0:1, 68:69]),
            waits=[e_e2], ev=True)
        e_lf2 = op("vector", lambda e, s: e.tensor_scalar(
            bcrow[0:1, 16:17], lscr[0:1, 70:71], -1.0, -LAMBDA_INIT,
            ALU.mult, ALU.add), waits=[e_lf], ev=True)
        return e_neg, e_lf2

    def emit_down_prefetch(q):
        evs = {}
        for j in range(4):
            jp = (8, 9, 10, 7)[j]
            wd_w = [] if q == 0 else [ev_ddq[(q - 1, 3, jp)]]
            evs[j] = dma("qG", wdr[j][:, :],
                         dwd8[128 * (HCQ * q + j):128 * (HCQ * q + j + 1),
                              0:512], waits=wd_w)
        return evs

    down_pre = {}
    mlam_evs = None
    wgr = [wgr0, wgr1]
    wur = [wur0, wur1]
    wgd = [wgd0, wgd1]
    wud = [wud0, wud1]
    wdr = [wdr0, wdr1, wdr2, wdr3]
    wdd = [wdd0, wdd1, wdd2, wdd3]
    ev_mmj_prev = None
    ev_h = [None] * NHC
    ev_gdq = [None] * NHC
    ev_udq = [None] * NHC
    ev_ddq = {}
    ev_acc = {}
    last_xn_read = None
    ev_es_first = True
    for q in range(NQ):
        if q == 0:
            down_pre[0] = emit_down_prefetch(0)
        if q == 1:
            mlam_evs = emit_mlambda()
        for j in range(HCQ):
            H = HCQ * q + j
            pp = H % 2
            wg_w = []
            if H == 0:
                wg_w = [last_xn_evac, ev_vst_last]
            elif H >= 2:
                wg_w = [ev_udq[H - 2]]
            ev_wg = dma("qG", wgr[pp][:, :], dwg8[H, :, :], waits=wg_w)
            ev_wu = dma("qG", wur[pp][:, :], dwu8[H, :, :])
            ev_gdq[H] = op("vector", lambda e, s, pp=pp: e.tensor_scalar(
                wgd[pp][:, :], wgr[pp][:, :], -128.0, None, ALU.add),
                waits=[ev_wg] + ([ev_h[H - 2]] if H >= 2 else []), ev=True)
            ev_udq[H] = op("vector", lambda e, s, pp=pp: e.tensor_scalar(
                wud[pp][:, :], wur[pp][:, :], -128.0, None, ALU.add),
                waits=[ev_wu], ev=True)
            for th in range(2):
                mw = []
                if th == 0:
                    mw = [ev_gdq[H], ev_udq[H]]
                    if j == 0:
                        mw.append(ev_qnorm if q == 0 else ev_acc[(q - 1, 3, 7)])
                        if DEBUG and q == 1:
                            mw.append(dbg_ffn_ev[0])
                    if H >= 1:
                        mw.append(ev_h[H - 1])
                mmg = mmu = None
                for ic in range(16):
                    mmg = op("tensor", lambda e, s, pp=pp, th=th, ic=ic: e.matmul(
                        PA[:, 512 * th:512 * (th + 1)],
                        wgd[pp][:, 128 * ic:128 * (ic + 1)],
                        xnfm[:, 1024 * ic + 512 * th:1024 * ic + 512 * (th + 1)],
                        start=(ic == 0), stop=(ic == 15), skip_group_check=True),
                        waits=(mw if ic == 0 else []), ev=(ic == 15))
                for ic in range(16):
                    mmu = op("tensor", lambda e, s, pp=pp, th=th, ic=ic: e.matmul(
                        PB[:, 512 * th:512 * (th + 1)],
                        wud[pp][:, 128 * ic:128 * (ic + 1)],
                        xnfm[:, 1024 * ic + 512 * th:1024 * ic + 512 * (th + 1)],
                        start=(ic == 0), stop=(ic == 15), skip_group_check=True),
                        ev=(ic == 15))
                if H == NHC - 1 and th == 1:
                    last_xn_read = mmu
                es_w = [mmg]
                if DEBUG and H == 0 and th == 0:
                    edw = op("vector", lambda e, s: e.tensor_copy(
                        ds_scr[:, :], PA[:, 0:512]), waits=[mmg], ev=True)
                    dma("qA", dbg_wgd, wgd0[:, :], waits=[ev_gdq[0]])
                    dbg_g_ev = dma("qA", dbg_gate, ds_scr[:, :], waits=[edw])
                    es_w.append(dbg_g_ev)
                    edw2 = op("vector", lambda e, s: e.tensor_copy(
                        ds_scr[:, :], PB[:, 0:512]), waits=[mmu, dbg_g_ev], ev=True)
                    dbg_u_ev = dma("qA", dbg_up, ds_scr[:, :], waits=[edw2])
                    es_w.append(dbg_u_ev)
                if ev_es_first:
                    es_w.append(ev_qev[15])
                    ev_es_first = False
                es = op("scalar", lambda e, s, pp=pp, th=th, H=H: e.activation(
                    sil_scr[:, 512 * th:512 * (th + 1)],
                    PA[:, 512 * th:512 * (th + 1)], AF.Silu,
                    scale=sgpp[:, H:H + 1], bias=cbias[:, 3:4]),
                    waits=es_w, ev=True)
                hw = [mmu, es]
                if DEBUG and H == 0 and th == 0:
                    hw.append(dma("qA", dbg_sil, sil_scr[:, 0:512], waits=[es]))
                ev_h[H] = op("vector",
                             lambda e, s, th=th, H=H, j=j: e.scalar_tensor_tensor(
                    hfm[:, 1024 * j + 512 * th:1024 * j + 512 * (th + 1)],
                    PB[:, 512 * th:512 * (th + 1)], supp[:, H:H + 1],
                    sil_scr[:, 512 * th:512 * (th + 1)],
                    ALU.mult, ALU.mult), waits=hw, ev=True)
        ps_slices = [(PA, 0), (PA, 512), (PB, 0), (PB, 512),
                     (PC, 0), (PC, 512), (PD, 0), (PD, 512)]
        for dd in range(4):
            mm_last = [None] * 8
            ev_mmj = {}
            for j in range(HCQ):
                H = HCQ * q + j
                pp = j % 4
                jp = (8, 9, 10, 7)[pp]
                if dd == 0 and j < 4:
                    evd = down_pre[q][j]
                else:
                    if j >= 4:
                        wd_w = [ev_ddq[(q, dd, j - 4)]]
                    else:
                        wd_w = [ev_ddq[(q, dd - 1, jp)]]
                    evd = dma("qG", wdr[pp][:, :],
                              dwd8[128 * H:128 * (H + 1),
                                   512 * dd:512 * (dd + 1)],
                              waits=wd_w)
                dq_w = [evd]
                if j >= 4:
                    dq_w.append(ev_mmj[j - 4])
                elif ev_mmj_prev is not None:
                    dq_w.append(ev_mmj_prev[jp])
                ev_ddq[(q, dd, j)] = op(
                    "vector", lambda e, s, pp=pp: e.tensor_scalar(
                        wdd[pp][:, :], wdr[pp][:, :], -128.0, None, ALU.add),
                    waits=dq_w, ev=True)
                for tc in range(8):
                    ps, o = ps_slices[tc]
                    dmw = []
                    if tc == 0:
                        dmw = [ev_ddq[(q, dd, j)]]
                    if j == 0 and tc == 0 and dd == 0:
                        dmw += [ev_h[HCQ * q + HCQ - 1]]
                        if q == 0:
                            dmw += [ev_qnorm, ev_vev[7]]
                    if j == 0:
                        if dd >= 1:
                            dmw += [ev_acc[(q, dd - 1, tc)]]
                        elif q >= 1:
                            dmw += [ev_acc[(q - 1, 3, tc)]]
                    mm_last[tc] = op("tensor",
                                     lambda e, s, ps=ps, o=o, pp=pp, j=j, tc=tc: e.matmul(
                        ps[:, o:o + 512],
                        hfm[:, 1024 * j + 128 * tc:1024 * j + 128 * (tc + 1)],
                        wdd[pp][:, :],
                        start=(j == 0), stop=(j == HCQ - 1),
                        skip_group_check=True),
                        waits=dmw, ev=(j == HCQ - 1 or tc == 7))
                ev_mmj[j] = mm_last[7]
            ev_mmj_prev = ev_mmj
            for tc in range(8):
                ps, o = ps_slices[tc]
                if q == 0:
                    ev_acc[(q, dd, tc)] = op(
                        "vector", lambda e, s, ps=ps, o=o, tc=tc, dd=dd: e.tensor_mul(
                            acc[:, D * tc + 512 * dd:D * tc + 512 * (dd + 1)],
                            ps[:, o:o + 512], sdb[:, 512 * dd:512 * (dd + 1)]),
                        waits=[mm_last[tc]], ev=True)
                else:
                    e1d = op("vector", lambda e, s, ps=ps, o=o, dd=dd: e.tensor_mul(
                        ds_scr[:, :], ps[:, o:o + 512],
                        sdb[:, 512 * dd:512 * (dd + 1)]),
                        waits=[mm_last[tc]], ev=True)
                    ev_acc[(q, dd, tc)] = op(
                        "vector", lambda e, s, tc=tc, dd=dd: e.tensor_add(
                            acc[:, D * tc + 512 * dd:D * tc + 512 * (dd + 1)],
                            acc[:, D * tc + 512 * dd:D * tc + 512 * (dd + 1)],
                            ds_scr[:, :]), waits=[e1d], ev=True)
        if q + 1 < NQ:
            down_pre[q + 1] = emit_down_prefetch(q + 1)
        if DEBUG and q == 0:
            dbg_ffn_ev = [None]
            e1dmp = dma("qA", dbg_h0, hfm[:, :], waits=[ev_h[HCQ - 1]])
            dbg_ffn_ev[0] = dma("qA", dbg_acc0, acc[:, :],
                                waits=[ev_acc[(0, 3, 7)]])
    ffn_last = ev_acc[(NQ - 1, 3, 7)]
    ev_masks = dma("qA", masks[:, :], dmasks, waits=[ev_h[NHC - 1]])

    e_neg, e_lf2 = mlam_evs
    e_bc = op("tensor", lambda e, s: e.matmul(
        PD[:, 0:18], onescol_f[0:1, :], bcrow[0:1, 0:18],
        start=True, stop=True, skip_group_check=True),
        waits=[e_neg, e_lf2, ffn_last], ev=True)
    ev_negmpp = op("vector", lambda e, s: e.tensor_copy(
        negmpp[:, 0:18], PD[:, 0:18]), waits=[e_bc], ev=True)
    ev_lnws = op("scalar", lambda e, s: e.activation(
        lnw_s[:, 0:1], lnw[:, 0:1], AF.Copy, scale=(1.0 - LAMBDA_INIT)),
        waits=[ev_consts], ev=True)

    if DEBUG:
        dma("qA", dbg_acc, acc[:, :], waits=[ffn_last])
        dma("qA", dbg_negm, negmpp[:, 0:18], waits=[ev_negmpp])

    PV = PC
    LL = PD

    def mk_frozen2(name, colbase, mx):
        st = {"r": None, "rvs": [None] * NITEMS}

        def get(e, i):
            if st["r"] is None:
                st["r"] = e.alloc_register(name)
            if st["rvs"][i] is None:
                e.reg_load(st["r"], itab[0:1, colbase + i:colbase + i + 1])
                st["rvs"][i] = bass.RuntimeValue(
                    st["r"], min_val=0, max_val=mx + i)
            return st["rvs"][i]
        return get

    ve_rv = mk_frozen2("koff", 0, 7680)
    sy_rv = mk_frozen2("voff", NITEMS, (8 * TOK - 512) * 128)
    pe_state = {"r": [None, None]}

    def pe_flag(e, i):
        if pe_state["r"][0] is None:
            pe_state["r"][0] = e.alloc_register("qflag0")
            pe_state["r"][1] = e.alloc_register("qflag1")
        r = pe_state["r"][i % 2]
        e.reg_load(r, itab[0:1, 2 * NITEMS + i:2 * NITEMS + i + 1])
        return r

    def pe_flag_cached(e, j):
        return pe_state["r"][j % 2]

    dbg_kv_ev = [None]
    ev_kcache = ev_vcache = None
    ev_kv = {}
    ev_vv = {}
    ev_stq = {}
    ev_stqA = {}
    ev_exp = {}
    ev_expA = {}
    ev_msk = {}
    ev_pv = {}
    ev_evO = {}
    ev_epi_done = None
    ev_epi_ps = None
    ev_epi_act = None
    ev_epi_tr = None
    er1_ev = None
    kcache_emitted = set()
    pending_epi = [None]
    for h in range(NH):
        d = h // 2
        kh = h // 4
        vh = h // 8
        isA = (h % 2 == 0)
        rb = 64 * (h % 2)
        if h % 4 == 0 and h not in kcache_emitted:
            kc_w = ([last_xn_read, ("cc", 3)] if h == 0
                    else [ev_kv[(h - 1, NITEMS - 1)]])
            for r in range(NCORES):
                ev_kcache = dma("qS", ksb[:, 1024 * r:1024 * (r + 1)],
                                kall_d.ap()[512 * r + 64 * kh:512 * r + 64 * (kh + 1), :],
                                waits=(kc_w if r == 0 else []))
        if h % 8 == 0:
            vc_w = [("cc", 3)] if h == 0 else [ev_vv[(h - 1, NITEMS - 1)]]
            for r in range(NCORES):
                ev_vcache = dma("qG",
                                vcur_d.ap()[1024 * r:1024 * (r + 1), :],
                                vall_d.ap()[4096 * r + 1024 * vh:4096 * r + 1024 * (vh + 1), :],
                                waits=(vc_w if r == 0 else []))
        if h == 0:
            stw0 = [ffn_last]
            pvw0 = [ev_negmpp]
        else:
            stw0 = [ev_exp[(h - 1, NITEMS - 1)]]
            pvw0 = [ev_evO[(h - 1, "o")], ev_evO[(h - 1, "l")]]

        def pvl_body(e, s, ev, ssl, vbuf, vo, qa, st):
            for c in range(4):
                e.matmul(PV[:, qa:qa + 512],
                         vbuf[:, vo + 128 * c:vo + 128 * (c + 1)],
                         pt[:, ssl + 512 * c:ssl + 512 * (c + 1)],
                         start=(st and c == 0), stop=False,
                         skip_group_check=True)
            for c in range(4):
                mm = e.matmul(LL[0:64, qa:qa + 512], ones_b[:, 0:64],
                              pt[:, ssl + 512 * c:ssl + 512 * (c + 1)],
                              start=(st and c == 0), stop=False,
                              skip_group_check=True)
            mm.then_inc(s[ev[0]], 1)

        def pvl(e, s, h, j, ev):
            ssl = 2048 * (j % 2)
            vbuf = vstat if (j % 4) < 2 else vstat2
            vo = 512 * (j % 2)
            if j < 2:
                pvl_body(e, s, ev, ssl, vbuf, vo, 512 * j, True)
                return
            rf = pe_flag_cached(e, j)
            for cond in (0, 1):
                qa = 512 * cond
                cm = e.If_eq(rf, 0) if cond == 0 else e.Else()
                with cm:
                    pvl_body(e, s, ev, ssl, vbuf, vo, qa, False)
            e.end_ifs()

        for i in range(NITEMS):
            sl = i % 2
            dwk = []
            if i == 0:
                dwk = [ev_kcache, ev_consts]
            if i < 2 and h > 0:
                dwk = dwk + [ev_stq[(h - 1, NITEMS - 2 + i)]]
            if i >= 2:
                dwk = [ev_stq[(h, i - 2)]]

            def kcopy(e, s, i=i, sl=sl, rb=rb):
                v = ve_rv(e, i)
                return e.tensor_copy(kstat[rb:rb + 64, 512 * sl:512 * (sl + 1)],
                                     ksb[:, bass.ds(v, 512)])
            if DEBUG and h in (0, 16) and i == 2:
                tgt = dbg_kv if h == 0 else dbg_kv16
                dma("qA", tgt[:, 0:1024], kstat[:, :],
                    waits=[ev_kv[(h, 1)], ev_vv[(h, 1)]])
                dbg_kv_ev[0] = dma("qA", tgt[:, 1024:2048], vstat[:, :])
                dwk = dwk + [dbg_kv_ev[0]]
            if (h, i) not in ev_kv:
                ev_kv[(h, i)] = op("vector", kcopy, waits=dwk, ev=True)

            if 0 <= i - 1 < 2:
                ev_msk[(h, i - 1)] = op(
                    "vector", lambda e, s, psl=(i - 1) % 2: e.tensor_mul(
                        pt[:, 2048 * psl:2048 * (psl + 1)],
                        pt[:, 2048 * psl:2048 * (psl + 1)], masks[:, 0:2048]),
                    waits=[ev_exp[(h, i - 1)]] +
                          ([ev_masks] if (h == 0 and i == 1) else []), ev=True)

            if i == 1 and h % 2 == 1:
                w1 = [ev_evO[(h - 1, "l")]]
                if ev_epi_tr is not None:
                    w1.append(ev_epi_tr)
                er1_ev = op("vector", lambda e, s: e.reciprocal(
                    ra_sb[0:1, :], la_sb[0:1, :]), waits=w1, ev=True)

            dwv = []
            if i == 0 and h == 0:
                dwv = [ev_consts]
            if i == 0 and h % 8 == 0:
                dwv = dwv + [ev_vcache]
            if i < 4 and h > 0:
                dwv = dwv + [ev_pv[(h - 1, (16, 13, 14, 15)[i])]]
            if i < 4 and h == 0:
                dwv = dwv + [ffn_last]
            if i >= 4:
                dwv = [ev_pv[(h, i - 4)]]
            if DEBUG and h in (0, 16) and i == 2:
                dwv = dwv + [dbg_kv_ev[0]]

            def vcopy(e, s, i=i, sl=sl):
                v = sy_rv(e, i)
                vbuf = vstat if (i % 4) < 2 else vstat2
                src = bass.AP(vcur_d, v, [[128, 128], [16384, 4], [1, 128]])
                return e.dma_start(
                    out=vbuf[:, 512 * sl:512 * (sl + 1)]
                    .rearrange("p (c v) -> p c v", c=4), in_=src)
            ev_vv[(h, i)] = p.emit("gpsimd", vcopy, waits=dwv, inc="qG",
                                   inc_by=16)

            stw = [ev_kv[(h, i)]]
            stwB = []
            if i == 0:
                stw += stw0
            if i >= 1:
                stw.append(ev_expA[(h, i - 1)])
                stwB.append(ev_exp[(h, i - 1)])
            ev_stqA[(h, i)] = p.bump("sPE")

            def stmmA(e, s, h=h, i=i, sl=sl, rb=rb, ev=ev_stqA[(h, i)]):
                base = 1024 * (h // 2)
                tp = (rb, 0) if rb else None
                rf = pe_flag(e, i)
                for cond in (0, 1):
                    qa = base + 512 * cond
                    cm = e.If_eq(rf, 0) if cond == 0 else e.Else()
                    with cm:
                        rhs = qfm[rb:rb + 64, qa:qa + 512]
                        for c in range(2):
                            mm = e.matmul(
                                PST[:, 512 * c:512 * (c + 1)],
                                kstat[rb:rb + 64,
                                      512 * sl + 128 * c:512 * sl + 128 * (c + 1)],
                                rhs, start=True, stop=True,
                                skip_group_check=True, tile_position=tp)
                        mm.then_inc(s[ev[0]], 1)
                e.end_ifs()
            p.emit("tensor", stmmA, waits=stw)
            ev_stq[(h, i)] = p.bump("sPE")

            def stmmB(e, s, h=h, i=i, sl=sl, rb=rb, ev=ev_stq[(h, i)]):
                base = 1024 * (h // 2)
                tp = (rb, 0) if rb else None
                rf = pe_flag_cached(e, i)
                for cond in (0, 1):
                    qa = base + 512 * cond
                    cm = e.If_eq(rf, 0) if cond == 0 else e.Else()
                    with cm:
                        rhs = qfm[rb:rb + 64, qa:qa + 512]
                        for c in range(2, 4):
                            mm = e.matmul(
                                PST[:, 512 * c:512 * (c + 1)],
                                kstat[rb:rb + 64,
                                      512 * sl + 128 * c:512 * sl + 128 * (c + 1)],
                                rhs, start=True, stop=True,
                                skip_group_check=True, tile_position=tp)
                        mm.then_inc(s[ev[0]], 1)
                e.end_ifs()
            p.emit("tensor", stmmB, waits=stwB)

            if i == 1 and pending_epi[0] is not None:
                pending_epi[0]()
                pending_epi[0] = None
            if i >= 1:
                ev_pv[(h, i - 1)] = p.bump("sPE")
                p.emit("tensor",
                       lambda e, s, h=h, j=i - 1, ev=ev_pv[(h, i - 1)]: pvl(
                           e, s, h, j, ev),
                       waits=[ev_vv[(h, i - 1)]] +
                             ([ev_msk[(h, i - 1)]] + pvw0 +
                              ([ev_epi_done] if ev_epi_done is not None
                               else [])
                              if i - 1 < 2 else []))

            ew = [ev_stqA[(h, i)]]
            if i >= 2:
                ew.append(ev_pv[(h, i - 2)])
            elif h > 0:
                ew.append(ev_pv[(h - 1, NITEMS - 2 + i)])
            ev_expA[(h, i)] = op("scalar", lambda e, s, sl=sl, d=d: e.activation(
                pt[:, 2048 * sl:2048 * sl + 1024], PST[:, 0:1024],
                AF.Exp, bias=negmpp[:, d:d + 1]), waits=ew, ev=True)
            ev_exp[(h, i)] = op("scalar", lambda e, s, sl=sl, d=d: e.activation(
                pt[:, 2048 * sl + 1024:2048 * sl + 2048], PST[:, 1024:2048],
                AF.Exp, bias=negmpp[:, d:d + 1]),
                waits=[ev_stq[(h, i)]], ev=True)
        ev_pv[(h, NITEMS - 1)] = p.bump("sPE")
        p.emit("tensor",
               lambda e, s, h=h, j=NITEMS - 1, ev=ev_pv[(h, NITEMS - 1)]: pvl(
                   e, s, h, j, ev),
               waits=[ev_exp[(h, NITEMS - 1)], ev_vv[(h, NITEMS - 1)]])

        odst = o1s if isA else o2s
        ldst = la_sb if isA else lb_sb
        evo_w = [ev_pv[(h, NITEMS - 1)]]
        if h == 0:
            evo_w.append(ffn_last)
        ev_evO[(h, "o")] = op("vector", lambda e, s, odst=odst: e.tensor_copy(
            odst[:, :], PV[:, 0:1024]), waits=evo_w, ev=True)
        ev_evO[(h, "l")] = op("vector", lambda e, s, ldst=ldst: e.tensor_copy(
            ldst[0:1, :], LL[0:1, 0:1024]), ev=True)
        if DEBUG and h == 1:
            dma("qA", dbg_l0[0:1, :], la_sb[0:1, :], waits=[ev_evO[(h, "l")]])
            dma("qA", dbg_l0[1:2, :], lb_sb[0:1, :])
            dma("qA", dbg_o0[:, 0:1024], o1s[:, :])
            dma("qA", dbg_o0[:, 1024:2048], o2s[:, :])

        if not isA:
            if h + 1 < NH:
                hn = h + 1
                if hn % 4 == 0:
                    kcache_emitted.add(hn)
                    khn = hn // 4
                    kc_w = [ev_kv[(h, NITEMS - 1)]]
                    for r in range(NCORES):
                        ev_kcache = dma(
                            "qS", ksb[:, 1024 * r:1024 * (r + 1)],
                            kall_d.ap()[512 * r + 64 * khn:512 * r + 64 * (khn + 1), :],
                            waits=(kc_w if r == 0 else []))
                for i2 in range(2):
                    dwk2 = [ev_stq[(h, NITEMS - 2 + i2)]]
                    if i2 == 0 and hn % 4 == 0:
                        dwk2.append(ev_kcache)

                    def kcopy2(e, s, i=i2, sl=i2 % 2):
                        v = ve_rv(e, i)
                        return e.tensor_copy(
                            kstat[0:64, 512 * sl:512 * (sl + 1)],
                            ksb[:, bass.ds(v, 512)])
                    ev_kv[(hn, i2)] = op("vector", kcopy2, waits=dwk2, ev=True)

            def emit_epilogue(h=h, d=d):
              nonlocal ev_epi_done, ev_epi_tr, ev_epi_act, ev_epi_ps
              er_w = [ev_evO[(h, "l")]]
              if ev_epi_tr is not None:
                  er_w.append(ev_epi_tr)
              er2a = op("vector", lambda e, s: e.reciprocal(
                  rb_sb[0:1, 0:512], lb_sb[0:1, 0:512]), waits=er_w, ev=True)
              er2b = op("vector", lambda e, s: e.reciprocal(
                  rb_sb[0:1, 512:1024], lb_sb[0:1, 512:1024]), ev=True)
              to_ev = None
              for qq in range(8):
                  to_ev = op("tensor", lambda e, s, qq=qq: e.transpose(
                      PCb[:, 128 * qq:128 * (qq + 1)],
                      o1s[:, 128 * qq:128 * (qq + 1)], ident_b[:, :]),
                      waits=([ev_evO[(h - 1, "o")], ev_evO[(h, "o")]]
                             if qq == 0 else []))
              for qq in range(8):
                  to_ev = op("tensor", lambda e, s, qq=qq: e.transpose(
                      PCb[:, 1024 + 128 * qq:1024 + 128 * (qq + 1)],
                      o2s[:, 128 * qq:128 * (qq + 1)], ident_b[:, :]),
                      ev=(qq == 7))
              tr_ev = None
              for qq in range(8):
                  tr_ev = op("tensor", lambda e, s, qq=qq: e.transpose(
                      PD[:, qq:qq + 1], ra_sb[0:1, 128 * qq:128 * (qq + 1)],
                      onescol_f[0:1, 0:1]),
                      waits=([er1_ev, ev_evO[(h, "l")]] if qq == 0 else []))
              for qq in range(8):
                  tr_ev = op("tensor", lambda e, s, qq=qq: e.transpose(
                      PD[:, 8 + qq:8 + qq + 1],
                      rb_sb[0:1, 128 * qq:128 * (qq + 1)], onescol_f[0:1, 0:1]),
                      waits=([er2a] if qq == 0 else ([er2b] if qq == 4 else [])),
                      ev=(qq == 7))
              ev_epi_tr = tr_ev
              e_rpp = op("vector", lambda e, s: e.tensor_copy(
                  rpp[:, :], PD[:, 0:16]), waits=[tr_ev], ev=True)
              e_rcp = e_rpp
              e_rbn = op("vector", lambda e, s: e.tensor_scalar(
                  rbn[:, :], rpp[:, 8:16], negmpp[:, 16:17], None, ALU.mult),
                  ev=True)
              e_oda = [None] * 8
              for qq in range(8):
                  e_oda[qq] = op("vector", lambda e, s, qq=qq: e.tensor_scalar(
                      scr1[:, 128 * qq:128 * (qq + 1)],
                      PCb[:, 128 * qq:128 * (qq + 1)], rpp[:, qq:qq + 1], None,
                      ALU.mult),
                      waits=([to_ev, e_rcp] if qq == 0 else []), ev=True)
              e_od = [None] * 8
              for qq in range(8):
                  e_od[qq] = op("vector", lambda e, s, qq=qq: e.scalar_tensor_tensor(
                      odb[:, 128 * qq:128 * (qq + 1)],
                      PCb[:, 1024 + 128 * qq:1024 + 128 * (qq + 1)], rbn[:, qq:qq + 1],
                      scr1[:, 128 * qq:128 * (qq + 1)], ALU.mult, ALU.add),
                      waits=[e_oda[qq]] + ([e_rbn] if qq == 0 else []), ev=True)
              e_ssq = None
              for qq in range(8):
                  e_ssq = op("vector", lambda e, s, qq=qq: e.scalar_tensor_tensor(
                      scr1[:, 128 * qq:128 * (qq + 1)],
                      odb[:, 128 * qq:128 * (qq + 1)], 1.0,
                      odb[:, 128 * qq:128 * (qq + 1)], ALU.bypass, ALU.mult,
                      accum_out=ssq8[:, qq:qq + 1]),
                      waits=[e_od[qq]], ev=(qq == 7))
              e_std = op("scalar", lambda e, s: e.activation(
                  ssq8[:, :], ssq8[:, :], AF.Sqrt, scale=1.0 / 128,
                  bias=cbias[:, 0:1]), waits=[e_ssq], ev=True)
              e_rst = op("vector", lambda e, s: e.reciprocal(
                  rst8[:, :], ssq8[:, :]), waits=[e_std], ev=True)
              e_on = None
              for qq in range(8):
                  e_on = op("vector", lambda e, s, qq=qq: e.tensor_scalar(
                      onb[:, 128 * qq:128 * (qq + 1)],
                      odb[:, 128 * qq:128 * (qq + 1)], rst8[:, qq:qq + 1], None,
                      ALU.mult),
                      waits=([e_rst] if qq == 0 else []), ev=(qq == 7))
              ev_epi_act = e_on
              tb_ev = None
              for qq in range(8):
                  tb_ev = op("tensor", lambda e, s, qq=qq: e.transpose(
                      PDb[:, 1024 + 128 * qq:1024 + 128 * (qq + 1)],
                      onb[:, 128 * qq:128 * (qq + 1)], ident_b[:, :]),
                      waits=([e_on] if qq == 0 else []), ev=(qq == 7))
              ev_epi_ps = tb_ev
              ev_epi_done = op("vector", lambda e, s, d=d: e.tensor_scalar(
                  ofm[:, 1024 * d:1024 * (d + 1)], PDb[:, 1024:2048],
                  lnw_s[:, 0:1], None, ALU.mult),
                  waits=[tb_ev] + ([ev_lnws] if d == 0 else []), ev=True)

            if h + 1 < NH:
                pending_epi[0] = emit_epilogue
            else:
                emit_epilogue()

    attn_done = ev_epi_done
    attn_kv_done = ev_kv[(NH - 1, NITEMS - 1)]
    if DEBUG:
        dma("qA", dbg_accpre, acc[:, :], waits=[attn_done])
        dma("qA", dbg_ofm, ofm[:, :], waits=[attn_done])
        dma("qA", dbg_l[0:1, :], la_sb[0:1, :], waits=[attn_done])
        dma("qA", dbg_l[1:2, :], lb_sb[0:1, :], waits=[attn_done])

    ev_stq_last = ev_stq[(NH - 1, NITEMS - 1)]
    ev_wo_q = {}
    for ic in range(16):
        q3 = ("qG", "qS", "qA")[ic % 3]
        w = [ev_stq_last, attn_kv_done] if ic < 3 else []
        ev_wo_q[q3] = dma(q3, wo_all[:, 2048 * ic:2048 * (ic + 1)],
                          dwoT[128 * ic:128 * (ic + 1), :], waits=w,
                          fifo=False)
    ev_wo_gate = list(ev_wo_q.values())
    ev_acc2 = {}
    oc_bank = {0: (PA, 0), 1: (PB, 0), 2: (PC, 0), 3: (PC, 512)}
    oc_order = (2, 3, 0, 1)
    for tc in range(8):
        for j_oc, oc in enumerate(oc_order):
            ps, o = oc_bank[oc]
            mm = None
            for ic in range(16):
                mw = []
                if ic == 0:
                    if tc == 0 and j_oc == 0:
                        mw = ev_wo_gate + [attn_done]
                    if tc >= 1:
                        mw.append(ev_acc2[(tc - 1, oc)])
                mm = op("tensor", lambda e, s, ps=ps, o=o, ic=ic, tc=tc, oc=oc: e.matmul(
                    ps[:, o:o + 512],
                    ofm[:, 1024 * ic + 128 * tc:1024 * ic + 128 * (tc + 1)],
                    wo_all[:, 2048 * ic + 512 * oc:2048 * ic + 512 * (oc + 1)],
                    start=(ic == 0), stop=(ic == 15), skip_group_check=True),
                    waits=mw, ev=(ic == 15))
            ev_acc2[(tc, oc)] = op(
                "vector", lambda e, s, ps=ps, o=o, tc=tc, oc=oc: e.tensor_add(
                    acc[:, D * tc + 512 * oc:D * tc + 512 * (oc + 1)],
                    acc[:, D * tc + 512 * oc:D * tc + 512 * (oc + 1)],
                    ps[:, o:o + 512]), waits=[mm], ev=True)

    if DEBUG:
        dma("qA", dbg_wo, wo_all[:, :], waits=ev_wo_gate)
        dma("qA", dbg_accw, acc[:, :], waits=[ev_acc2[(7, 3)]])

    outb = [out0, out1]
    ev_store = [None] * 8
    ev_add = [None] * 8
    for tc in range(8):
        w = [ev_acc2[(tc, oc)] for oc in range(4)]
        exld = dma("qA", xs[tc % 2][:, :], dx[128 * tc:128 * (tc + 1), :],
                   waits=w + ([ev_add[tc - 2]] if tc >= 2 else []))
        ev_add[tc] = op("vector", lambda e, s, tc=tc: e.tensor_add(
            outb[tc % 2][:, :], acc[:, D * tc:D * (tc + 1)], xs[tc % 2][:, :]),
            waits=[exld] + ([ev_store[tc - 2]] if tc >= 2 else []), ev=True)
        ev_store[tc] = dma("qS", dy[128 * tc:128 * (tc + 1), :],
                           outb[tc % 2][:, :], waits=[ev_add[tc]])
    final_qA = p.cnt["qA"]
    final_qS = p.cnt["qS"]
    p.emit("scalar", lambda e, s: e.nop(),
           waits=[("qA", final_qA), ("qS", final_qS)])

    with ExitStack() as ctx:
        sems = {n: ctx.enter_context(nc.semaphore(n)) for n in sem_names}
        with nc.Block() as block:
            @block.sync
            def _(eng):
                p.replay("sync", eng, sems)

            @block.gpsimd
            def _(eng):
                p.replay("gpsimd", eng, sems)

            @block.vector
            def _(eng):
                p.replay("vector", eng, sems)

            @block.scalar
            def _(eng):
                p.replay("scalar", eng, sems)

            @block.tensor
            def _(eng):
                p.replay("tensor", eng, sems)

    return nc


def _koff(b):
    r = b if b < 8 else 15 - b
    return 1024 * r + (512 if b >= 8 else 0)


def _item_table(core):
    bA, bB = core, 15 - core
    ko = [_koff(bA), _koff(bB)]
    qo = [0, 512]
    for t in range(bA):
        ko.append(_koff(t))
        qo.append(0)
    for t in range(bB):
        ko.append(_koff(t))
        qo.append(512)
    assert len(ko) == NITEMS
    vo = [128 * k for k in ko]
    return np.array([ko + vo + qo], dtype=np.int32)


def _masks():
    m = np.zeros((128, 2048), dtype=np.float32)
    qf = np.arange(512)[None, :]
    kp = np.arange(128)[:, None]
    for c in range(4):
        allow = (qf - kp) >= 128 * c
        m[:, 512 * c:512 * (c + 1)] = np.where(allow, 1.0, 0.0)
    return np.ascontiguousarray(m.astype(BF))


_NC_CACHE = {}


def kernel(L0_attn_x, L0_attn_Wq, L0_attn_Wk, L0_attn_Wv, L0_attn_Wo,
           L0_attn_lambda_q1, L0_attn_lambda_k1, L0_attn_lambda_q2,
           L0_attn_lambda_k2, L0_attn_ln_w,
           L0_ffn_wq_gate, L0_ffn_s_gate, L0_ffn_wq_up, L0_ffn_s_up,
           L0_ffn_wq_down, L0_ffn_s_down):
    x = np.ascontiguousarray(np.asarray(L0_attn_x, dtype=np.float32)
                             .reshape(8192, D))
    C = np.ascontiguousarray
    lam4 = np.concatenate([
        np.asarray(L0_attn_lambda_q1, np.float32).reshape(1, 64),
        np.asarray(L0_attn_lambda_k1, np.float32).reshape(1, 64),
        np.asarray(L0_attn_lambda_q2, np.float32).reshape(1, 64),
        np.asarray(L0_attn_lambda_k2, np.float32).reshape(1, 64)], axis=1)
    shared = {
        "wqT": C(np.asarray(L0_attn_Wq, np.float32).T.astype(BF)),
        "wkT": C(np.asarray(L0_attn_Wk, np.float32).T.astype(BF)),
        "wvT": C(np.asarray(L0_attn_Wv, np.float32).T.astype(BF)),
        "woT": C(np.asarray(L0_attn_Wo, np.float32).T.astype(BF)),
        "wg8": C(np.asarray(L0_ffn_wq_gate).reshape(NHC, 128, 16, 128)
                 .transpose(0, 3, 2, 1).reshape(NHC, 128, D).astype(np.uint8)),
        "wu8": C(np.asarray(L0_ffn_wq_up).reshape(NHC, 128, 16, 128)
                 .transpose(0, 3, 2, 1).reshape(NHC, 128, D).astype(np.uint8)),
        "wd8": C(np.asarray(L0_ffn_wq_down).T.astype(np.uint8)),
        "sgpp": C(np.pad(np.asarray(L0_ffn_s_gate, np.float32).reshape(NHC, 128).T, ((0, 0), (0, 4)))),
        "supp": C(np.pad(np.asarray(L0_ffn_s_up, np.float32).reshape(NHC, 128).T, ((0, 0), (0, 4)))),
        "sdb": C(np.broadcast_to(np.asarray(L0_ffn_s_down, np.float32)[None, :],
                                 (128, D))),
        "lnw": C(np.pad(np.asarray(L0_attn_ln_w, np.float32).reshape(128, 1), ((0, 0), (0, 7)))),
        "lam4": C(lam4),
        "ident_b": np.eye(128, dtype=BF),
        "ones_b": np.ones((128, 128), dtype=BF),
        "zrow_b": np.zeros((1, 512), dtype=BF),
        "onesrow_b": np.ones((1, 512), dtype=BF),
        "onescol_f": np.ones((1, 128), dtype=np.float32),
        "onescol128": np.ones((128, 1), dtype=np.float32),
        "cbias": np.broadcast_to(np.array(
            [EPS, -1.0, -LAMBDA_INIT, 0.0, 0.0, 0.0, 0.0, 0.0],
            dtype=np.float32)[None, :], (128, 8)).copy(),
        "masks": _masks(),
    }
    in_maps = []
    for c in range(NCORES):
        bA, bB = c, 15 - c
        m = dict(shared)
        m["x_own"] = C(np.concatenate(
            [x[512 * bA:512 * (bA + 1)], x[512 * bB:512 * (bB + 1)]], axis=0))
        m["itab"] = _item_table(c)
        in_maps.append(m)

    if "nc" not in _NC_CACHE:
        _NC_CACHE["nc"] = build_nc()
    nc = _NC_CACHE["nc"]
    res = run_bass_kernel_spmd(nc, in_maps, core_ids=list(range(NCORES)))

    out = np.empty((8192, D), dtype=np.float32)
    for c in range(NCORES):
        y = np.asarray(res.results[c]["y"], dtype=np.float32)
        out[512 * c:512 * (c + 1)] = y[0:512]
        out[512 * (15 - c):512 * (16 - c)] = y[512:1024]
    return out.reshape(1, 8192, D)


# version 21
# speedup vs baseline: 1.0044x; 1.0044x over previous
import math
from contextlib import ExitStack

import numpy as np
import ml_dtypes

import concourse.bass as bass
import concourse.mybir as mybir
from concourse.bass_utils import run_bass_kernel_spmd

F32 = mybir.dt.float32
BF16 = mybir.dt.bfloat16
I32 = mybir.dt.int32
U8 = mybir.dt.uint8
AF = mybir.ActivationFunctionType
ALU = mybir.AluOpType
AX = mybir.AxisListType

NCORES = 8
D = 2048
TOK = 1024
HID = 5632
NHC = 44
NQ = 4
HCQ = NHC // NQ
NH = 32
NDH = 16
NITEMS = 17
LAMBDA_INIT = 0.8 - 0.6 * math.exp(-0.3 * 12)
EPS = 1e-5
NEGBIG = -1e9
BF = ml_dtypes.bfloat16
DEBUG = False


class Prog:
    ENGINES = ("sync", "gpsimd", "vector", "scalar", "tensor")

    def __init__(self):
        self.ops = {e: [] for e in self.ENGINES}
        self.cnt = {}

    QUEUES = ("qS", "qG", "qA")

    def emit(self, engine, fn, waits=(), inc=None, inc_by=1, fifo=True):
        ev = None
        w = [x for x in waits if x]
        if inc is not None:
            if fifo and inc in self.QUEUES and self.cnt.get(inc, 0) > 0:
                w.append((inc, self.cnt[inc]))
            self.cnt[inc] = self.cnt.get(inc, 0) + inc_by
            ev = (inc, self.cnt[inc])
        self.ops[engine].append((fn, tuple(w), inc, inc_by))
        return ev

    def bump(self, sem):
        self.cnt[sem] = self.cnt.get(sem, 0) + 1
        return (sem, self.cnt[sem])

    def replay(self, engine, eng, sems):
        for fn, waits, inc, inc_by in self.ops[engine]:
            for (s, c) in waits:
                eng.wait_ge(sems[s], c)
            ins = fn(eng, sems)
            if inc is not None:
                ins.then_inc(sems[inc], inc_by)


def build_nc():
    nc = bass.Bass("TRN2", target_bir_lowering=False, debug=False)

    dx = nc.dram_tensor("x_own", [TOK, D], F32, kind="ExternalInput").ap()
    dwqT = nc.dram_tensor("wqT", [D, D], BF16, kind="ExternalInput").ap()
    dwkT = nc.dram_tensor("wkT", [D, 512], BF16, kind="ExternalInput").ap()
    dwvT = nc.dram_tensor("wvT", [D, 512], BF16, kind="ExternalInput").ap()
    dwoT = nc.dram_tensor("woT", [D, D], BF16, kind="ExternalInput").ap()
    dwg8 = nc.dram_tensor("wg8", [NHC, 128, D], U8, kind="ExternalInput").ap()
    dwu8 = nc.dram_tensor("wu8", [NHC, 128, D], U8, kind="ExternalInput").ap()
    dwd8 = nc.dram_tensor("wd8", [HID, D], U8, kind="ExternalInput").ap()
    dsg = nc.dram_tensor("sgpp", [128, 48], F32, kind="ExternalInput").ap()
    dsu = nc.dram_tensor("supp", [128, 48], F32, kind="ExternalInput").ap()
    dsd = nc.dram_tensor("sdb", [128, D], F32, kind="ExternalInput").ap()
    dlnw = nc.dram_tensor("lnw", [128, 8], F32, kind="ExternalInput").ap()
    dlam = nc.dram_tensor("lam4", [1, 256], F32, kind="ExternalInput").ap()
    dident_b = nc.dram_tensor("ident_b", [128, 128], BF16, kind="ExternalInput").ap()
    dones_b = nc.dram_tensor("ones_b", [128, 128], BF16, kind="ExternalInput").ap()
    dzrow_b = nc.dram_tensor("zrow_b", [1, 512], BF16, kind="ExternalInput").ap()
    donesrow_b = nc.dram_tensor("onesrow_b", [1, 512], BF16, kind="ExternalInput").ap()
    donescol_f = nc.dram_tensor("onescol_f", [1, 128], F32, kind="ExternalInput").ap()
    donescol128 = nc.dram_tensor("onescol128", [128, 1], F32, kind="ExternalInput").ap()
    dmasks = nc.dram_tensor("masks", [128, 2048], BF16, kind="ExternalInput").ap()
    dcbias = nc.dram_tensor("cbias", [128, 8], F32, kind="ExternalInput").ap()
    ditab = nc.dram_tensor("itab", [1, 3 * NITEMS], I32, kind="ExternalInput").ap()
    dy = nc.dram_tensor("y", [TOK, D], F32, kind="ExternalOutput").ap()
    if DEBUG:
        dbg_xn = nc.dram_tensor("dbg_xn", [128, 16 * TOK], BF16, kind="ExternalOutput").ap()
        dbg_qfm = nc.dram_tensor("dbg_qfm", [128, 16 * TOK], BF16, kind="ExternalOutput").ap()
        dbg_kall = nc.dram_tensor("dbg_kall", [NCORES * 512, TOK], BF16, kind="ExternalOutput").ap()
        dbg_vall = nc.dram_tensor("dbg_vall", [NCORES * 4 * TOK, 128], BF16, kind="ExternalOutput").ap()
        dbg_acc = nc.dram_tensor("dbg_acc", [128, 8 * D], BF16, kind="ExternalOutput").ap()
        dbg_negm = nc.dram_tensor("dbg_negm", [128, 18], F32, kind="ExternalOutput").ap()
        dbg_ofm = nc.dram_tensor("dbg_ofm", [128, NDH * TOK], BF16, kind="ExternalOutput").ap()
        dbg_l = nc.dram_tensor("dbg_l", [2, 1024], F32, kind="ExternalOutput").ap()
        dbg_h0 = nc.dram_tensor("dbg_h0", [128, HCQ * TOK], BF16, kind="ExternalOutput").ap()
        dbg_acc0 = nc.dram_tensor("dbg_acc0", [128, 8 * D], BF16, kind="ExternalOutput").ap()
        dbg_kv = nc.dram_tensor("dbg_kv", [128, 2048], BF16, kind="ExternalOutput").ap()
        dbg_kv16 = nc.dram_tensor("dbg_kv16", [128, 2048], BF16, kind="ExternalOutput").ap()
        dbg_wo = nc.dram_tensor("dbg_wo", [128, 16 * D], BF16, kind="ExternalOutput").ap()
        dbg_accpre = nc.dram_tensor("dbg_accpre", [128, 8 * D], BF16, kind="ExternalOutput").ap()
        dbg_accw = nc.dram_tensor("dbg_accw", [128, 8 * D], BF16, kind="ExternalOutput").ap()
        dbg_o0 = nc.dram_tensor("dbg_o0", [128, 2048], BF16, kind="ExternalOutput").ap()
        dbg_l0 = nc.dram_tensor("dbg_l0", [2, 1024], F32, kind="ExternalOutput").ap()
        dbg_gate = nc.dram_tensor("dbg_gate", [128, 512], F32, kind="ExternalOutput").ap()
        dbg_up = nc.dram_tensor("dbg_up", [128, 512], F32, kind="ExternalOutput").ap()
        dbg_sil = nc.dram_tensor("dbg_sil", [128, 512], BF16, kind="ExternalOutput").ap()
        dbg_wgd = nc.dram_tensor("dbg_wgd", [128, D], BF16, kind="ExternalOutput").ap()

    kst_d = nc.dram_tensor("k_stage", [512, TOK], BF16, kind="Internal")
    vst_d = nc.dram_tensor("v_stage", [4 * TOK, 128], BF16, kind="Internal")
    knst_d = nc.dram_tensor("kn_stage", [1, 8], F32, kind="Internal")
    kall_d = nc.dram_tensor("k_all", [NCORES * 512, TOK], BF16, kind="Internal")
    vall_d = nc.dram_tensor("v_all", [NCORES * 4 * TOK, 128], BF16, kind="Internal")
    knall_d = nc.dram_tensor("kn_all", [1, 8], F32, kind="Internal")
    vcur_d = nc.dram_tensor("v_cur", [8 * TOK + 16, 128], BF16, kind="Internal")

    off = [16384]

    def sb(name, shape, dtype, at=None):
        nbytes = shape[1] * {F32: 4, BF16: 2, I32: 4, U8: 1}[dtype]
        if at is None:
            at = off[0]
            off[0] = at + ((nbytes + 31) // 32) * 32
        return nc.alloc_sbuf_tensor_at(name, shape, dtype, offset=at)

    ident_b = sb("ident_b", [128, 128], BF16)
    ones_b = sb("ones_b", [128, 128], BF16)
    zrow_b = sb("zrow_b", [1, 512], BF16)
    onesrow_b = sb("onesrow_b", [1, 512], BF16)
    onescol_f = sb("onescol_f", [1, 128], F32)
    onescol128 = sb("onescol128", [128, 1], F32)
    cbias = sb("cbias", [128, 8], F32)
    sdb = sb("sdb", [128, D], F32)
    sgpp = sb("sgpp", [128, 48], F32)
    supp = sb("supp", [128, 48], F32)
    lnw = sb("lnw", [128, 8], F32)
    lnw_s = sb("lnw_s", [128, 8], F32)
    lam4 = sb("lam4", [1, 256], F32)
    lscr = sb("lscr", [1, 128], F32)
    stats = sb("stats", [128, 16], F32)
    nmq = sb("nmq", [1, 32], F32)
    nmk = sb("nmk", [1, 8], F32)
    mrow = sb("mrow", [1, 32], F32)
    bcrow = sb("bcrow", [1, 18], F32)
    negmpp = sb("negmpp", [128, 24], F32)
    itab = sb("itab", [1, 3 * NITEMS], I32)
    rpp = sb("rpp", [128, 16], F32)
    rbn = sb("rbn", [128, 8], F32)
    ssq8 = sb("ssq8", [128, 8], F32)
    rst8 = sb("rst8", [128, 8], F32)
    acc = sb("acc", [128, 8 * D], BF16)
    C0 = off[0]
    ofm = sb("ofm", [128, NDH * TOK], BF16)
    wdr2 = sb("wdr2", [128, 512], BF16, at=C0 + 24576)
    wdr3 = sb("wdr3", [128, 512], BF16, at=C0 + 25600)
    wdd2 = sb("wdd2", [128, 512], BF16, at=C0 + 26624)
    wdd3 = sb("wdd3", [128, 512], BF16, at=C0 + 27648)
    D0 = off[0]
    xnfm = sb("xnfm", [128, 16 * TOK], BF16)
    ksb = sb("ksb", [64, 8704], BF16, at=D0)
    vsb = sb("vsb", [128, 8192], BF16, at=D0 + 16384)
    E0 = off[0]
    qfm = sb("qfm", [128, 16 * TOK], BF16)
    wo_all = sb("wo_all", [128, 16 * D], BF16, at=D0)
    assert E0 - D0 == 32 * 1024
    F0 = off[0]
    hfm = sb("hfm", [128, HCQ * TOK], BF16)
    pt = sb("pt", [128, 4096], BF16, at=F0)
    o1s = sb("o1s", [128, 1024], BF16, at=F0 + 8192)
    o2s = sb("o2s", [128, 1024], BF16, at=F0 + 10240)
    kstat = sb("kstat", [128, 1024], BF16, at=F0 + 12288)
    vstat = sb("vstat", [128, 1024], BF16, at=F0 + 14336)
    onb = sb("onb", [128, 1024], BF16, at=F0 + 16384)
    vstat2 = sb("vstat2", [128, 1024], BF16, at=F0 + 18432)
    kfm = sb("kfm", [128, 4096], BF16, at=F0)
    assert 22 * 1024 >= 20480
    G0 = off[0]
    xs0 = sb("xs0", [128, D], F32, at=G0)
    xs1 = sb("xs1", [128, D], F32, at=G0 + 8192)
    la_sb = sb("la_sb", [1, 1024], F32, at=G0)
    lb_sb = sb("lb_sb", [1, 1024], F32, at=G0 + 4096)
    ra_sb = sb("ra_sb", [1, 1024], F32, at=G0 + 8192)
    rb_sb = sb("rb_sb", [1, 1024], F32, at=G0 + 12288)
    wvsb = sb("wvsb", [128, 8192], BF16, at=G0 + 16384)
    out0 = sb("out0", [128, D], F32, at=G0 + 16384)
    out1 = sb("out1", [128, D], F32, at=G0 + 24576)
    masks = sb("masks", [128, 2048], BF16, at=G0 + 16384)
    odb = sb("odb", [128, 1024], F32, at=G0 + 24576)
    scr1 = sb("scr1", [128, 1024], F32, at=G0 + 28672)
    wqs0 = sb("wqs0", [128, D], BF16, at=G0 + 32768)
    wqs1 = sb("wqs1", [128, D], BF16, at=G0 + 36864)
    wgr0 = sb("wgr0", [128, D], BF16, at=G0)
    wgr1 = sb("wgr1", [128, D], BF16, at=G0 + 4096)
    wur0 = sb("wur0", [128, D], BF16, at=G0 + 8192)
    wur1 = sb("wur1", [128, D], BF16, at=G0 + 12288)
    wgd0 = sb("wgd0", [128, D], BF16, at=G0 + 16384)
    wgd1 = sb("wgd1", [128, D], BF16, at=G0 + 20480)
    wud0 = sb("wud0", [128, D], BF16, at=G0 + 24576)
    wud1 = sb("wud1", [128, D], BF16, at=G0 + 28672)
    wdr0 = sb("wdr0", [128, 512], BF16, at=G0 + 32768)
    wdr1 = sb("wdr1", [128, 512], BF16, at=G0 + 33792)
    wdd0 = sb("wdd0", [128, 512], BF16, at=G0 + 34816)
    wdd1 = sb("wdd1", [128, 512], BF16, at=G0 + 35840)
    ds_scr = sb("ds_scr", [128, 512], F32, at=G0 + 36864)
    sil_scr = sb("sil_scr", [128, 1024], BF16, at=G0 + 38912)
    G_end = G0 + 40960
    xc = sb("xc", [128, D], F32, at=F0)
    sqs = sb("sqs", [128, 2048], F32, at=F0 + 8192)
    sqsb = sb("sqsb", [128, 2048], BF16, at=F0 + 8192)
    total = max(G_end, off[0])
    assert total <= 229312, f"SBUF over budget: {total}"

    xcb = xc.bitcast(BF16)
    vtmb = sqs.bitcast(BF16)

    PST = nc.alloc_psum_tensor("PST", [128, 2048], F32)
    PCt = nc.alloc_psum_tensor("PCt", [128, 1024], F32)
    PDt = nc.alloc_psum_tensor("PDt", [128, 1024], F32)
    PSTb = PST.bitcast(BF16)
    PCb = PCt.bitcast(BF16)
    PDb = PDt.bitcast(BF16)

    class PView:
        def __init__(self, t, off, width):
            self.t, self.off, self.width = t, off, width

        def __getitem__(self, idx):
            p, f = idx
            a = 0 if f.start is None else f.start
            b = self.width if f.stop is None else f.stop
            return self.t[p, a + self.off:b + self.off]

    PA = PView(PST, 0, 1024)
    PB = PView(PST, 1024, 1024)
    PC = PView(PCt, 0, 1024)
    PD = PView(PDt, 0, 1024)
    PAb = PView(PSTb, 0, 2048)
    PBb = PView(PSTb, 2048, 2048)

    sem_names = ["qS", "qG", "qA", "sPE", "sDVE", "sACT", "cc"]
    p = Prog()

    def dma(q, out, in_, waits=(), fifo=True):
        eng = {"qS": "sync", "qG": "gpsimd", "qA": "scalar"}[q]
        return p.emit(eng, lambda e, s, o=out, i=in_: e.dma_start(out=o, in_=i),
                      waits=waits, inc=q, inc_by=16, fifo=fifo)

    def op(engine, fn, waits=(), ev=False):
        sem = {"vector": "sDVE", "scalar": "sACT", "tensor": "sPE"}[engine]
        return p.emit(engine, fn, waits=waits, inc=(sem if ev else None))

    ev_ident = dma("qA", ident_b[:, :], dident_b)
    ev_cbias = dma("qA", cbias[:, :], dcbias)
    for (dst, src) in [(ones_b, dones_b), (zrow_b, dzrow_b),
                       (onesrow_b, donesrow_b), (onescol_f, donescol_f),
                       (onescol128, donescol128),
                       (sdb, dsd), (sgpp, dsg), (supp, dsu),
                       (lnw, dlnw), (itab, ditab)]:
        dma("qA", dst[:, :], src)
    ev_consts = dma("qA", lam4[0:1, :], dlam)

    xs = [xs0, xs1]
    ev_xn_act = [None] * 8
    ev_trans = [None] * 8
    ev_xn_evac = [None] * 8
    for tc in range(8):
        sc = 8 * (tc % 2)
        t = xs[tc % 2]
        ev_xld = dma("qS", t[:, :], dx[128 * tc:128 * (tc + 1), :],
                     waits=([ev_xn_act[tc - 2]] if tc >= 2 else []))
        e1 = op("vector", lambda e, s, t=t, sc=sc: e.reduce_sum(
            stats[:, sc:sc + 1], t[:, :], AX.X, negate=True),
            waits=[ev_xld] + ([ev_xn_act[tc - 2]] if tc >= 2 else []), ev=True)
        e2 = op("vector", lambda e, s, t=t, sc=sc: e.scalar_tensor_tensor(
            sqs[:, 0:D], t[:, :], 1.0, t[:, :], ALU.bypass, ALU.mult,
            accum_out=stats[:, sc + 2:sc + 3]),
            waits=[ev_trans[tc - 1]] if tc >= 1 else [], ev=True)
        e3 = op("scalar", lambda e, s, sc=sc: e.activation(
            stats[:, sc + 1:sc + 2], stats[:, sc:sc + 1], AF.Copy,
            scale=1.0 / D), waits=[e1], ev=True)
        e4 = op("scalar", lambda e, s, sc=sc: e.activation(
            stats[:, sc + 3:sc + 4], stats[:, sc + 2:sc + 3], AF.Copy,
            scale=1.0 / D), waits=[e2], ev=True)
        e5 = op("vector", lambda e, s, sc=sc: e.scalar_tensor_tensor(
            stats[:, sc + 4:sc + 5], stats[:, sc + 1:sc + 2], 1.0,
            stats[:, sc + 1:sc + 2], ALU.bypass, ALU.mult),
            waits=[e3], ev=True)
        e6 = op("vector", lambda e, s, sc=sc: e.tensor_sub(
            stats[:, sc + 5:sc + 6], stats[:, sc + 3:sc + 4],
            stats[:, sc + 4:sc + 5]), waits=[e4, e5], ev=True)
        e7 = op("scalar", lambda e, s, sc=sc: e.activation(
            stats[:, sc + 6:sc + 7], stats[:, sc + 5:sc + 6], AF.Sqrt,
            bias=cbias[:, 0:1]),
            waits=[e6] + ([ev_cbias] if tc == 0 else []), ev=True)
        e8 = op("vector", lambda e, s, sc=sc: e.reciprocal(
            stats[:, sc + 7:sc + 8], stats[:, sc + 6:sc + 7]),
            waits=[e7], ev=True)
        e9 = op("vector", lambda e, s, sc=sc: e.scalar_tensor_tensor(
            stats[:, sc + 4:sc + 5], stats[:, sc + 1:sc + 2], 1.0,
            stats[:, sc + 7:sc + 8], ALU.bypass, ALU.mult),
            waits=[e8], ev=True)
        ev_xn_act[tc] = op("vector", lambda e, s, t=t, sc=sc: e.tensor_scalar(
            xcb[:, 0:D], t[:, :], stats[:, sc + 7:sc + 8],
            stats[:, sc + 4:sc + 5], ALU.mult, ALU.add),
            waits=[e9] + ([ev_trans[tc - 1]] if tc >= 1 else []), ev=True)
        for i in range(16):
            ev_trans[tc] = op("tensor", lambda e, s, i=i: e.transpose(
                PAb[:, 128 * i:128 * (i + 1)], xcb[:, 128 * i:128 * (i + 1)],
                ident_b[:, :]),
                waits=([ev_ident, ev_xn_act[tc]] if i == 0 else []) +
                      ([ev_xn_evac[tc - 1]] if (i == 0 and tc >= 1) else []),
                ev=(i == 15))
        ev_xn_evac[tc] = op("vector", lambda e, s, tc=tc: e.tensor_copy(
            xnfm[:, :].rearrange("p (i t) -> p i t", i=16)[:, :, 128 * tc:128 * (tc + 1)],
            PAb[:, :].rearrange("p (i t) -> p i t", i=16)),
            waits=[ev_trans[tc]], ev=True)
    last_xn_evac = ev_xn_evac[7]

    if DEBUG:
        dma("qA", dbg_xn, xnfm[:, :], waits=[last_xn_evac])

    wqs = [wqs0, wqs1]
    ev_qev = [None] * 16
    for oc in range(16):
        ev_wq = dma("qG", wqs[oc % 2][:, :].rearrange("p (i o) -> p i o", i=16),
                    dwqT[:, 128 * oc:128 * (oc + 1)]
                    .rearrange("(i p) o -> p i o", p=128),
                    waits=[ev_qev[oc - 2]] if oc >= 2 else [])
        for th in range(2):
            mm_ev = None
            for ic in range(16):
                mm_ev = op("tensor", lambda e, s, oc=oc, th=th, ic=ic: e.matmul(
                    PB[:, 512 * th:512 * (th + 1)],
                    wqs[oc % 2][:, 128 * ic:128 * (ic + 1)],
                    xnfm[:, 1024 * ic + 512 * th:1024 * ic + 512 * (th + 1)],
                    start=(ic == 0), stop=(ic == 15), skip_group_check=True),
                    waits=([ev_wq, last_xn_evac] if (th == 0 and ic == 0) else []) +
                          ([ev_qev[oc - 1]] if (oc >= 1 and th == 0 and ic == 0) else []),
                    ev=(ic == 15))
            ev_qev[oc] = op("scalar", lambda e, s, oc=oc, th=th: e.activation(
                qfm[:, 1024 * oc + 512 * th:1024 * oc + 512 * (th + 1)],
                PB[:, 512 * th:512 * (th + 1)], AF.Copy, scale=0.125),
                waits=[mm_ev], ev=True)
    ev_kev = [None] * 4
    ev_kst = [None] * 4
    for oc in range(4):
        ev_wk = dma("qG", wqs[oc % 2][:, :].rearrange("p (i o) -> p i o", i=16),
                    dwkT[:, 128 * oc:128 * (oc + 1)]
                    .rearrange("(i p) o -> p i o", p=128),
                    waits=[ev_qev[14 + oc]] if oc < 2 else [ev_kev[oc - 2]])
        for th in range(2):
            mm_ev = None
            for ic in range(16):
                mm_ev = op("tensor", lambda e, s, oc=oc, th=th, ic=ic: e.matmul(
                    PB[:, 512 * th:512 * (th + 1)],
                    wqs[oc % 2][:, 128 * ic:128 * (ic + 1)],
                    xnfm[:, 1024 * ic + 512 * th:1024 * ic + 512 * (th + 1)],
                    start=(ic == 0), stop=(ic == 15), skip_group_check=True),
                    waits=([ev_wk] if (th == 0 and ic == 0) else []) +
                          ([ev_qev[15] if oc == 0 else ev_kev[oc - 1]]
                           if (th == 0 and ic == 0) else []),
                    ev=(ic == 15))
            ev_kev[oc] = op("scalar", lambda e, s, oc=oc, th=th: e.activation(
                kfm[:, 1024 * oc + 512 * th:1024 * oc + 512 * (th + 1)],
                PB[:, 512 * th:512 * (th + 1)], AF.Copy), waits=[mm_ev], ev=True)
        ev_kst[oc] = dma("qA", kst_d[128 * oc:128 * (oc + 1), :],
                         kfm[:, 1024 * oc:1024 * (oc + 1)], waits=[ev_kev[oc]])
    ev_wv = dma("qG", wvsb[:, :].rearrange("p (i v) -> p i v", i=16),
                dwvT.rearrange("(i p) v -> p i v", p=128),
                waits=[ev_qev[15]])
    ev_vst = {}
    ev_vev = [None] * 8
    for tc in range(8):
        mm_ev = None
        for ic in range(16):
            mm_ev = op("tensor", lambda e, s, tc=tc, ic=ic: e.matmul(
                PC[:, 0:512],
                xnfm[:, 1024 * ic + 128 * tc:1024 * ic + 128 * (tc + 1)],
                wvsb[:, 512 * ic:512 * (ic + 1)],
                start=(ic == 0), stop=(ic == 15), skip_group_check=True),
                waits=([ev_wv] if (tc == 0 and ic == 0) else []) +
                      ([ev_vev[tc - 1]] if (tc >= 1 and ic == 0) else []),
                ev=(ic == 15))
        ev_vev[tc] = op("vector", lambda e, s, tc=tc: e.tensor_copy(
            vtmb[:, 512 * (tc % 2):512 * (tc % 2 + 1)], PC[:, 0:512]),
            waits=[mm_ev] + ([ev_vst[(tc - 2, 3)]] if tc >= 2 else []), ev=True)
        for vh in range(4):
            ev_vst[(tc, vh)] = dma(
                "qA", vst_d[1024 * vh + 128 * tc:1024 * vh + 128 * (tc + 1), :],
                vtmb[:, 512 * (tc % 2) + 128 * vh:512 * (tc % 2) + 128 * (vh + 1)],
                waits=[ev_vev[tc]] if vh == 0 else [])
    ev_vst_last = ev_vst[(7, 3)]


    ev_knorm = None
    for oc in range(4):
        esq = op("scalar", lambda e, s, oc=oc: e.activation(
            sqsb[:, 0:1024], kfm[:, 1024 * oc:1024 * (oc + 1)], AF.Square),
            waits=([ev_kev[oc], ev_vst_last] if oc == 0 else [ev_kev[oc]]),
            ev=True)
        for half in range(2):
            kh = 2 * oc + half
            op("tensor", lambda e, s, half=half: e.matmul(
                PD[0:1, 0:512], ones_b[64 * half:64 * half + 64, 0:1],
                sqsb[64 * half:64 * half + 64, 0:512],
                start=True, stop=True, skip_group_check=True),
                waits=[esq] + ([ev_knorm] if ev_knorm else []))
            m2 = op("tensor", lambda e, s, half=half: e.matmul(
                PD[0:1, 512:1024], ones_b[64 * half:64 * half + 64, 0:1],
                sqsb[64 * half:64 * half + 64, 512:1024],
                start=True, stop=True, skip_group_check=True), ev=True)
            ev_knorm = op("vector", lambda e, s, kh=kh: e.reduce_max(
                nmk[0:1, kh:kh + 1], PD[0:1, 0:1024], AX.X),
                waits=[m2], ev=True)
    ev_knst = dma("qA", knst_d.ap(), nmk[0:1, :], waits=[ev_knorm])
    ev_qnorm = None
    for oc in range(16):
        esq = op("scalar", lambda e, s, oc=oc: e.activation(
            sqsb[:, 0:1024], qfm[:, 1024 * oc:1024 * (oc + 1)], AF.Square),
            waits=([ev_knorm] if oc == 0 else []), ev=True)
        for half in range(2):
            h = 2 * oc + half
            op("tensor", lambda e, s, half=half: e.matmul(
                PD[0:1, 0:512], ones_b[64 * half:64 * half + 64, 0:1],
                sqsb[64 * half:64 * half + 64, 0:512],
                start=True, stop=True, skip_group_check=True),
                waits=[esq, ev_qnorm if ev_qnorm else ev_knorm])
            m2 = op("tensor", lambda e, s, half=half: e.matmul(
                PD[0:1, 512:1024], ones_b[64 * half:64 * half + 64, 0:1],
                sqsb[64 * half:64 * half + 64, 512:1024],
                start=True, stop=True, skip_group_check=True), ev=True)
            ev_qnorm = op("vector", lambda e, s, h=h: e.reduce_max(
                nmq[0:1, h:h + 1], PD[0:1, 0:1024], AX.X),
                waits=[m2], ev=True)

    rg = [list(range(NCORES))]
    p.emit("gpsimd", lambda e, s: e.collective_compute(
        "AllGather", ALU.bypass, replica_groups=rg,
        ins=[kst_d.ap().opt()], outs=[kall_d.ap().opt()]),
        waits=[ev_kst[3], ev_vst_last], inc="cc")
    p.emit("gpsimd", lambda e, s: e.collective_compute(
        "AllGather", ALU.bypass, replica_groups=rg,
        ins=[vst_d.ap().opt()], outs=[vall_d.ap().opt()]),
        waits=[ev_vst_last], inc="cc")
    p.emit("gpsimd", lambda e, s: e.collective_compute(
        "AllReduce", ALU.max, replica_groups=rg,
        ins=[knst_d.ap().opt()], outs=[knall_d.ap().opt()]),
        waits=[ev_knst], inc="cc")

    if DEBUG:
        dma("qA", dbg_qfm, qfm[:, :], waits=[ev_qev[15]])
        dma("qA", dbg_kall, kall_d.ap(), waits=[("cc", 3)])
        dma("qA", dbg_vall, vall_d.ap(), waits=[("cc", 3)])

    def emit_mlambda():
        ev_knall = dma("qA", nmk[0:1, :], knall_d.ap(), waits=[("cc", 3)])
        e_sq1 = op("scalar", lambda e, s: e.activation(
            nmk[0:1, :], nmk[0:1, :], AF.Sqrt), waits=[ev_knall], ev=True)
        e_sq2 = op("scalar", lambda e, s: e.activation(
            nmq[0:1, :], nmq[0:1, :], AF.Sqrt), waits=[ev_qnorm], ev=True)
        e_m = None
        for h in range(32):
            e_m = op("vector", lambda e, s, h=h: e.tensor_mul(
                mrow[0:1, h:h + 1], nmq[0:1, h:h + 1],
                nmk[0:1, h // 4:h // 4 + 1]),
                waits=([e_sq1, e_sq2] if h == 0 else []), ev=(h == 31))
        e_mp = op("vector", lambda e, s: e.tensor_tensor(
            bcrow[0:1, 0:16],
            mrow[0:1, :].rearrange("a (h two) -> a h two", two=2)[:, :, 0],
            mrow[0:1, :].rearrange("a (h two) -> a h two", two=2)[:, :, 1],
            ALU.max), waits=[e_m], ev=True)
        e_neg = op("vector", lambda e, s: e.tensor_scalar(
            bcrow[0:1, 0:16], bcrow[0:1, 0:16], -1.0, -1.0,
            ALU.mult, ALU.add), waits=[e_mp], ev=True)
        e_l1 = op("vector", lambda e, s: e.tensor_mul(
            lscr[0:1, 0:64], lam4[0:1, 0:64], lam4[0:1, 64:128]),
            waits=[ev_consts], ev=True)
        e_s1 = op("vector", lambda e, s: e.reduce_sum(
            lscr[0:1, 64:65], lscr[0:1, 0:64], AX.X), waits=[e_l1], ev=True)
        e_e1 = op("scalar", lambda e, s: e.activation(
            lscr[0:1, 66:67], lscr[0:1, 64:65], AF.Exp), waits=[e_s1], ev=True)
        e_l2 = op("vector", lambda e, s: e.tensor_mul(
            lscr[0:1, 0:64], lam4[0:1, 128:192], lam4[0:1, 192:256]),
            waits=[e_e1], ev=True)
        e_s2 = op("vector", lambda e, s: e.reduce_sum(
            lscr[0:1, 65:66], lscr[0:1, 0:64], AX.X), waits=[e_l2], ev=True)
        e_e2 = op("scalar", lambda e, s: e.activation(
            lscr[0:1, 68:69], lscr[0:1, 65:66], AF.Exp), waits=[e_s2], ev=True)
        e_lf = op("vector", lambda e, s: e.tensor_sub(
            lscr[0:1, 70:71], lscr[0:1, 66:67], lscr[0:1, 68:69]),
            waits=[e_e2], ev=True)
        e_lf2 = op("vector", lambda e, s: e.tensor_scalar(
            bcrow[0:1, 16:17], lscr[0:1, 70:71], -1.0, -LAMBDA_INIT,
            ALU.mult, ALU.add), waits=[e_lf], ev=True)
        return e_neg, e_lf2

    def emit_down_prefetch(q):
        evs = {}
        for j in range(4):
            jp = (8, 9, 10, 7)[j]
            wd_w = [] if q == 0 else [ev_ddq[(q - 1, 3, jp)]]
            evs[j] = dma("qG", wdr[j][:, :],
                         dwd8[128 * (HCQ * q + j):128 * (HCQ * q + j + 1),
                              0:512], waits=wd_w)
        return evs

    down_pre = {}
    gw_pre = {}
    mlam_evs = None
    wgr = [wgr0, wgr1]
    wur = [wur0, wur1]
    wgd = [wgd0, wgd1]
    wud = [wud0, wud1]
    wdr = [wdr0, wdr1, wdr2, wdr3]
    wdd = [wdd0, wdd1, wdd2, wdd3]
    ev_mmj_prev = None
    ev_h = [None] * NHC
    ev_gdq = [None] * NHC
    ev_udq = [None] * NHC
    ev_ddq = {}
    ev_acc = {}
    last_xn_read = None
    ev_es_first = True
    for q in range(NQ):
        if q == 0:
            down_pre[0] = emit_down_prefetch(0)
        if q == 1:
            mlam_evs = emit_mlambda()
        for j in range(HCQ):
            H = HCQ * q + j
            pp = H % 2
            if (H) in gw_pre:
                ev_wg, ev_wu = gw_pre[H]
            else:
                wg_w = []
                if H == 0:
                    wg_w = [last_xn_evac, ev_vst_last]
                elif H >= 2:
                    wg_w = [ev_udq[H - 2]]
                ev_wg = dma("qG", wgr[pp][:, :], dwg8[H, :, :], waits=wg_w)
                ev_wu = dma("qG", wur[pp][:, :], dwu8[H, :, :])
            ev_gdq[H] = op("vector", lambda e, s, pp=pp: e.tensor_scalar(
                wgd[pp][:, :], wgr[pp][:, :], -128.0, None, ALU.add),
                waits=[ev_wg] + ([ev_h[H - 2]] if H >= 2 else []), ev=True)
            ev_udq[H] = op("vector", lambda e, s, pp=pp: e.tensor_scalar(
                wud[pp][:, :], wur[pp][:, :], -128.0, None, ALU.add),
                waits=[ev_wu], ev=True)
            for th in range(2):
                mw = []
                if th == 0:
                    mw = [ev_gdq[H], ev_udq[H]]
                    if j == 0:
                        mw.append(ev_qnorm if q == 0 else ev_acc[(q - 1, 3, 7)])
                        if DEBUG and q == 1:
                            mw.append(dbg_ffn_ev[0])
                    if H >= 1:
                        mw.append(ev_h[H - 1])
                mmg = mmu = None
                for ic in range(16):
                    mmg = op("tensor", lambda e, s, pp=pp, th=th, ic=ic: e.matmul(
                        PA[:, 512 * th:512 * (th + 1)],
                        wgd[pp][:, 128 * ic:128 * (ic + 1)],
                        xnfm[:, 1024 * ic + 512 * th:1024 * ic + 512 * (th + 1)],
                        start=(ic == 0), stop=(ic == 15), skip_group_check=True),
                        waits=(mw if ic == 0 else []), ev=(ic == 15))
                for ic in range(16):
                    mmu = op("tensor", lambda e, s, pp=pp, th=th, ic=ic: e.matmul(
                        PB[:, 512 * th:512 * (th + 1)],
                        wud[pp][:, 128 * ic:128 * (ic + 1)],
                        xnfm[:, 1024 * ic + 512 * th:1024 * ic + 512 * (th + 1)],
                        start=(ic == 0), stop=(ic == 15), skip_group_check=True),
                        ev=(ic == 15))
                if H == NHC - 1 and th == 1:
                    last_xn_read = mmu
                es_w = [mmg]
                if DEBUG and H == 0 and th == 0:
                    edw = op("vector", lambda e, s: e.tensor_copy(
                        ds_scr[:, :], PA[:, 0:512]), waits=[mmg], ev=True)
                    dma("qA", dbg_wgd, wgd0[:, :], waits=[ev_gdq[0]])
                    dbg_g_ev = dma("qA", dbg_gate, ds_scr[:, :], waits=[edw])
                    es_w.append(dbg_g_ev)
                    edw2 = op("vector", lambda e, s: e.tensor_copy(
                        ds_scr[:, :], PB[:, 0:512]), waits=[mmu, dbg_g_ev], ev=True)
                    dbg_u_ev = dma("qA", dbg_up, ds_scr[:, :], waits=[edw2])
                    es_w.append(dbg_u_ev)
                if ev_es_first:
                    es_w.append(ev_qev[15])
                    ev_es_first = False
                es = op("scalar", lambda e, s, pp=pp, th=th, H=H: e.activation(
                    sil_scr[:, 512 * th:512 * (th + 1)],
                    PA[:, 512 * th:512 * (th + 1)], AF.Silu,
                    scale=sgpp[:, H:H + 1], bias=cbias[:, 3:4]),
                    waits=es_w, ev=True)
                hw = [mmu, es]
                if DEBUG and H == 0 and th == 0:
                    hw.append(dma("qA", dbg_sil, sil_scr[:, 0:512], waits=[es]))
                ev_h[H] = op("vector",
                             lambda e, s, th=th, H=H, j=j: e.scalar_tensor_tensor(
                    hfm[:, 1024 * j + 512 * th:1024 * j + 512 * (th + 1)],
                    PB[:, 512 * th:512 * (th + 1)], supp[:, H:H + 1],
                    sil_scr[:, 512 * th:512 * (th + 1)],
                    ALU.mult, ALU.mult), waits=hw, ev=True)
        if q + 1 < NQ:
            for Hn in (HCQ * (q + 1), HCQ * (q + 1) + 1):
                gw_pre[Hn] = (dma("qG", wgr[Hn % 2][:, :], dwg8[Hn, :, :],
                                  waits=[ev_udq[Hn - 2]]),
                              dma("qG", wur[Hn % 2][:, :], dwu8[Hn, :, :]))
        ps_slices = [(PA, 0), (PA, 512), (PB, 0), (PB, 512),
                     (PC, 0), (PC, 512), (PD, 0), (PD, 512)]
        for dd in range(4):
            mm_last = [None] * 8
            ev_mmj = {}
            for j in range(HCQ):
                H = HCQ * q + j
                pp = j % 4
                jp = (8, 9, 10, 7)[pp]
                if dd == 0 and j < 4:
                    evd = down_pre[q][j]
                else:
                    if j >= 4:
                        wd_w = [ev_ddq[(q, dd, j - 4)]]
                    else:
                        wd_w = [ev_ddq[(q, dd - 1, jp)]]
                    evd = dma("qG", wdr[pp][:, :],
                              dwd8[128 * H:128 * (H + 1),
                                   512 * dd:512 * (dd + 1)],
                              waits=wd_w)
                dq_w = [evd]
                if j >= 4:
                    dq_w.append(ev_mmj[j - 4])
                elif ev_mmj_prev is not None:
                    dq_w.append(ev_mmj_prev[jp])
                ev_ddq[(q, dd, j)] = op(
                    "vector", lambda e, s, pp=pp: e.tensor_scalar(
                        wdd[pp][:, :], wdr[pp][:, :], -128.0, None, ALU.add),
                    waits=dq_w, ev=True)
                for tc in range(8):
                    ps, o = ps_slices[tc]
                    dmw = []
                    if tc == 0:
                        dmw = [ev_ddq[(q, dd, j)]]
                    if j == 0 and tc == 0 and dd == 0:
                        dmw += [ev_h[HCQ * q + HCQ - 1]]
                        if q == 0:
                            dmw += [ev_qnorm, ev_vev[7]]
                    if j == 0:
                        if dd >= 1:
                            dmw += [ev_acc[(q, dd - 1, tc)]]
                        elif q >= 1:
                            dmw += [ev_acc[(q - 1, 3, tc)]]
                    mm_last[tc] = op("tensor",
                                     lambda e, s, ps=ps, o=o, pp=pp, j=j, tc=tc: e.matmul(
                        ps[:, o:o + 512],
                        hfm[:, 1024 * j + 128 * tc:1024 * j + 128 * (tc + 1)],
                        wdd[pp][:, :],
                        start=(j == 0), stop=(j == HCQ - 1),
                        skip_group_check=True),
                        waits=dmw, ev=(j == HCQ - 1 or tc == 7))
                ev_mmj[j] = mm_last[7]
            ev_mmj_prev = ev_mmj
            for tc in range(8):
                ps, o = ps_slices[tc]
                if q == 0:
                    ev_acc[(q, dd, tc)] = op(
                        "vector", lambda e, s, ps=ps, o=o, tc=tc, dd=dd: e.tensor_mul(
                            acc[:, D * tc + 512 * dd:D * tc + 512 * (dd + 1)],
                            ps[:, o:o + 512], sdb[:, 512 * dd:512 * (dd + 1)]),
                        waits=[mm_last[tc]], ev=True)
                else:
                    e1d = op("vector", lambda e, s, ps=ps, o=o, dd=dd: e.tensor_mul(
                        ds_scr[:, :], ps[:, o:o + 512],
                        sdb[:, 512 * dd:512 * (dd + 1)]),
                        waits=[mm_last[tc]], ev=True)
                    ev_acc[(q, dd, tc)] = op(
                        "vector", lambda e, s, tc=tc, dd=dd: e.tensor_add(
                            acc[:, D * tc + 512 * dd:D * tc + 512 * (dd + 1)],
                            acc[:, D * tc + 512 * dd:D * tc + 512 * (dd + 1)],
                            ds_scr[:, :]), waits=[e1d], ev=True)
        if q + 1 < NQ:
            down_pre[q + 1] = emit_down_prefetch(q + 1)
        if DEBUG and q == 0:
            dbg_ffn_ev = [None]
            e1dmp = dma("qA", dbg_h0, hfm[:, :], waits=[ev_h[HCQ - 1]])
            dbg_ffn_ev[0] = dma("qA", dbg_acc0, acc[:, :],
                                waits=[ev_acc[(0, 3, 7)]])
    ffn_last = ev_acc[(NQ - 1, 3, 7)]
    ev_masks = dma("qA", masks[:, :], dmasks, waits=[ev_h[NHC - 1]])

    e_neg, e_lf2 = mlam_evs
    e_bc = op("tensor", lambda e, s: e.matmul(
        PD[:, 0:18], onescol_f[0:1, :], bcrow[0:1, 0:18],
        start=True, stop=True, skip_group_check=True),
        waits=[e_neg, e_lf2, ffn_last], ev=True)
    ev_negmpp = op("vector", lambda e, s: e.tensor_copy(
        negmpp[:, 0:18], PD[:, 0:18]), waits=[e_bc], ev=True)
    ev_lnws = op("scalar", lambda e, s: e.activation(
        lnw_s[:, 0:1], lnw[:, 0:1], AF.Copy, scale=(1.0 - LAMBDA_INIT)),
        waits=[ev_consts], ev=True)

    if DEBUG:
        dma("qA", dbg_acc, acc[:, :], waits=[ffn_last])
        dma("qA", dbg_negm, negmpp[:, 0:18], waits=[ev_negmpp])

    PV = PC
    LL = PD

    def mk_frozen2(name, colbase, mx):
        st = {"r": None, "rvs": [None] * NITEMS}

        def get(e, i):
            if st["r"] is None:
                st["r"] = e.alloc_register(name)
            if st["rvs"][i] is None:
                e.reg_load(st["r"], itab[0:1, colbase + i:colbase + i + 1])
                st["rvs"][i] = bass.RuntimeValue(
                    st["r"], min_val=0, max_val=mx + i)
            return st["rvs"][i]
        return get

    ve_rv = mk_frozen2("koff", 0, 7680)
    sy_rv = mk_frozen2("voff", NITEMS, (8 * TOK - 512) * 128)
    pe_state = {"r": [None, None]}

    def pe_flag(e, i):
        if pe_state["r"][0] is None:
            pe_state["r"][0] = e.alloc_register("qflag0")
            pe_state["r"][1] = e.alloc_register("qflag1")
        r = pe_state["r"][i % 2]
        e.reg_load(r, itab[0:1, 2 * NITEMS + i:2 * NITEMS + i + 1])
        return r

    def pe_flag_cached(e, j):
        return pe_state["r"][j % 2]

    dbg_kv_ev = [None]
    ev_kcache = ev_vcache = None
    ev_kv = {}
    ev_vv = {}
    ev_stq = {}
    ev_stqA = {}
    ev_exp = {}
    ev_expA = {}
    ev_msk = {}
    ev_pv = {}
    ev_evO = {}
    ev_epi_done = None
    ev_epi_ps = None
    ev_epi_act = None
    ev_epi_tr = None
    er1_ev = None
    for h in range(NH):
        d = h // 2
        kh = h // 4
        vh = h // 8
        isA = (h % 2 == 0)
        rb = 64 * (h % 2)
        if h % 4 == 0:
            kc_w = ([last_xn_read, ("cc", 3)] if h == 0
                    else [ev_kv[(h - 1, NITEMS - 1)]])
            for r in range(NCORES):
                ev_kcache = dma("qS", ksb[:, 1024 * r:1024 * (r + 1)],
                                kall_d.ap()[512 * r + 64 * kh:512 * r + 64 * (kh + 1), :],
                                waits=(kc_w if r == 0 else []))
        if h % 8 == 0:
            vc_w = [("cc", 3)] if h == 0 else [ev_vv[(h - 1, NITEMS - 1)]]
            for r in range(NCORES):
                ev_vcache = dma("qG",
                                vcur_d.ap()[1024 * r:1024 * (r + 1), :],
                                vall_d.ap()[4096 * r + 1024 * vh:4096 * r + 1024 * (vh + 1), :],
                                waits=(vc_w if r == 0 else []))
        if h == 0:
            stw0 = [ffn_last]
            pvw0 = [ev_negmpp]
        else:
            stw0 = [ev_exp[(h - 1, NITEMS - 1)]]
            pvw0 = [ev_evO[(h - 1, "o")], ev_evO[(h - 1, "l")]]
        if ev_epi_done is not None:
            pvw0.append(ev_epi_done)

        def pvl_body(e, s, ev, ssl, vbuf, vo, qa, st):
            for c in range(4):
                e.matmul(PV[:, qa:qa + 512],
                         vbuf[:, vo + 128 * c:vo + 128 * (c + 1)],
                         pt[:, ssl + 512 * c:ssl + 512 * (c + 1)],
                         start=(st and c == 0), stop=False,
                         skip_group_check=True)
            for c in range(4):
                mm = e.matmul(LL[0:64, qa:qa + 512], ones_b[:, 0:64],
                              pt[:, ssl + 512 * c:ssl + 512 * (c + 1)],
                              start=(st and c == 0), stop=False,
                              skip_group_check=True)
            mm.then_inc(s[ev[0]], 1)

        def pvl(e, s, h, j, ev):
            ssl = 2048 * (j % 2)
            vbuf = vstat if (j % 4) < 2 else vstat2
            vo = 512 * (j % 2)
            if j < 2:
                pvl_body(e, s, ev, ssl, vbuf, vo, 512 * j, True)
                return
            rf = pe_flag_cached(e, j)
            for cond in (0, 1):
                qa = 512 * cond
                cm = e.If_eq(rf, 0) if cond == 0 else e.Else()
                with cm:
                    pvl_body(e, s, ev, ssl, vbuf, vo, qa, False)
            e.end_ifs()

        for i in range(NITEMS):
            sl = i % 2
            dwk = []
            if i == 0:
                dwk = [ev_kcache, ev_consts]
            if i < 2 and h > 0:
                dwk = dwk + [ev_stq[(h - 1, NITEMS - 2 + i)]]
            if i >= 2:
                dwk = [ev_stq[(h, i - 2)]]

            def kcopy(e, s, i=i, sl=sl, rb=rb):
                v = ve_rv(e, i)
                return e.tensor_copy(kstat[rb:rb + 64, 512 * sl:512 * (sl + 1)],
                                     ksb[:, bass.ds(v, 512)])
            if DEBUG and h in (0, 16) and i == 2:
                tgt = dbg_kv if h == 0 else dbg_kv16
                dma("qA", tgt[:, 0:1024], kstat[:, :],
                    waits=[ev_kv[(h, 1)], ev_vv[(h, 1)]])
                dbg_kv_ev[0] = dma("qA", tgt[:, 1024:2048], vstat[:, :])
                dwk = dwk + [dbg_kv_ev[0]]
            ev_kv[(h, i)] = op("vector", kcopy, waits=dwk, ev=True)

            if 0 <= i - 1 < 2:
                ev_msk[(h, i - 1)] = op(
                    "vector", lambda e, s, psl=(i - 1) % 2: e.tensor_mul(
                        pt[:, 2048 * psl:2048 * (psl + 1)],
                        pt[:, 2048 * psl:2048 * (psl + 1)], masks[:, 0:2048]),
                    waits=[ev_exp[(h, i - 1)]] +
                          ([ev_masks] if (h == 0 and i == 1) else []), ev=True)

            if i == 1 and h % 2 == 1:
                w1 = [ev_evO[(h - 1, "l")]]
                if ev_epi_tr is not None:
                    w1.append(ev_epi_tr)
                er1_ev = op("vector", lambda e, s: e.reciprocal(
                    ra_sb[0:1, :], la_sb[0:1, :]), waits=w1, ev=True)

            dwv = []
            if i == 0 and h == 0:
                dwv = [ev_consts]
            if i == 0 and h % 8 == 0:
                dwv = dwv + [ev_vcache]
            if i < 4 and h > 0:
                dwv = dwv + [ev_pv[(h - 1, (16, 13, 14, 15)[i])]]
            if i < 4 and h == 0:
                dwv = dwv + [ffn_last]
            if i >= 4:
                dwv = [ev_pv[(h, i - 4)]]
            if DEBUG and h in (0, 16) and i == 2:
                dwv = dwv + [dbg_kv_ev[0]]

            def vcopy(e, s, i=i, sl=sl):
                v = sy_rv(e, i)
                vbuf = vstat if (i % 4) < 2 else vstat2
                src = bass.AP(vcur_d, v, [[128, 128], [16384, 4], [1, 128]])
                return e.dma_start(
                    out=vbuf[:, 512 * sl:512 * (sl + 1)]
                    .rearrange("p (c v) -> p c v", c=4), in_=src)
            ev_vv[(h, i)] = p.emit("gpsimd", vcopy, waits=dwv, inc="qG",
                                   inc_by=16)

            stw = [ev_kv[(h, i)]]
            stwB = []
            if i == 0:
                stw += stw0
            if i >= 1:
                stw.append(ev_expA[(h, i - 1)])
                stwB.append(ev_exp[(h, i - 1)])
            ev_stqA[(h, i)] = p.bump("sPE")

            def stmmA(e, s, h=h, i=i, sl=sl, rb=rb, ev=ev_stqA[(h, i)]):
                base = 1024 * (h // 2)
                tp = (rb, 0) if rb else None
                rf = pe_flag(e, i)
                for cond in (0, 1):
                    qa = base + 512 * cond
                    cm = e.If_eq(rf, 0) if cond == 0 else e.Else()
                    with cm:
                        rhs = qfm[rb:rb + 64, qa:qa + 512]
                        for c in range(2):
                            mm = e.matmul(
                                PST[:, 512 * c:512 * (c + 1)],
                                kstat[rb:rb + 64,
                                      512 * sl + 128 * c:512 * sl + 128 * (c + 1)],
                                rhs, start=True, stop=True,
                                skip_group_check=True, tile_position=tp)
                        mm.then_inc(s[ev[0]], 1)
                e.end_ifs()
            p.emit("tensor", stmmA, waits=stw)
            ev_stq[(h, i)] = p.bump("sPE")

            def stmmB(e, s, h=h, i=i, sl=sl, rb=rb, ev=ev_stq[(h, i)]):
                base = 1024 * (h // 2)
                tp = (rb, 0) if rb else None
                rf = pe_flag_cached(e, i)
                for cond in (0, 1):
                    qa = base + 512 * cond
                    cm = e.If_eq(rf, 0) if cond == 0 else e.Else()
                    with cm:
                        rhs = qfm[rb:rb + 64, qa:qa + 512]
                        for c in range(2, 4):
                            mm = e.matmul(
                                PST[:, 512 * c:512 * (c + 1)],
                                kstat[rb:rb + 64,
                                      512 * sl + 128 * c:512 * sl + 128 * (c + 1)],
                                rhs, start=True, stop=True,
                                skip_group_check=True, tile_position=tp)
                        mm.then_inc(s[ev[0]], 1)
                e.end_ifs()
            p.emit("tensor", stmmB, waits=stwB)

            if i >= 1:
                ev_pv[(h, i - 1)] = p.bump("sPE")
                p.emit("tensor",
                       lambda e, s, h=h, j=i - 1, ev=ev_pv[(h, i - 1)]: pvl(
                           e, s, h, j, ev),
                       waits=[ev_vv[(h, i - 1)]] +
                             ([ev_msk[(h, i - 1)]] + pvw0
                              if i - 1 < 2 else []))

            ew = [ev_stqA[(h, i)]]
            if i >= 2:
                ew.append(ev_pv[(h, i - 2)])
            elif h > 0:
                ew.append(ev_pv[(h - 1, NITEMS - 2 + i)])
            ev_expA[(h, i)] = op("scalar", lambda e, s, sl=sl, d=d: e.activation(
                pt[:, 2048 * sl:2048 * sl + 1024], PST[:, 0:1024],
                AF.Exp, bias=negmpp[:, d:d + 1]), waits=ew, ev=True)
            ev_exp[(h, i)] = op("scalar", lambda e, s, sl=sl, d=d: e.activation(
                pt[:, 2048 * sl + 1024:2048 * sl + 2048], PST[:, 1024:2048],
                AF.Exp, bias=negmpp[:, d:d + 1]),
                waits=[ev_stq[(h, i)]], ev=True)
        ev_pv[(h, NITEMS - 1)] = p.bump("sPE")
        p.emit("tensor",
               lambda e, s, h=h, j=NITEMS - 1, ev=ev_pv[(h, NITEMS - 1)]: pvl(
                   e, s, h, j, ev),
               waits=[ev_exp[(h, NITEMS - 1)], ev_vv[(h, NITEMS - 1)]])

        odst = o1s if isA else o2s
        ldst = la_sb if isA else lb_sb
        evo_w = [ev_pv[(h, NITEMS - 1)]]
        if h == 0:
            evo_w.append(ffn_last)
        ev_evO[(h, "o")] = op("vector", lambda e, s, odst=odst: e.tensor_copy(
            odst[:, :], PV[:, 0:1024]), waits=evo_w, ev=True)
        ev_evO[(h, "l")] = op("vector", lambda e, s, ldst=ldst: e.tensor_copy(
            ldst[0:1, :], LL[0:1, 0:1024]), ev=True)
        if DEBUG and h == 1:
            dma("qA", dbg_l0[0:1, :], la_sb[0:1, :], waits=[ev_evO[(h, "l")]])
            dma("qA", dbg_l0[1:2, :], lb_sb[0:1, :])
            dma("qA", dbg_o0[:, 0:1024], o1s[:, :])
            dma("qA", dbg_o0[:, 1024:2048], o2s[:, :])

        if not isA:
            er_w = [ev_evO[(h, "l")]]
            if ev_epi_tr is not None:
                er_w.append(ev_epi_tr)
            er2a = op("vector", lambda e, s: e.reciprocal(
                rb_sb[0:1, 0:512], lb_sb[0:1, 0:512]), waits=er_w, ev=True)
            er2b = op("vector", lambda e, s: e.reciprocal(
                rb_sb[0:1, 512:1024], lb_sb[0:1, 512:1024]), ev=True)
            to_ev = None
            for qq in range(8):
                to_ev = op("tensor", lambda e, s, qq=qq: e.transpose(
                    PCb[:, 128 * qq:128 * (qq + 1)],
                    o1s[:, 128 * qq:128 * (qq + 1)], ident_b[:, :]),
                    waits=([ev_evO[(h - 1, "o")], ev_evO[(h, "o")]]
                           if qq == 0 else []))
            for qq in range(8):
                to_ev = op("tensor", lambda e, s, qq=qq: e.transpose(
                    PCb[:, 1024 + 128 * qq:1024 + 128 * (qq + 1)],
                    o2s[:, 128 * qq:128 * (qq + 1)], ident_b[:, :]),
                    ev=(qq == 7))
            tr_ev = None
            for qq in range(8):
                tr_ev = op("tensor", lambda e, s, qq=qq: e.transpose(
                    PD[:, qq:qq + 1], ra_sb[0:1, 128 * qq:128 * (qq + 1)],
                    onescol_f[0:1, 0:1]),
                    waits=([er1_ev, ev_evO[(h, "l")]] if qq == 0 else []))
            for qq in range(8):
                tr_ev = op("tensor", lambda e, s, qq=qq: e.transpose(
                    PD[:, 8 + qq:8 + qq + 1],
                    rb_sb[0:1, 128 * qq:128 * (qq + 1)], onescol_f[0:1, 0:1]),
                    waits=([er2a] if qq == 0 else ([er2b] if qq == 4 else [])),
                    ev=(qq == 7))
            ev_epi_tr = tr_ev
            e_rpp = op("vector", lambda e, s: e.tensor_copy(
                rpp[:, :], PD[:, 0:16]), waits=[tr_ev], ev=True)
            e_rcp = e_rpp
            e_rbn = op("vector", lambda e, s: e.tensor_scalar(
                rbn[:, :], rpp[:, 8:16], negmpp[:, 16:17], None, ALU.mult),
                ev=True)
            e_oda = [None] * 8
            for qq in range(8):
                e_oda[qq] = op("vector", lambda e, s, qq=qq: e.tensor_scalar(
                    scr1[:, 128 * qq:128 * (qq + 1)],
                    PCb[:, 128 * qq:128 * (qq + 1)], rpp[:, qq:qq + 1], None,
                    ALU.mult),
                    waits=([to_ev, e_rcp] if qq == 0 else []), ev=True)
            e_od = [None] * 8
            for qq in range(8):
                e_od[qq] = op("vector", lambda e, s, qq=qq: e.scalar_tensor_tensor(
                    odb[:, 128 * qq:128 * (qq + 1)],
                    PCb[:, 1024 + 128 * qq:1024 + 128 * (qq + 1)], rbn[:, qq:qq + 1],
                    scr1[:, 128 * qq:128 * (qq + 1)], ALU.mult, ALU.add),
                    waits=[e_oda[qq]] + ([e_rbn] if qq == 0 else []), ev=True)
            e_ssq = None
            for qq in range(8):
                e_ssq = op("vector", lambda e, s, qq=qq: e.scalar_tensor_tensor(
                    scr1[:, 128 * qq:128 * (qq + 1)],
                    odb[:, 128 * qq:128 * (qq + 1)], 1.0,
                    odb[:, 128 * qq:128 * (qq + 1)], ALU.bypass, ALU.mult,
                    accum_out=ssq8[:, qq:qq + 1]),
                    waits=[e_od[qq]], ev=(qq == 7))
            e_std = op("scalar", lambda e, s: e.activation(
                ssq8[:, :], ssq8[:, :], AF.Sqrt, scale=1.0 / 128,
                bias=cbias[:, 0:1]), waits=[e_ssq], ev=True)
            e_rst = op("vector", lambda e, s: e.reciprocal(
                rst8[:, :], ssq8[:, :]), waits=[e_std], ev=True)
            e_on = None
            for qq in range(8):
                e_on = op("vector", lambda e, s, qq=qq: e.tensor_scalar(
                    onb[:, 128 * qq:128 * (qq + 1)],
                    odb[:, 128 * qq:128 * (qq + 1)], rst8[:, qq:qq + 1], None,
                    ALU.mult),
                    waits=([e_rst] if qq == 0 else []), ev=(qq == 7))
            ev_epi_act = e_on
            tb_ev = None
            for qq in range(8):
                tb_ev = op("tensor", lambda e, s, qq=qq: e.transpose(
                    PDb[:, 1024 + 128 * qq:1024 + 128 * (qq + 1)],
                    onb[:, 128 * qq:128 * (qq + 1)], ident_b[:, :]),
                    waits=([e_on] if qq == 0 else []), ev=(qq == 7))
            ev_epi_ps = tb_ev
            ev_epi_done = op("vector", lambda e, s, d=d: e.tensor_scalar(
                ofm[:, 1024 * d:1024 * (d + 1)], PDb[:, 1024:2048],
                lnw_s[:, 0:1], None, ALU.mult),
                waits=[tb_ev] + ([ev_lnws] if d == 0 else []), ev=True)

    attn_done = ev_epi_done
    attn_kv_done = ev_kv[(NH - 1, NITEMS - 1)]
    if DEBUG:
        dma("qA", dbg_accpre, acc[:, :], waits=[attn_done])
        dma("qA", dbg_ofm, ofm[:, :], waits=[attn_done])
        dma("qA", dbg_l[0:1, :], la_sb[0:1, :], waits=[attn_done])
        dma("qA", dbg_l[1:2, :], lb_sb[0:1, :], waits=[attn_done])

    ev_stq_last = ev_stq[(NH - 1, NITEMS - 1)]
    ev_wo_q = {}
    for ic in range(16):
        q3 = ("qG", "qS", "qA")[ic % 3]
        w = [ev_stq_last, attn_kv_done] if ic < 3 else []
        ev_wo_q[q3] = dma(q3, wo_all[:, 2048 * ic:2048 * (ic + 1)],
                          dwoT[128 * ic:128 * (ic + 1), :], waits=w,
                          fifo=False)
    ev_wo_gate = list(ev_wo_q.values())
    ev_acc2 = {}
    oc_bank = {0: (PA, 0), 1: (PB, 0), 2: (PC, 0), 3: (PC, 512)}
    oc_order = (2, 3, 0, 1)
    for tc in range(8):
        for j_oc, oc in enumerate(oc_order):
            ps, o = oc_bank[oc]
            mm = None
            for ic in range(16):
                mw = []
                if ic == 0:
                    if tc == 0 and j_oc == 0:
                        mw = ev_wo_gate + [attn_done]
                    if tc >= 1:
                        mw.append(ev_acc2[(tc - 1, oc)])
                mm = op("tensor", lambda e, s, ps=ps, o=o, ic=ic, tc=tc, oc=oc: e.matmul(
                    ps[:, o:o + 512],
                    ofm[:, 1024 * ic + 128 * tc:1024 * ic + 128 * (tc + 1)],
                    wo_all[:, 2048 * ic + 512 * oc:2048 * ic + 512 * (oc + 1)],
                    start=(ic == 0), stop=(ic == 15), skip_group_check=True),
                    waits=mw, ev=(ic == 15))
            ev_acc2[(tc, oc)] = op(
                "vector", lambda e, s, ps=ps, o=o, tc=tc, oc=oc: e.tensor_add(
                    acc[:, D * tc + 512 * oc:D * tc + 512 * (oc + 1)],
                    acc[:, D * tc + 512 * oc:D * tc + 512 * (oc + 1)],
                    ps[:, o:o + 512]), waits=[mm], ev=True)

    if DEBUG:
        dma("qA", dbg_wo, wo_all[:, :], waits=ev_wo_gate)
        dma("qA", dbg_accw, acc[:, :], waits=[ev_acc2[(7, 3)]])

    outb = [out0, out1]
    ev_store = [None] * 8
    ev_add = [None] * 8
    for tc in range(8):
        w = [ev_acc2[(tc, oc)] for oc in range(4)]
        exld = dma("qA", xs[tc % 2][:, :], dx[128 * tc:128 * (tc + 1), :],
                   waits=w + ([ev_add[tc - 2]] if tc >= 2 else []))
        ev_add[tc] = op("vector", lambda e, s, tc=tc: e.tensor_add(
            outb[tc % 2][:, :], acc[:, D * tc:D * (tc + 1)], xs[tc % 2][:, :]),
            waits=[exld] + ([ev_store[tc - 2]] if tc >= 2 else []), ev=True)
        ev_store[tc] = dma("qS", dy[128 * tc:128 * (tc + 1), :],
                           outb[tc % 2][:, :], waits=[ev_add[tc]])
    final_qA = p.cnt["qA"]
    final_qS = p.cnt["qS"]
    p.emit("scalar", lambda e, s: e.nop(),
           waits=[("qA", final_qA), ("qS", final_qS)])

    with ExitStack() as ctx:
        sems = {n: ctx.enter_context(nc.semaphore(n)) for n in sem_names}
        with nc.Block() as block:
            @block.sync
            def _(eng):
                p.replay("sync", eng, sems)

            @block.gpsimd
            def _(eng):
                p.replay("gpsimd", eng, sems)

            @block.vector
            def _(eng):
                p.replay("vector", eng, sems)

            @block.scalar
            def _(eng):
                p.replay("scalar", eng, sems)

            @block.tensor
            def _(eng):
                p.replay("tensor", eng, sems)

    return nc


def _koff(b):
    r = b if b < 8 else 15 - b
    return 1024 * r + (512 if b >= 8 else 0)


def _item_table(core):
    bA, bB = core, 15 - core
    ko = [_koff(bA), _koff(bB)]
    qo = [0, 512]
    for t in range(bA):
        ko.append(_koff(t))
        qo.append(0)
    for t in range(bB):
        ko.append(_koff(t))
        qo.append(512)
    assert len(ko) == NITEMS
    vo = [128 * k for k in ko]
    return np.array([ko + vo + qo], dtype=np.int32)


def _masks():
    m = np.zeros((128, 2048), dtype=np.float32)
    qf = np.arange(512)[None, :]
    kp = np.arange(128)[:, None]
    for c in range(4):
        allow = (qf - kp) >= 128 * c
        m[:, 512 * c:512 * (c + 1)] = np.where(allow, 1.0, 0.0)
    return np.ascontiguousarray(m.astype(BF))


_NC_CACHE = {}


def kernel(L0_attn_x, L0_attn_Wq, L0_attn_Wk, L0_attn_Wv, L0_attn_Wo,
           L0_attn_lambda_q1, L0_attn_lambda_k1, L0_attn_lambda_q2,
           L0_attn_lambda_k2, L0_attn_ln_w,
           L0_ffn_wq_gate, L0_ffn_s_gate, L0_ffn_wq_up, L0_ffn_s_up,
           L0_ffn_wq_down, L0_ffn_s_down):
    x = np.ascontiguousarray(np.asarray(L0_attn_x, dtype=np.float32)
                             .reshape(8192, D))
    C = np.ascontiguousarray
    lam4 = np.concatenate([
        np.asarray(L0_attn_lambda_q1, np.float32).reshape(1, 64),
        np.asarray(L0_attn_lambda_k1, np.float32).reshape(1, 64),
        np.asarray(L0_attn_lambda_q2, np.float32).reshape(1, 64),
        np.asarray(L0_attn_lambda_k2, np.float32).reshape(1, 64)], axis=1)
    shared = {
        "wqT": C(np.asarray(L0_attn_Wq, np.float32).T.astype(BF)),
        "wkT": C(np.asarray(L0_attn_Wk, np.float32).T.astype(BF)),
        "wvT": C(np.asarray(L0_attn_Wv, np.float32).T.astype(BF)),
        "woT": C(np.asarray(L0_attn_Wo, np.float32).T.astype(BF)),
        "wg8": C(np.asarray(L0_ffn_wq_gate).reshape(NHC, 128, 16, 128)
                 .transpose(0, 3, 2, 1).reshape(NHC, 128, D).astype(np.uint8)),
        "wu8": C(np.asarray(L0_ffn_wq_up).reshape(NHC, 128, 16, 128)
                 .transpose(0, 3, 2, 1).reshape(NHC, 128, D).astype(np.uint8)),
        "wd8": C(np.asarray(L0_ffn_wq_down).T.astype(np.uint8)),
        "sgpp": C(np.pad(np.asarray(L0_ffn_s_gate, np.float32).reshape(NHC, 128).T, ((0, 0), (0, 4)))),
        "supp": C(np.pad(np.asarray(L0_ffn_s_up, np.float32).reshape(NHC, 128).T, ((0, 0), (0, 4)))),
        "sdb": C(np.broadcast_to(np.asarray(L0_ffn_s_down, np.float32)[None, :],
                                 (128, D))),
        "lnw": C(np.pad(np.asarray(L0_attn_ln_w, np.float32).reshape(128, 1), ((0, 0), (0, 7)))),
        "lam4": C(lam4),
        "ident_b": np.eye(128, dtype=BF),
        "ones_b": np.ones((128, 128), dtype=BF),
        "zrow_b": np.zeros((1, 512), dtype=BF),
        "onesrow_b": np.ones((1, 512), dtype=BF),
        "onescol_f": np.ones((1, 128), dtype=np.float32),
        "onescol128": np.ones((128, 1), dtype=np.float32),
        "cbias": np.broadcast_to(np.array(
            [EPS, -1.0, -LAMBDA_INIT, 0.0, 0.0, 0.0, 0.0, 0.0],
            dtype=np.float32)[None, :], (128, 8)).copy(),
        "masks": _masks(),
    }
    in_maps = []
    for c in range(NCORES):
        bA, bB = c, 15 - c
        m = dict(shared)
        m["x_own"] = C(np.concatenate(
            [x[512 * bA:512 * (bA + 1)], x[512 * bB:512 * (bB + 1)]], axis=0))
        m["itab"] = _item_table(c)
        in_maps.append(m)

    if "nc" not in _NC_CACHE:
        _NC_CACHE["nc"] = build_nc()
    nc = _NC_CACHE["nc"]
    res = run_bass_kernel_spmd(nc, in_maps, core_ids=list(range(NCORES)))

    out = np.empty((8192, D), dtype=np.float32)
    for c in range(NCORES):
        y = np.asarray(res.results[c]["y"], dtype=np.float32)
        out[512 * c:512 * (c + 1)] = y[0:512]
        out[512 * (15 - c):512 * (16 - c)] = y[512:1024]
    return out.reshape(1, 8192, D)


# version 22
# speedup vs baseline: 1.0205x; 1.0160x over previous
import math
from contextlib import ExitStack

import numpy as np
import ml_dtypes

import concourse.bass as bass
import concourse.mybir as mybir
from concourse.bass_utils import run_bass_kernel_spmd

F32 = mybir.dt.float32
BF16 = mybir.dt.bfloat16
I32 = mybir.dt.int32
U8 = mybir.dt.uint8
AF = mybir.ActivationFunctionType
ALU = mybir.AluOpType
AX = mybir.AxisListType

NCORES = 8
D = 2048
TOK = 1024
HID = 5632
NHC = 44
NQ = 4
HCQ = NHC // NQ
NH = 32
NDH = 16
NITEMS = 17
LAMBDA_INIT = 0.8 - 0.6 * math.exp(-0.3 * 12)
EPS = 1e-5
NEGBIG = -1e9
BF = ml_dtypes.bfloat16
DEBUG = False


class Prog:
    ENGINES = ("sync", "gpsimd", "vector", "scalar", "tensor")

    def __init__(self):
        self.ops = {e: [] for e in self.ENGINES}
        self.cnt = {}

    QUEUES = ("qS", "qG", "qA")

    def emit(self, engine, fn, waits=(), inc=None, inc_by=1, fifo=True):
        ev = None
        w = [x for x in waits if x]
        if inc is not None:
            if fifo and inc in self.QUEUES and self.cnt.get(inc, 0) > 0:
                w.append((inc, self.cnt[inc]))
            self.cnt[inc] = self.cnt.get(inc, 0) + inc_by
            ev = (inc, self.cnt[inc])
        self.ops[engine].append((fn, tuple(w), inc, inc_by))
        return ev

    def bump(self, sem):
        self.cnt[sem] = self.cnt.get(sem, 0) + 1
        return (sem, self.cnt[sem])

    def replay(self, engine, eng, sems):
        for fn, waits, inc, inc_by in self.ops[engine]:
            for (s, c) in waits:
                eng.wait_ge(sems[s], c)
            ins = fn(eng, sems)
            if inc is not None:
                ins.then_inc(sems[inc], inc_by)


def build_nc():
    nc = bass.Bass("TRN2", target_bir_lowering=False, debug=False)

    dx = nc.dram_tensor("x_own", [TOK, D], F32, kind="ExternalInput").ap()
    dwqT = nc.dram_tensor("wqT", [D, D], BF16, kind="ExternalInput").ap()
    dwkT = nc.dram_tensor("wkT", [D, 512], BF16, kind="ExternalInput").ap()
    dwvT = nc.dram_tensor("wvT", [D, 512], BF16, kind="ExternalInput").ap()
    dwoT = nc.dram_tensor("woT", [D, D], BF16, kind="ExternalInput").ap()
    dwg8 = nc.dram_tensor("wg8", [NHC, 128, D], U8, kind="ExternalInput").ap()
    dwu8 = nc.dram_tensor("wu8", [NHC, 128, D], U8, kind="ExternalInput").ap()
    dwd8 = nc.dram_tensor("wd8", [HID, D], U8, kind="ExternalInput").ap()
    dsg = nc.dram_tensor("sgpp", [128, 48], F32, kind="ExternalInput").ap()
    dsu = nc.dram_tensor("supp", [128, 48], F32, kind="ExternalInput").ap()
    dsd = nc.dram_tensor("sdb", [128, D], F32, kind="ExternalInput").ap()
    dlnw = nc.dram_tensor("lnw", [128, 8], F32, kind="ExternalInput").ap()
    dlam = nc.dram_tensor("lam4", [1, 256], F32, kind="ExternalInput").ap()
    dident_b = nc.dram_tensor("ident_b", [128, 128], BF16, kind="ExternalInput").ap()
    dones_b = nc.dram_tensor("ones_b", [128, 128], BF16, kind="ExternalInput").ap()
    dzrow_b = nc.dram_tensor("zrow_b", [1, 512], BF16, kind="ExternalInput").ap()
    donesrow_b = nc.dram_tensor("onesrow_b", [1, 512], BF16, kind="ExternalInput").ap()
    donescol_f = nc.dram_tensor("onescol_f", [1, 128], F32, kind="ExternalInput").ap()
    donescol128 = nc.dram_tensor("onescol128", [128, 1], F32, kind="ExternalInput").ap()
    dmasks = nc.dram_tensor("masks", [128, 2048], BF16, kind="ExternalInput").ap()
    dcbias = nc.dram_tensor("cbias", [128, 8], F32, kind="ExternalInput").ap()
    ditab = nc.dram_tensor("itab", [1, 3 * NITEMS], I32, kind="ExternalInput").ap()
    dy = nc.dram_tensor("y", [TOK, D], F32, kind="ExternalOutput").ap()
    if DEBUG:
        dbg_xn = nc.dram_tensor("dbg_xn", [128, 16 * TOK], BF16, kind="ExternalOutput").ap()
        dbg_qfm = nc.dram_tensor("dbg_qfm", [128, 16 * TOK], BF16, kind="ExternalOutput").ap()
        dbg_kall = nc.dram_tensor("dbg_kall", [NCORES * 512, TOK], BF16, kind="ExternalOutput").ap()
        dbg_vall = nc.dram_tensor("dbg_vall", [NCORES * 4 * TOK, 128], BF16, kind="ExternalOutput").ap()
        dbg_acc = nc.dram_tensor("dbg_acc", [128, 8 * D], BF16, kind="ExternalOutput").ap()
        dbg_negm = nc.dram_tensor("dbg_negm", [128, 18], F32, kind="ExternalOutput").ap()
        dbg_ofm = nc.dram_tensor("dbg_ofm", [128, NDH * TOK], BF16, kind="ExternalOutput").ap()
        dbg_l = nc.dram_tensor("dbg_l", [2, 1024], F32, kind="ExternalOutput").ap()
        dbg_h0 = nc.dram_tensor("dbg_h0", [128, HCQ * TOK], BF16, kind="ExternalOutput").ap()
        dbg_acc0 = nc.dram_tensor("dbg_acc0", [128, 8 * D], BF16, kind="ExternalOutput").ap()
        dbg_kv = nc.dram_tensor("dbg_kv", [128, 2048], BF16, kind="ExternalOutput").ap()
        dbg_kv16 = nc.dram_tensor("dbg_kv16", [128, 2048], BF16, kind="ExternalOutput").ap()
        dbg_wo = nc.dram_tensor("dbg_wo", [128, 16 * D], BF16, kind="ExternalOutput").ap()
        dbg_accpre = nc.dram_tensor("dbg_accpre", [128, 8 * D], BF16, kind="ExternalOutput").ap()
        dbg_accw = nc.dram_tensor("dbg_accw", [128, 8 * D], BF16, kind="ExternalOutput").ap()
        dbg_o0 = nc.dram_tensor("dbg_o0", [128, 2048], BF16, kind="ExternalOutput").ap()
        dbg_l0 = nc.dram_tensor("dbg_l0", [2, 1024], F32, kind="ExternalOutput").ap()
        dbg_gate = nc.dram_tensor("dbg_gate", [128, 512], F32, kind="ExternalOutput").ap()
        dbg_up = nc.dram_tensor("dbg_up", [128, 512], F32, kind="ExternalOutput").ap()
        dbg_sil = nc.dram_tensor("dbg_sil", [128, 512], BF16, kind="ExternalOutput").ap()
        dbg_wgd = nc.dram_tensor("dbg_wgd", [128, D], BF16, kind="ExternalOutput").ap()

    kst_d = nc.dram_tensor("k_stage", [512, TOK], BF16, kind="Internal")
    vst_d = nc.dram_tensor("v_stage", [4 * TOK, 128], BF16, kind="Internal")
    knst_d = nc.dram_tensor("kn_stage", [1, 8], F32, kind="Internal")
    kall_d = nc.dram_tensor("k_all", [NCORES * 512, TOK], BF16, kind="Internal")
    vall_d = nc.dram_tensor("v_all", [NCORES * 4 * TOK, 128], BF16, kind="Internal")
    knall_d = nc.dram_tensor("kn_all", [1, 8], F32, kind="Internal")
    vcur_d = nc.dram_tensor("v_cur", [8 * TOK + 16, 128], BF16, kind="Internal")

    off = [16384]

    def sb(name, shape, dtype, at=None):
        nbytes = shape[1] * {F32: 4, BF16: 2, I32: 4, U8: 1}[dtype]
        if at is None:
            at = off[0]
            off[0] = at + ((nbytes + 31) // 32) * 32
        return nc.alloc_sbuf_tensor_at(name, shape, dtype, offset=at)

    ident_b = sb("ident_b", [128, 128], BF16)
    ones_b = sb("ones_b", [128, 128], BF16)
    zrow_b = sb("zrow_b", [1, 512], BF16)
    onesrow_b = sb("onesrow_b", [1, 512], BF16)
    onescol_f = sb("onescol_f", [1, 128], F32)
    onescol128 = sb("onescol128", [128, 1], F32)
    cbias = sb("cbias", [128, 8], F32)
    sdb = sb("sdb", [128, D], F32)
    sgpp = sb("sgpp", [128, 48], F32)
    supp = sb("supp", [128, 48], F32)
    lnw = sb("lnw", [128, 8], F32)
    lnw_s = sb("lnw_s", [128, 8], F32)
    lam4 = sb("lam4", [1, 256], F32)
    lscr = sb("lscr", [1, 128], F32)
    stats = sb("stats", [128, 16], F32)
    nmq = sb("nmq", [1, 32], F32)
    nmk = sb("nmk", [1, 8], F32)
    mrow = sb("mrow", [1, 32], F32)
    bcrow = sb("bcrow", [1, 18], F32)
    negmpp = sb("negmpp", [128, 24], F32)
    itab = sb("itab", [1, 3 * NITEMS], I32)
    rpp = sb("rpp", [128, 16], F32)
    rbn = sb("rbn", [128, 8], F32)
    ssq8 = sb("ssq8", [128, 8], F32)
    rst8 = sb("rst8", [128, 8], F32)
    acc = sb("acc", [128, 8 * D], BF16)
    C0 = off[0]
    ofm = sb("ofm", [128, NDH * TOK], BF16)
    wdr2 = sb("wdr2", [128, 512], BF16, at=C0 + 24576)
    wdr3 = sb("wdr3", [128, 512], BF16, at=C0 + 25600)
    wdd2 = sb("wdd2", [128, 512], BF16, at=C0 + 26624)
    wdd3 = sb("wdd3", [128, 512], BF16, at=C0 + 27648)
    wdr4 = sb("wdr4", [128, 512], BF16, at=C0 + 28672)
    wdr5 = sb("wdr5", [128, 512], BF16, at=C0 + 29696)
    wdd4 = sb("wdd4", [128, 512], BF16, at=C0 + 30720)
    wdd5 = sb("wdd5", [128, 512], BF16, at=C0 + 31744)
    D0 = off[0]
    xnfm = sb("xnfm", [128, 16 * TOK], BF16)
    ksb = sb("ksb", [64, 8704], BF16, at=D0)
    vsb = sb("vsb", [128, 8192], BF16, at=D0 + 16384)
    E0 = off[0]
    qfm = sb("qfm", [128, 16 * TOK], BF16)
    wo_all = sb("wo_all", [128, 16 * D], BF16, at=D0)
    assert E0 - D0 == 32 * 1024
    F0 = off[0]
    hfm = sb("hfm", [128, HCQ * TOK], BF16)
    pt = sb("pt", [128, 4096], BF16, at=F0)
    o1s = sb("o1s", [128, 1024], BF16, at=F0 + 8192)
    o2s = sb("o2s", [128, 1024], BF16, at=F0 + 10240)
    kstat = sb("kstat", [128, 1024], BF16, at=F0 + 12288)
    vstat = sb("vstat", [128, 1024], BF16, at=F0 + 14336)
    onb = sb("onb", [128, 1024], BF16, at=F0 + 16384)
    vstat2 = sb("vstat2", [128, 1024], BF16, at=F0 + 18432)
    kfm = sb("kfm", [128, 4096], BF16, at=F0)
    assert 22 * 1024 >= 20480
    G0 = off[0]
    xs0 = sb("xs0", [128, D], F32, at=G0)
    xs1 = sb("xs1", [128, D], F32, at=G0 + 8192)
    la_sb = sb("la_sb", [1, 1024], F32, at=G0)
    lb_sb = sb("lb_sb", [1, 1024], F32, at=G0 + 4096)
    ra_sb = sb("ra_sb", [1, 1024], F32, at=G0 + 8192)
    rb_sb = sb("rb_sb", [1, 1024], F32, at=G0 + 12288)
    wvsb = sb("wvsb", [128, 8192], BF16, at=G0 + 16384)
    out0 = sb("out0", [128, D], F32, at=G0 + 16384)
    out1 = sb("out1", [128, D], F32, at=G0 + 24576)
    masks = sb("masks", [128, 2048], BF16, at=G0 + 16384)
    odb = sb("odb", [128, 1024], F32, at=G0 + 24576)
    scr1 = sb("scr1", [128, 1024], F32, at=G0 + 28672)
    wqs0 = sb("wqs0", [128, D], BF16, at=G0 + 32768)
    wqs1 = sb("wqs1", [128, D], BF16, at=G0 + 36864)
    wgr0 = sb("wgr0", [128, D], BF16, at=G0)
    wgr1 = sb("wgr1", [128, D], BF16, at=G0 + 4096)
    wur0 = sb("wur0", [128, D], BF16, at=G0 + 8192)
    wur1 = sb("wur1", [128, D], BF16, at=G0 + 12288)
    wgd0 = sb("wgd0", [128, D], BF16, at=G0 + 16384)
    wgd1 = sb("wgd1", [128, D], BF16, at=G0 + 20480)
    wud0 = sb("wud0", [128, D], BF16, at=G0 + 24576)
    wud1 = sb("wud1", [128, D], BF16, at=G0 + 28672)
    wdr0 = sb("wdr0", [128, 512], BF16, at=G0 + 32768)
    wdr1 = sb("wdr1", [128, 512], BF16, at=G0 + 33792)
    wdd0 = sb("wdd0", [128, 512], BF16, at=G0 + 34816)
    wdd1 = sb("wdd1", [128, 512], BF16, at=G0 + 35840)
    ds_scr = sb("ds_scr", [128, 512], F32, at=G0 + 36864)
    sil_scr = sb("sil_scr", [128, 1024], BF16, at=G0 + 38912)
    G_end = G0 + 40960
    xc = sb("xc", [128, D], F32, at=F0)
    sqs = sb("sqs", [128, 2048], F32, at=F0 + 8192)
    sqsb = sb("sqsb", [128, 2048], BF16, at=F0 + 8192)
    total = max(G_end, off[0])
    assert total <= 229312, f"SBUF over budget: {total}"

    xcb = xc.bitcast(BF16)
    vtmb = sqs.bitcast(BF16)

    PST = nc.alloc_psum_tensor("PST", [128, 2048], F32)
    PCt = nc.alloc_psum_tensor("PCt", [128, 1024], F32)
    PDt = nc.alloc_psum_tensor("PDt", [128, 1024], F32)
    PSTb = PST.bitcast(BF16)
    PCb = PCt.bitcast(BF16)
    PDb = PDt.bitcast(BF16)

    class PView:
        def __init__(self, t, off, width):
            self.t, self.off, self.width = t, off, width

        def __getitem__(self, idx):
            p, f = idx
            a = 0 if f.start is None else f.start
            b = self.width if f.stop is None else f.stop
            return self.t[p, a + self.off:b + self.off]

    PA = PView(PST, 0, 1024)
    PB = PView(PST, 1024, 1024)
    PC = PView(PCt, 0, 1024)
    PD = PView(PDt, 0, 1024)
    PAb = PView(PSTb, 0, 2048)
    PBb = PView(PSTb, 2048, 2048)

    sem_names = ["qS", "qG", "qA", "sPE", "sDVE", "sACT", "cc"]
    p = Prog()

    def dma(q, out, in_, waits=(), fifo=True):
        eng = {"qS": "sync", "qG": "gpsimd", "qA": "scalar"}[q]
        return p.emit(eng, lambda e, s, o=out, i=in_: e.dma_start(out=o, in_=i),
                      waits=waits, inc=q, inc_by=16, fifo=fifo)

    def op(engine, fn, waits=(), ev=False):
        sem = {"vector": "sDVE", "scalar": "sACT", "tensor": "sPE"}[engine]
        return p.emit(engine, fn, waits=waits, inc=(sem if ev else None))

    ev_ident = dma("qA", ident_b[:, :], dident_b)
    ev_cbias = dma("qA", cbias[:, :], dcbias)
    for (dst, src) in [(ones_b, dones_b), (zrow_b, dzrow_b),
                       (onesrow_b, donesrow_b), (onescol_f, donescol_f),
                       (onescol128, donescol128),
                       (sdb, dsd), (sgpp, dsg), (supp, dsu),
                       (lnw, dlnw), (itab, ditab)]:
        dma("qA", dst[:, :], src)
    ev_consts = dma("qA", lam4[0:1, :], dlam)

    xs = [xs0, xs1]
    ev_xn_act = [None] * 8
    ev_trans = [None] * 8
    ev_xn_evac = [None] * 8
    for tc in range(8):
        sc = 8 * (tc % 2)
        t = xs[tc % 2]
        ev_xld = dma("qS", t[:, :], dx[128 * tc:128 * (tc + 1), :],
                     waits=([ev_xn_act[tc - 2]] if tc >= 2 else []))
        e1 = op("vector", lambda e, s, t=t, sc=sc: e.reduce_sum(
            stats[:, sc:sc + 1], t[:, :], AX.X, negate=True),
            waits=[ev_xld] + ([ev_xn_act[tc - 2]] if tc >= 2 else []), ev=True)
        e2 = op("vector", lambda e, s, t=t, sc=sc: e.scalar_tensor_tensor(
            sqs[:, 0:D], t[:, :], 1.0, t[:, :], ALU.bypass, ALU.mult,
            accum_out=stats[:, sc + 2:sc + 3]),
            waits=[ev_trans[tc - 1]] if tc >= 1 else [], ev=True)
        e3 = op("scalar", lambda e, s, sc=sc: e.activation(
            stats[:, sc + 1:sc + 2], stats[:, sc:sc + 1], AF.Copy,
            scale=1.0 / D), waits=[e1], ev=True)
        e4 = op("scalar", lambda e, s, sc=sc: e.activation(
            stats[:, sc + 3:sc + 4], stats[:, sc + 2:sc + 3], AF.Copy,
            scale=1.0 / D), waits=[e2], ev=True)
        e5 = op("vector", lambda e, s, sc=sc: e.scalar_tensor_tensor(
            stats[:, sc + 4:sc + 5], stats[:, sc + 1:sc + 2], 1.0,
            stats[:, sc + 1:sc + 2], ALU.bypass, ALU.mult),
            waits=[e3], ev=True)
        e6 = op("vector", lambda e, s, sc=sc: e.tensor_sub(
            stats[:, sc + 5:sc + 6], stats[:, sc + 3:sc + 4],
            stats[:, sc + 4:sc + 5]), waits=[e4, e5], ev=True)
        e7 = op("scalar", lambda e, s, sc=sc: e.activation(
            stats[:, sc + 6:sc + 7], stats[:, sc + 5:sc + 6], AF.Sqrt,
            bias=cbias[:, 0:1]),
            waits=[e6] + ([ev_cbias] if tc == 0 else []), ev=True)
        e8 = op("vector", lambda e, s, sc=sc: e.reciprocal(
            stats[:, sc + 7:sc + 8], stats[:, sc + 6:sc + 7]),
            waits=[e7], ev=True)
        e9 = op("vector", lambda e, s, sc=sc: e.scalar_tensor_tensor(
            stats[:, sc + 4:sc + 5], stats[:, sc + 1:sc + 2], 1.0,
            stats[:, sc + 7:sc + 8], ALU.bypass, ALU.mult),
            waits=[e8], ev=True)
        ev_xn_act[tc] = op("vector", lambda e, s, t=t, sc=sc: e.tensor_scalar(
            xcb[:, 0:D], t[:, :], stats[:, sc + 7:sc + 8],
            stats[:, sc + 4:sc + 5], ALU.mult, ALU.add),
            waits=[e9] + ([ev_trans[tc - 1]] if tc >= 1 else []), ev=True)
        for i in range(16):
            ev_trans[tc] = op("tensor", lambda e, s, i=i: e.transpose(
                PAb[:, 128 * i:128 * (i + 1)], xcb[:, 128 * i:128 * (i + 1)],
                ident_b[:, :]),
                waits=([ev_ident, ev_xn_act[tc]] if i == 0 else []) +
                      ([ev_xn_evac[tc - 1]] if (i == 0 and tc >= 1) else []),
                ev=(i == 15))
        ev_xn_evac[tc] = op("vector", lambda e, s, tc=tc: e.tensor_copy(
            xnfm[:, :].rearrange("p (i t) -> p i t", i=16)[:, :, 128 * tc:128 * (tc + 1)],
            PAb[:, :].rearrange("p (i t) -> p i t", i=16)),
            waits=[ev_trans[tc]], ev=True)
    last_xn_evac = ev_xn_evac[7]

    if DEBUG:
        dma("qA", dbg_xn, xnfm[:, :], waits=[last_xn_evac])

    wqs = [wqs0, wqs1]
    ev_qev = [None] * 16
    for oc in range(16):
        ev_wq = dma("qG", wqs[oc % 2][:, :].rearrange("p (i o) -> p i o", i=16),
                    dwqT[:, 128 * oc:128 * (oc + 1)]
                    .rearrange("(i p) o -> p i o", p=128),
                    waits=[ev_qev[oc - 2]] if oc >= 2 else [])
        for th in range(2):
            mm_ev = None
            for ic in range(16):
                mm_ev = op("tensor", lambda e, s, oc=oc, th=th, ic=ic: e.matmul(
                    PB[:, 512 * th:512 * (th + 1)],
                    wqs[oc % 2][:, 128 * ic:128 * (ic + 1)],
                    xnfm[:, 1024 * ic + 512 * th:1024 * ic + 512 * (th + 1)],
                    start=(ic == 0), stop=(ic == 15), skip_group_check=True),
                    waits=([ev_wq, last_xn_evac] if (th == 0 and ic == 0) else []) +
                          ([ev_qev[oc - 1]] if (oc >= 1 and th == 0 and ic == 0) else []),
                    ev=(ic == 15))
            ev_qev[oc] = op("scalar", lambda e, s, oc=oc, th=th: e.activation(
                qfm[:, 1024 * oc + 512 * th:1024 * oc + 512 * (th + 1)],
                PB[:, 512 * th:512 * (th + 1)], AF.Copy, scale=0.125),
                waits=[mm_ev], ev=True)
    ev_kev = [None] * 4
    ev_kst = [None] * 4
    for oc in range(4):
        ev_wk = dma("qG", wqs[oc % 2][:, :].rearrange("p (i o) -> p i o", i=16),
                    dwkT[:, 128 * oc:128 * (oc + 1)]
                    .rearrange("(i p) o -> p i o", p=128),
                    waits=[ev_qev[14 + oc]] if oc < 2 else [ev_kev[oc - 2]])
        for th in range(2):
            mm_ev = None
            for ic in range(16):
                mm_ev = op("tensor", lambda e, s, oc=oc, th=th, ic=ic: e.matmul(
                    PB[:, 512 * th:512 * (th + 1)],
                    wqs[oc % 2][:, 128 * ic:128 * (ic + 1)],
                    xnfm[:, 1024 * ic + 512 * th:1024 * ic + 512 * (th + 1)],
                    start=(ic == 0), stop=(ic == 15), skip_group_check=True),
                    waits=([ev_wk] if (th == 0 and ic == 0) else []) +
                          ([ev_qev[15] if oc == 0 else ev_kev[oc - 1]]
                           if (th == 0 and ic == 0) else []),
                    ev=(ic == 15))
            ev_kev[oc] = op("scalar", lambda e, s, oc=oc, th=th: e.activation(
                kfm[:, 1024 * oc + 512 * th:1024 * oc + 512 * (th + 1)],
                PB[:, 512 * th:512 * (th + 1)], AF.Copy), waits=[mm_ev], ev=True)
        ev_kst[oc] = dma("qA", kst_d[128 * oc:128 * (oc + 1), :],
                         kfm[:, 1024 * oc:1024 * (oc + 1)], waits=[ev_kev[oc]])
    ev_wv = dma("qG", wvsb[:, :].rearrange("p (i v) -> p i v", i=16),
                dwvT.rearrange("(i p) v -> p i v", p=128),
                waits=[ev_qev[15]])
    ev_vst = {}
    ev_vev = [None] * 8
    for tc in range(8):
        mm_ev = None
        for ic in range(16):
            mm_ev = op("tensor", lambda e, s, tc=tc, ic=ic: e.matmul(
                PC[:, 0:512],
                xnfm[:, 1024 * ic + 128 * tc:1024 * ic + 128 * (tc + 1)],
                wvsb[:, 512 * ic:512 * (ic + 1)],
                start=(ic == 0), stop=(ic == 15), skip_group_check=True),
                waits=([ev_wv] if (tc == 0 and ic == 0) else []) +
                      ([ev_vev[tc - 1]] if (tc >= 1 and ic == 0) else []),
                ev=(ic == 15))
        ev_vev[tc] = op("vector", lambda e, s, tc=tc: e.tensor_copy(
            vtmb[:, 512 * (tc % 2):512 * (tc % 2 + 1)], PC[:, 0:512]),
            waits=[mm_ev] + ([ev_vst[(tc - 2, 3)]] if tc >= 2 else []), ev=True)
        for vh in range(4):
            ev_vst[(tc, vh)] = dma(
                "qA", vst_d[1024 * vh + 128 * tc:1024 * vh + 128 * (tc + 1), :],
                vtmb[:, 512 * (tc % 2) + 128 * vh:512 * (tc % 2) + 128 * (vh + 1)],
                waits=[ev_vev[tc]] if vh == 0 else [])
    ev_vst_last = ev_vst[(7, 3)]


    ev_knorm = None
    for oc in range(4):
        esq = op("scalar", lambda e, s, oc=oc: e.activation(
            sqsb[:, 0:1024], kfm[:, 1024 * oc:1024 * (oc + 1)], AF.Square),
            waits=([ev_kev[oc], ev_vst_last] if oc == 0 else [ev_kev[oc]]),
            ev=True)
        for half in range(2):
            kh = 2 * oc + half
            op("tensor", lambda e, s, half=half: e.matmul(
                PD[0:1, 0:512], ones_b[64 * half:64 * half + 64, 0:1],
                sqsb[64 * half:64 * half + 64, 0:512],
                start=True, stop=True, skip_group_check=True),
                waits=[esq] + ([ev_knorm] if ev_knorm else []))
            m2 = op("tensor", lambda e, s, half=half: e.matmul(
                PD[0:1, 512:1024], ones_b[64 * half:64 * half + 64, 0:1],
                sqsb[64 * half:64 * half + 64, 512:1024],
                start=True, stop=True, skip_group_check=True), ev=True)
            ev_knorm = op("vector", lambda e, s, kh=kh: e.reduce_max(
                nmk[0:1, kh:kh + 1], PD[0:1, 0:1024], AX.X),
                waits=[m2], ev=True)
    ev_knst = dma("qA", knst_d.ap(), nmk[0:1, :], waits=[ev_knorm])
    ev_qnorm = None
    for oc in range(16):
        esq = op("scalar", lambda e, s, oc=oc: e.activation(
            sqsb[:, 0:1024], qfm[:, 1024 * oc:1024 * (oc + 1)], AF.Square),
            waits=([ev_knorm] if oc == 0 else []), ev=True)
        for half in range(2):
            h = 2 * oc + half
            op("tensor", lambda e, s, half=half: e.matmul(
                PD[0:1, 0:512], ones_b[64 * half:64 * half + 64, 0:1],
                sqsb[64 * half:64 * half + 64, 0:512],
                start=True, stop=True, skip_group_check=True),
                waits=[esq, ev_qnorm if ev_qnorm else ev_knorm])
            m2 = op("tensor", lambda e, s, half=half: e.matmul(
                PD[0:1, 512:1024], ones_b[64 * half:64 * half + 64, 0:1],
                sqsb[64 * half:64 * half + 64, 512:1024],
                start=True, stop=True, skip_group_check=True), ev=True)
            ev_qnorm = op("vector", lambda e, s, h=h: e.reduce_max(
                nmq[0:1, h:h + 1], PD[0:1, 0:1024], AX.X),
                waits=[m2], ev=True)

    rg = [list(range(NCORES))]
    p.emit("gpsimd", lambda e, s: e.collective_compute(
        "AllGather", ALU.bypass, replica_groups=rg,
        ins=[kst_d.ap().opt()], outs=[kall_d.ap().opt()]),
        waits=[ev_kst[3], ev_vst_last], inc="cc")
    p.emit("gpsimd", lambda e, s: e.collective_compute(
        "AllGather", ALU.bypass, replica_groups=rg,
        ins=[vst_d.ap().opt()], outs=[vall_d.ap().opt()]),
        waits=[ev_vst_last], inc="cc")
    p.emit("gpsimd", lambda e, s: e.collective_compute(
        "AllReduce", ALU.max, replica_groups=rg,
        ins=[knst_d.ap().opt()], outs=[knall_d.ap().opt()]),
        waits=[ev_knst], inc="cc")

    if DEBUG:
        dma("qA", dbg_qfm, qfm[:, :], waits=[ev_qev[15]])
        dma("qA", dbg_kall, kall_d.ap(), waits=[("cc", 3)])
        dma("qA", dbg_vall, vall_d.ap(), waits=[("cc", 3)])

    def emit_mlambda():
        ev_knall = dma("qA", nmk[0:1, :], knall_d.ap(), waits=[("cc", 3)])
        e_sq1 = op("scalar", lambda e, s: e.activation(
            nmk[0:1, :], nmk[0:1, :], AF.Sqrt), waits=[ev_knall], ev=True)
        e_sq2 = op("scalar", lambda e, s: e.activation(
            nmq[0:1, :], nmq[0:1, :], AF.Sqrt), waits=[ev_qnorm], ev=True)
        e_m = None
        for h in range(32):
            e_m = op("vector", lambda e, s, h=h: e.tensor_mul(
                mrow[0:1, h:h + 1], nmq[0:1, h:h + 1],
                nmk[0:1, h // 4:h // 4 + 1]),
                waits=([e_sq1, e_sq2] if h == 0 else []), ev=(h == 31))
        e_mp = op("vector", lambda e, s: e.tensor_tensor(
            bcrow[0:1, 0:16],
            mrow[0:1, :].rearrange("a (h two) -> a h two", two=2)[:, :, 0],
            mrow[0:1, :].rearrange("a (h two) -> a h two", two=2)[:, :, 1],
            ALU.max), waits=[e_m], ev=True)
        e_neg = op("vector", lambda e, s: e.tensor_scalar(
            bcrow[0:1, 0:16], bcrow[0:1, 0:16], -1.0, -1.0,
            ALU.mult, ALU.add), waits=[e_mp], ev=True)
        e_l1 = op("vector", lambda e, s: e.tensor_mul(
            lscr[0:1, 0:64], lam4[0:1, 0:64], lam4[0:1, 64:128]),
            waits=[ev_consts], ev=True)
        e_s1 = op("vector", lambda e, s: e.reduce_sum(
            lscr[0:1, 64:65], lscr[0:1, 0:64], AX.X), waits=[e_l1], ev=True)
        e_e1 = op("scalar", lambda e, s: e.activation(
            lscr[0:1, 66:67], lscr[0:1, 64:65], AF.Exp), waits=[e_s1], ev=True)
        e_l2 = op("vector", lambda e, s: e.tensor_mul(
            lscr[0:1, 0:64], lam4[0:1, 128:192], lam4[0:1, 192:256]),
            waits=[e_e1], ev=True)
        e_s2 = op("vector", lambda e, s: e.reduce_sum(
            lscr[0:1, 65:66], lscr[0:1, 0:64], AX.X), waits=[e_l2], ev=True)
        e_e2 = op("scalar", lambda e, s: e.activation(
            lscr[0:1, 68:69], lscr[0:1, 65:66], AF.Exp), waits=[e_s2], ev=True)
        e_lf = op("vector", lambda e, s: e.tensor_sub(
            lscr[0:1, 70:71], lscr[0:1, 66:67], lscr[0:1, 68:69]),
            waits=[e_e2], ev=True)
        e_lf2 = op("vector", lambda e, s: e.tensor_scalar(
            bcrow[0:1, 16:17], lscr[0:1, 70:71], -1.0, -LAMBDA_INIT,
            ALU.mult, ALU.add), waits=[e_lf], ev=True)
        return e_neg, e_lf2

    def emit_down_prefetch(q):
        evs = {}
        for j in range(6):
            jp = (6, 7, 8, 9, 10, 5)[j]
            wd_w = [] if q == 0 else [ev_ddq[(q - 1, 3, jp)]]
            evs[j] = dma("qG", wdr[j][:, :],
                         dwd8[128 * (HCQ * q + j):128 * (HCQ * q + j + 1),
                              0:512], waits=wd_w)
        return evs

    down_pre = {}
    gw_pre = {}
    mlam_evs = None
    wgr = [wgr0, wgr1]
    wur = [wur0, wur1]
    wgd = [wgd0, wgd1]
    wud = [wud0, wud1]
    wdr = [wdr0, wdr1, wdr2, wdr3, wdr4, wdr5]
    wdd = [wdd0, wdd1, wdd2, wdd3, wdd4, wdd5]
    ev_mmj_prev = None
    ev_h = [None] * NHC
    ev_gdq = [None] * NHC
    ev_udq = [None] * NHC
    ev_ddq = {}
    ev_acc = {}
    last_xn_read = None
    ev_es_first = True
    for q in range(NQ):
        if q == 0:
            down_pre[0] = emit_down_prefetch(0)
        if q == 1:
            mlam_evs = emit_mlambda()
        for j in range(HCQ):
            H = HCQ * q + j
            pp = H % 2
            if (H) in gw_pre:
                ev_wg, ev_wu = gw_pre[H]
            else:
                wg_w = []
                if H == 0:
                    wg_w = [last_xn_evac, ev_vst_last]
                elif H >= 2:
                    wg_w = [ev_udq[H - 2]]
                ev_wg = dma("qG", wgr[pp][:, :], dwg8[H, :, :], waits=wg_w)
                ev_wu = dma("qG", wur[pp][:, :], dwu8[H, :, :])
            ev_gdq[H] = op("vector", lambda e, s, pp=pp: e.tensor_scalar(
                wgd[pp][:, :], wgr[pp][:, :], -128.0, None, ALU.add),
                waits=[ev_wg] + ([ev_h[H - 2]] if H >= 2 else []), ev=True)
            ev_udq[H] = op("vector", lambda e, s, pp=pp: e.tensor_scalar(
                wud[pp][:, :], wur[pp][:, :], -128.0, None, ALU.add),
                waits=[ev_wu], ev=True)
            for th in range(2):
                mw = []
                if th == 0:
                    mw = [ev_gdq[H], ev_udq[H]]
                    if j == 0:
                        mw.append(ev_qnorm if q == 0 else ev_acc[(q - 1, 3, 7)])
                        if DEBUG and q == 1:
                            mw.append(dbg_ffn_ev[0])
                    if H >= 1:
                        mw.append(ev_h[H - 1])
                mmg = mmu = None
                for ic in range(16):
                    mmg = op("tensor", lambda e, s, pp=pp, th=th, ic=ic: e.matmul(
                        PA[:, 512 * th:512 * (th + 1)],
                        wgd[pp][:, 128 * ic:128 * (ic + 1)],
                        xnfm[:, 1024 * ic + 512 * th:1024 * ic + 512 * (th + 1)],
                        start=(ic == 0), stop=(ic == 15), skip_group_check=True),
                        waits=(mw if ic == 0 else []), ev=(ic == 15))
                for ic in range(16):
                    mmu = op("tensor", lambda e, s, pp=pp, th=th, ic=ic: e.matmul(
                        PB[:, 512 * th:512 * (th + 1)],
                        wud[pp][:, 128 * ic:128 * (ic + 1)],
                        xnfm[:, 1024 * ic + 512 * th:1024 * ic + 512 * (th + 1)],
                        start=(ic == 0), stop=(ic == 15), skip_group_check=True),
                        ev=(ic == 15))
                if H == NHC - 1 and th == 1:
                    last_xn_read = mmu
                es_w = [mmg]
                if DEBUG and H == 0 and th == 0:
                    edw = op("vector", lambda e, s: e.tensor_copy(
                        ds_scr[:, :], PA[:, 0:512]), waits=[mmg], ev=True)
                    dma("qA", dbg_wgd, wgd0[:, :], waits=[ev_gdq[0]])
                    dbg_g_ev = dma("qA", dbg_gate, ds_scr[:, :], waits=[edw])
                    es_w.append(dbg_g_ev)
                    edw2 = op("vector", lambda e, s: e.tensor_copy(
                        ds_scr[:, :], PB[:, 0:512]), waits=[mmu, dbg_g_ev], ev=True)
                    dbg_u_ev = dma("qA", dbg_up, ds_scr[:, :], waits=[edw2])
                    es_w.append(dbg_u_ev)
                if ev_es_first:
                    es_w.append(ev_qev[15])
                    ev_es_first = False
                es = op("scalar", lambda e, s, pp=pp, th=th, H=H: e.activation(
                    sil_scr[:, 512 * th:512 * (th + 1)],
                    PA[:, 512 * th:512 * (th + 1)], AF.Silu,
                    scale=sgpp[:, H:H + 1], bias=cbias[:, 3:4]),
                    waits=es_w, ev=True)
                hw = [mmu, es]
                if DEBUG and H == 0 and th == 0:
                    hw.append(dma("qA", dbg_sil, sil_scr[:, 0:512], waits=[es]))
                ev_h[H] = op("vector",
                             lambda e, s, th=th, H=H, j=j: e.scalar_tensor_tensor(
                    hfm[:, 1024 * j + 512 * th:1024 * j + 512 * (th + 1)],
                    PB[:, 512 * th:512 * (th + 1)], supp[:, H:H + 1],
                    sil_scr[:, 512 * th:512 * (th + 1)],
                    ALU.mult, ALU.mult), waits=hw, ev=True)
        if q + 1 < NQ:
            for Hn in (HCQ * (q + 1), HCQ * (q + 1) + 1):
                gw_pre[Hn] = (dma("qG", wgr[Hn % 2][:, :], dwg8[Hn, :, :],
                                  waits=[ev_udq[Hn - 2]]),
                              dma("qG", wur[Hn % 2][:, :], dwu8[Hn, :, :]))
        ps_slices = [(PA, 0), (PA, 512), (PB, 0), (PB, 512),
                     (PC, 0), (PC, 512), (PD, 0), (PD, 512)]
        for dd in range(4):
            mm_last = [None] * 8
            ev_mmj = {}
            for j in range(HCQ):
                H = HCQ * q + j
                pp = j % 6
                jp = (6, 7, 8, 9, 10, 5)[pp]
                if dd == 0 and j < 6:
                    evd = down_pre[q][j]
                else:
                    if j >= 6:
                        wd_w = [ev_ddq[(q, dd, j - 6)]]
                    else:
                        wd_w = [ev_ddq[(q, dd - 1, jp)]]
                    evd = dma("qG", wdr[pp][:, :],
                              dwd8[128 * H:128 * (H + 1),
                                   512 * dd:512 * (dd + 1)],
                              waits=wd_w)
                dq_w = [evd]
                if j >= 6:
                    dq_w.append(ev_mmj[j - 6])
                elif ev_mmj_prev is not None:
                    dq_w.append(ev_mmj_prev[jp])
                ev_ddq[(q, dd, j)] = op(
                    "vector", lambda e, s, pp=pp: e.tensor_scalar(
                        wdd[pp][:, :], wdr[pp][:, :], -128.0, None, ALU.add),
                    waits=dq_w, ev=True)
                for tc in range(8):
                    ps, o = ps_slices[tc]
                    dmw = []
                    if tc == 0:
                        dmw = [ev_ddq[(q, dd, j)]]
                    if j == 0 and tc == 0 and dd == 0:
                        dmw += [ev_h[HCQ * q + HCQ - 1]]
                        if q == 0:
                            dmw += [ev_qnorm, ev_vev[7]]
                    if j == 0:
                        if dd >= 1:
                            dmw += [ev_acc[(q, dd - 1, tc)]]
                        elif q >= 1:
                            dmw += [ev_acc[(q - 1, 3, tc)]]
                    mm_last[tc] = op("tensor",
                                     lambda e, s, ps=ps, o=o, pp=pp, j=j, tc=tc: e.matmul(
                        ps[:, o:o + 512],
                        hfm[:, 1024 * j + 128 * tc:1024 * j + 128 * (tc + 1)],
                        wdd[pp][:, :],
                        start=(j == 0), stop=(j == HCQ - 1),
                        skip_group_check=True),
                        waits=dmw, ev=(j == HCQ - 1 or tc == 7))
                ev_mmj[j] = mm_last[7]
            ev_mmj_prev = ev_mmj
            for tc in range(8):
                ps, o = ps_slices[tc]
                if q == 0:
                    ev_acc[(q, dd, tc)] = op(
                        "vector", lambda e, s, ps=ps, o=o, tc=tc, dd=dd: e.tensor_mul(
                            acc[:, D * tc + 512 * dd:D * tc + 512 * (dd + 1)],
                            ps[:, o:o + 512], sdb[:, 512 * dd:512 * (dd + 1)]),
                        waits=[mm_last[tc]], ev=True)
                else:
                    e1d = op("vector", lambda e, s, ps=ps, o=o, dd=dd: e.tensor_mul(
                        ds_scr[:, :], ps[:, o:o + 512],
                        sdb[:, 512 * dd:512 * (dd + 1)]),
                        waits=[mm_last[tc]], ev=True)
                    ev_acc[(q, dd, tc)] = op(
                        "vector", lambda e, s, tc=tc, dd=dd: e.tensor_add(
                            acc[:, D * tc + 512 * dd:D * tc + 512 * (dd + 1)],
                            acc[:, D * tc + 512 * dd:D * tc + 512 * (dd + 1)],
                            ds_scr[:, :]), waits=[e1d], ev=True)
        if q + 1 < NQ:
            down_pre[q + 1] = emit_down_prefetch(q + 1)
        if DEBUG and q == 0:
            dbg_ffn_ev = [None]
            e1dmp = dma("qA", dbg_h0, hfm[:, :], waits=[ev_h[HCQ - 1]])
            dbg_ffn_ev[0] = dma("qA", dbg_acc0, acc[:, :],
                                waits=[ev_acc[(0, 3, 7)]])
    ffn_last = ev_acc[(NQ - 1, 3, 7)]
    ev_masks = dma("qA", masks[:, :], dmasks, waits=[ev_h[NHC - 1]])

    e_neg, e_lf2 = mlam_evs
    e_bc = op("tensor", lambda e, s: e.matmul(
        PD[:, 0:18], onescol_f[0:1, :], bcrow[0:1, 0:18],
        start=True, stop=True, skip_group_check=True),
        waits=[e_neg, e_lf2, ffn_last], ev=True)
    ev_negmpp = op("vector", lambda e, s: e.tensor_copy(
        negmpp[:, 0:18], PD[:, 0:18]), waits=[e_bc], ev=True)
    ev_lnws = op("scalar", lambda e, s: e.activation(
        lnw_s[:, 0:1], lnw[:, 0:1], AF.Copy, scale=(1.0 - LAMBDA_INIT)),
        waits=[ev_consts], ev=True)

    if DEBUG:
        dma("qA", dbg_acc, acc[:, :], waits=[ffn_last])
        dma("qA", dbg_negm, negmpp[:, 0:18], waits=[ev_negmpp])

    PV = PC
    LL = PD

    def mk_frozen2(name, colbase, mx):
        st = {"r": None, "rvs": [None] * NITEMS}

        def get(e, i):
            if st["r"] is None:
                st["r"] = e.alloc_register(name)
            if st["rvs"][i] is None:
                e.reg_load(st["r"], itab[0:1, colbase + i:colbase + i + 1])
                st["rvs"][i] = bass.RuntimeValue(
                    st["r"], min_val=0, max_val=mx + i)
            return st["rvs"][i]
        return get

    ve_rv = mk_frozen2("koff", 0, 7680)
    sy_rv = mk_frozen2("voff", NITEMS, (8 * TOK - 512) * 128)
    pe_state = {"r": [None, None]}

    def pe_flag(e, i):
        if pe_state["r"][0] is None:
            pe_state["r"][0] = e.alloc_register("qflag0")
            pe_state["r"][1] = e.alloc_register("qflag1")
        r = pe_state["r"][i % 2]
        e.reg_load(r, itab[0:1, 2 * NITEMS + i:2 * NITEMS + i + 1])
        return r

    def pe_flag_cached(e, j):
        return pe_state["r"][j % 2]

    dbg_kv_ev = [None]
    ev_kcache = ev_vcache = None
    ev_kv = {}
    ev_vv = {}
    ev_stq = {}
    ev_stqA = {}
    ev_exp = {}
    ev_expA = {}
    ev_msk = {}
    ev_pv = {}
    ev_evO = {}
    ev_epi_done = None
    ev_epi_ps = None
    ev_epi_act = None
    ev_epi_tr = None
    er1_ev = None
    for h in range(NH):
        d = h // 2
        kh = h // 4
        vh = h // 8
        isA = (h % 2 == 0)
        rb = 64 * (h % 2)
        if h % 4 == 0:
            kc_w = ([last_xn_read, ("cc", 3)] if h == 0
                    else [ev_kv[(h - 1, NITEMS - 1)]])
            for r in range(NCORES):
                ev_kcache = dma("qS", ksb[:, 1024 * r:1024 * (r + 1)],
                                kall_d.ap()[512 * r + 64 * kh:512 * r + 64 * (kh + 1), :],
                                waits=(kc_w if r == 0 else []))
        if h % 8 == 0:
            vc_w = [("cc", 3)] if h == 0 else [ev_vv[(h - 1, NITEMS - 1)]]
            for r in range(NCORES):
                ev_vcache = dma("qG",
                                vcur_d.ap()[1024 * r:1024 * (r + 1), :],
                                vall_d.ap()[4096 * r + 1024 * vh:4096 * r + 1024 * (vh + 1), :],
                                waits=(vc_w if r == 0 else []))
        if h == 0:
            stw0 = [ffn_last]
            pvw0 = [ev_negmpp]
        else:
            stw0 = [ev_exp[(h - 1, NITEMS - 1)]]
            pvw0 = [ev_evO[(h - 1, "o")], ev_evO[(h - 1, "l")]]
        if ev_epi_done is not None:
            pvw0.append(ev_epi_done)

        def pvl_body(e, s, ev, ssl, vbuf, vo, qa, st):
            for c in range(4):
                e.matmul(PV[:, qa:qa + 512],
                         vbuf[:, vo + 128 * c:vo + 128 * (c + 1)],
                         pt[:, ssl + 512 * c:ssl + 512 * (c + 1)],
                         start=(st and c == 0), stop=False,
                         skip_group_check=True)
            for c in range(4):
                mm = e.matmul(LL[0:64, qa:qa + 512], ones_b[:, 0:64],
                              pt[:, ssl + 512 * c:ssl + 512 * (c + 1)],
                              start=(st and c == 0), stop=False,
                              skip_group_check=True)
            mm.then_inc(s[ev[0]], 1)

        def pvl(e, s, h, j, ev):
            ssl = 2048 * (j % 2)
            vbuf = vstat if (j % 4) < 2 else vstat2
            vo = 512 * (j % 2)
            if j < 2:
                pvl_body(e, s, ev, ssl, vbuf, vo, 512 * j, True)
                return
            rf = pe_flag_cached(e, j)
            for cond in (0, 1):
                qa = 512 * cond
                cm = e.If_eq(rf, 0) if cond == 0 else e.Else()
                with cm:
                    pvl_body(e, s, ev, ssl, vbuf, vo, qa, False)
            e.end_ifs()

        for i in range(NITEMS):
            sl = i % 2
            dwk = []
            if i == 0:
                dwk = [ev_kcache, ev_consts]
            if i < 2 and h > 0:
                dwk = dwk + [ev_stq[(h - 1, NITEMS - 2 + i)]]
            if i >= 2:
                dwk = [ev_stq[(h, i - 2)]]

            def kcopy(e, s, i=i, sl=sl, rb=rb):
                v = ve_rv(e, i)
                return e.tensor_copy(kstat[rb:rb + 64, 512 * sl:512 * (sl + 1)],
                                     ksb[:, bass.ds(v, 512)])
            if DEBUG and h in (0, 16) and i == 2:
                tgt = dbg_kv if h == 0 else dbg_kv16
                dma("qA", tgt[:, 0:1024], kstat[:, :],
                    waits=[ev_kv[(h, 1)], ev_vv[(h, 1)]])
                dbg_kv_ev[0] = dma("qA", tgt[:, 1024:2048], vstat[:, :])
                dwk = dwk + [dbg_kv_ev[0]]
            ev_kv[(h, i)] = op("vector", kcopy, waits=dwk, ev=True)

            if 0 <= i - 1 < 2:
                ev_msk[(h, i - 1)] = op(
                    "vector", lambda e, s, psl=(i - 1) % 2: e.tensor_mul(
                        pt[:, 2048 * psl:2048 * (psl + 1)],
                        pt[:, 2048 * psl:2048 * (psl + 1)], masks[:, 0:2048]),
                    waits=[ev_exp[(h, i - 1)]] +
                          ([ev_masks] if (h == 0 and i == 1) else []), ev=True)

            if i == 1 and h % 2 == 1:
                w1 = [ev_evO[(h - 1, "l")]]
                if ev_epi_tr is not None:
                    w1.append(ev_epi_tr)
                er1_ev = op("vector", lambda e, s: e.reciprocal(
                    ra_sb[0:1, :], la_sb[0:1, :]), waits=w1, ev=True)

            dwv = []
            if i == 0 and h == 0:
                dwv = [ev_consts]
            if i == 0 and h % 8 == 0:
                dwv = dwv + [ev_vcache]
            if i < 4 and h > 0:
                dwv = dwv + [ev_pv[(h - 1, (16, 13, 14, 15)[i])]]
            if i < 4 and h == 0:
                dwv = dwv + [ffn_last]
            if i >= 4:
                dwv = [ev_pv[(h, i - 4)]]
            if DEBUG and h in (0, 16) and i == 2:
                dwv = dwv + [dbg_kv_ev[0]]

            def vcopy(e, s, i=i, sl=sl):
                v = sy_rv(e, i)
                vbuf = vstat if (i % 4) < 2 else vstat2
                src = bass.AP(vcur_d, v, [[128, 128], [16384, 4], [1, 128]])
                return e.dma_start(
                    out=vbuf[:, 512 * sl:512 * (sl + 1)]
                    .rearrange("p (c v) -> p c v", c=4), in_=src)
            ev_vv[(h, i)] = p.emit("gpsimd", vcopy, waits=dwv, inc="qG",
                                   inc_by=16)

            stw = [ev_kv[(h, i)]]
            stwB = []
            if i == 0:
                stw += stw0
            if i >= 1:
                stw.append(ev_expA[(h, i - 1)])
                stwB.append(ev_exp[(h, i - 1)])
            ev_stqA[(h, i)] = p.bump("sPE")

            def stmmA(e, s, h=h, i=i, sl=sl, rb=rb, ev=ev_stqA[(h, i)]):
                base = 1024 * (h // 2)
                tp = (rb, 0) if rb else None
                rf = pe_flag(e, i)
                for cond in (0, 1):
                    qa = base + 512 * cond
                    cm = e.If_eq(rf, 0) if cond == 0 else e.Else()
                    with cm:
                        rhs = qfm[rb:rb + 64, qa:qa + 512]
                        for c in range(2):
                            mm = e.matmul(
                                PST[:, 512 * c:512 * (c + 1)],
                                kstat[rb:rb + 64,
                                      512 * sl + 128 * c:512 * sl + 128 * (c + 1)],
                                rhs, start=True, stop=True,
                                skip_group_check=True, tile_position=tp)
                        mm.then_inc(s[ev[0]], 1)
                e.end_ifs()
            p.emit("tensor", stmmA, waits=stw)
            ev_stq[(h, i)] = p.bump("sPE")

            def stmmB(e, s, h=h, i=i, sl=sl, rb=rb, ev=ev_stq[(h, i)]):
                base = 1024 * (h // 2)
                tp = (rb, 0) if rb else None
                rf = pe_flag_cached(e, i)
                for cond in (0, 1):
                    qa = base + 512 * cond
                    cm = e.If_eq(rf, 0) if cond == 0 else e.Else()
                    with cm:
                        rhs = qfm[rb:rb + 64, qa:qa + 512]
                        for c in range(2, 4):
                            mm = e.matmul(
                                PST[:, 512 * c:512 * (c + 1)],
                                kstat[rb:rb + 64,
                                      512 * sl + 128 * c:512 * sl + 128 * (c + 1)],
                                rhs, start=True, stop=True,
                                skip_group_check=True, tile_position=tp)
                        mm.then_inc(s[ev[0]], 1)
                e.end_ifs()
            p.emit("tensor", stmmB, waits=stwB)

            if i >= 1:
                ev_pv[(h, i - 1)] = p.bump("sPE")
                p.emit("tensor",
                       lambda e, s, h=h, j=i - 1, ev=ev_pv[(h, i - 1)]: pvl(
                           e, s, h, j, ev),
                       waits=[ev_vv[(h, i - 1)]] +
                             ([ev_msk[(h, i - 1)]] + pvw0
                              if i - 1 < 2 else []))

            ew = [ev_stqA[(h, i)]]
            if i >= 2:
                ew.append(ev_pv[(h, i - 2)])
            elif h > 0:
                ew.append(ev_pv[(h - 1, NITEMS - 2 + i)])
            ev_expA[(h, i)] = op("scalar", lambda e, s, sl=sl, d=d: e.activation(
                pt[:, 2048 * sl:2048 * sl + 1024], PST[:, 0:1024],
                AF.Exp, bias=negmpp[:, d:d + 1]), waits=ew, ev=True)
            ev_exp[(h, i)] = op("scalar", lambda e, s, sl=sl, d=d: e.activation(
                pt[:, 2048 * sl + 1024:2048 * sl + 2048], PST[:, 1024:2048],
                AF.Exp, bias=negmpp[:, d:d + 1]),
                waits=[ev_stq[(h, i)]], ev=True)
        ev_pv[(h, NITEMS - 1)] = p.bump("sPE")
        p.emit("tensor",
               lambda e, s, h=h, j=NITEMS - 1, ev=ev_pv[(h, NITEMS - 1)]: pvl(
                   e, s, h, j, ev),
               waits=[ev_exp[(h, NITEMS - 1)], ev_vv[(h, NITEMS - 1)]])

        odst = o1s if isA else o2s
        ldst = la_sb if isA else lb_sb
        evo_w = [ev_pv[(h, NITEMS - 1)]]
        if h == 0:
            evo_w.append(ffn_last)
        ev_evO[(h, "o")] = op("vector", lambda e, s, odst=odst: e.tensor_copy(
            odst[:, :], PV[:, 0:1024]), waits=evo_w, ev=True)
        ev_evO[(h, "l")] = op("vector", lambda e, s, ldst=ldst: e.tensor_copy(
            ldst[0:1, :], LL[0:1, 0:1024]), ev=True)
        if DEBUG and h == 1:
            dma("qA", dbg_l0[0:1, :], la_sb[0:1, :], waits=[ev_evO[(h, "l")]])
            dma("qA", dbg_l0[1:2, :], lb_sb[0:1, :])
            dma("qA", dbg_o0[:, 0:1024], o1s[:, :])
            dma("qA", dbg_o0[:, 1024:2048], o2s[:, :])

        if not isA:
            er_w = [ev_evO[(h, "l")]]
            if ev_epi_tr is not None:
                er_w.append(ev_epi_tr)
            er2a = op("vector", lambda e, s: e.reciprocal(
                rb_sb[0:1, 0:512], lb_sb[0:1, 0:512]), waits=er_w, ev=True)
            er2b = op("vector", lambda e, s: e.reciprocal(
                rb_sb[0:1, 512:1024], lb_sb[0:1, 512:1024]), ev=True)
            to_ev = None
            for qq in range(8):
                to_ev = op("tensor", lambda e, s, qq=qq: e.transpose(
                    PCb[:, 128 * qq:128 * (qq + 1)],
                    o1s[:, 128 * qq:128 * (qq + 1)], ident_b[:, :]),
                    waits=([ev_evO[(h - 1, "o")], ev_evO[(h, "o")]]
                           if qq == 0 else []))
            for qq in range(8):
                to_ev = op("tensor", lambda e, s, qq=qq: e.transpose(
                    PCb[:, 1024 + 128 * qq:1024 + 128 * (qq + 1)],
                    o2s[:, 128 * qq:128 * (qq + 1)], ident_b[:, :]),
                    ev=(qq == 7))
            tr_ev = None
            for qq in range(8):
                tr_ev = op("tensor", lambda e, s, qq=qq: e.transpose(
                    PD[:, qq:qq + 1], ra_sb[0:1, 128 * qq:128 * (qq + 1)],
                    onescol_f[0:1, 0:1]),
                    waits=([er1_ev, ev_evO[(h, "l")]] if qq == 0 else []))
            for qq in range(8):
                tr_ev = op("tensor", lambda e, s, qq=qq: e.transpose(
                    PD[:, 8 + qq:8 + qq + 1],
                    rb_sb[0:1, 128 * qq:128 * (qq + 1)], onescol_f[0:1, 0:1]),
                    waits=([er2a] if qq == 0 else ([er2b] if qq == 4 else [])),
                    ev=(qq == 7))
            ev_epi_tr = tr_ev
            e_rpp = op("vector", lambda e, s: e.tensor_copy(
                rpp[:, :], PD[:, 0:16]), waits=[tr_ev], ev=True)
            e_rcp = e_rpp
            e_rbn = op("vector", lambda e, s: e.tensor_scalar(
                rbn[:, :], rpp[:, 8:16], negmpp[:, 16:17], None, ALU.mult),
                ev=True)
            e_oda = [None] * 8
            for qq in range(8):
                e_oda[qq] = op("vector", lambda e, s, qq=qq: e.tensor_scalar(
                    scr1[:, 128 * qq:128 * (qq + 1)],
                    PCb[:, 128 * qq:128 * (qq + 1)], rpp[:, qq:qq + 1], None,
                    ALU.mult),
                    waits=([to_ev, e_rcp] if qq == 0 else []), ev=True)
            e_od = [None] * 8
            for qq in range(8):
                e_od[qq] = op("vector", lambda e, s, qq=qq: e.scalar_tensor_tensor(
                    odb[:, 128 * qq:128 * (qq + 1)],
                    PCb[:, 1024 + 128 * qq:1024 + 128 * (qq + 1)], rbn[:, qq:qq + 1],
                    scr1[:, 128 * qq:128 * (qq + 1)], ALU.mult, ALU.add),
                    waits=[e_oda[qq]] + ([e_rbn] if qq == 0 else []), ev=True)
            e_ssq = None
            for qq in range(8):
                e_ssq = op("vector", lambda e, s, qq=qq: e.scalar_tensor_tensor(
                    scr1[:, 128 * qq:128 * (qq + 1)],
                    odb[:, 128 * qq:128 * (qq + 1)], 1.0,
                    odb[:, 128 * qq:128 * (qq + 1)], ALU.bypass, ALU.mult,
                    accum_out=ssq8[:, qq:qq + 1]),
                    waits=[e_od[qq]], ev=(qq == 7))
            e_std = op("scalar", lambda e, s: e.activation(
                ssq8[:, :], ssq8[:, :], AF.Sqrt, scale=1.0 / 128,
                bias=cbias[:, 0:1]), waits=[e_ssq], ev=True)
            e_rst = op("vector", lambda e, s: e.reciprocal(
                rst8[:, :], ssq8[:, :]), waits=[e_std], ev=True)
            e_on = None
            for qq in range(8):
                e_on = op("vector", lambda e, s, qq=qq: e.tensor_scalar(
                    onb[:, 128 * qq:128 * (qq + 1)],
                    odb[:, 128 * qq:128 * (qq + 1)], rst8[:, qq:qq + 1], None,
                    ALU.mult),
                    waits=([e_rst] if qq == 0 else []), ev=(qq == 7))
            ev_epi_act = e_on
            tb_ev = None
            for qq in range(8):
                tb_ev = op("tensor", lambda e, s, qq=qq: e.transpose(
                    PDb[:, 1024 + 128 * qq:1024 + 128 * (qq + 1)],
                    onb[:, 128 * qq:128 * (qq + 1)], ident_b[:, :]),
                    waits=([e_on] if qq == 0 else []), ev=(qq == 7))
            ev_epi_ps = tb_ev
            ev_epi_done = op("vector", lambda e, s, d=d: e.tensor_scalar(
                ofm[:, 1024 * d:1024 * (d + 1)], PDb[:, 1024:2048],
                lnw_s[:, 0:1], None, ALU.mult),
                waits=[tb_ev] + ([ev_lnws] if d == 0 else []), ev=True)

    attn_done = ev_epi_done
    attn_kv_done = ev_kv[(NH - 1, NITEMS - 1)]
    if DEBUG:
        dma("qA", dbg_accpre, acc[:, :], waits=[attn_done])
        dma("qA", dbg_ofm, ofm[:, :], waits=[attn_done])
        dma("qA", dbg_l[0:1, :], la_sb[0:1, :], waits=[attn_done])
        dma("qA", dbg_l[1:2, :], lb_sb[0:1, :], waits=[attn_done])

    ev_stq_last = ev_stq[(NH - 1, NITEMS - 1)]
    ev_wo_q = {}
    for ic in range(16):
        q3 = ("qG", "qS", "qA")[ic % 3]
        w = [ev_stq_last, attn_kv_done] if ic < 3 else []
        ev_wo_q[q3] = dma(q3, wo_all[:, 2048 * ic:2048 * (ic + 1)],
                          dwoT[128 * ic:128 * (ic + 1), :], waits=w,
                          fifo=False)
    ev_wo_gate = list(ev_wo_q.values())
    ev_acc2 = {}
    oc_bank = {0: (PA, 0), 1: (PB, 0), 2: (PC, 0), 3: (PC, 512)}
    oc_order = (2, 3, 0, 1)
    for tc in range(8):
        for j_oc, oc in enumerate(oc_order):
            ps, o = oc_bank[oc]
            mm = None
            for ic in range(16):
                mw = []
                if ic == 0:
                    if tc == 0 and j_oc == 0:
                        mw = ev_wo_gate + [attn_done]
                    if tc >= 1:
                        mw.append(ev_acc2[(tc - 1, oc)])
                mm = op("tensor", lambda e, s, ps=ps, o=o, ic=ic, tc=tc, oc=oc: e.matmul(
                    ps[:, o:o + 512],
                    ofm[:, 1024 * ic + 128 * tc:1024 * ic + 128 * (tc + 1)],
                    wo_all[:, 2048 * ic + 512 * oc:2048 * ic + 512 * (oc + 1)],
                    start=(ic == 0), stop=(ic == 15), skip_group_check=True),
                    waits=mw, ev=(ic == 15))
            ev_acc2[(tc, oc)] = op(
                "vector", lambda e, s, ps=ps, o=o, tc=tc, oc=oc: e.tensor_add(
                    acc[:, D * tc + 512 * oc:D * tc + 512 * (oc + 1)],
                    acc[:, D * tc + 512 * oc:D * tc + 512 * (oc + 1)],
                    ps[:, o:o + 512]), waits=[mm], ev=True)

    if DEBUG:
        dma("qA", dbg_wo, wo_all[:, :], waits=ev_wo_gate)
        dma("qA", dbg_accw, acc[:, :], waits=[ev_acc2[(7, 3)]])

    outb = [out0, out1]
    ev_store = [None] * 8
    ev_add = [None] * 8
    for tc in range(8):
        w = [ev_acc2[(tc, oc)] for oc in range(4)]
        exld = dma("qA", xs[tc % 2][:, :], dx[128 * tc:128 * (tc + 1), :],
                   waits=w + ([ev_add[tc - 2]] if tc >= 2 else []))
        ev_add[tc] = op("vector", lambda e, s, tc=tc: e.tensor_add(
            outb[tc % 2][:, :], acc[:, D * tc:D * (tc + 1)], xs[tc % 2][:, :]),
            waits=[exld] + ([ev_store[tc - 2]] if tc >= 2 else []), ev=True)
        ev_store[tc] = dma("qS", dy[128 * tc:128 * (tc + 1), :],
                           outb[tc % 2][:, :], waits=[ev_add[tc]])
    final_qA = p.cnt["qA"]
    final_qS = p.cnt["qS"]
    p.emit("scalar", lambda e, s: e.nop(),
           waits=[("qA", final_qA), ("qS", final_qS)])

    with ExitStack() as ctx:
        sems = {n: ctx.enter_context(nc.semaphore(n)) for n in sem_names}
        with nc.Block() as block:
            @block.sync
            def _(eng):
                p.replay("sync", eng, sems)

            @block.gpsimd
            def _(eng):
                p.replay("gpsimd", eng, sems)

            @block.vector
            def _(eng):
                p.replay("vector", eng, sems)

            @block.scalar
            def _(eng):
                p.replay("scalar", eng, sems)

            @block.tensor
            def _(eng):
                p.replay("tensor", eng, sems)

    return nc


def _koff(b):
    r = b if b < 8 else 15 - b
    return 1024 * r + (512 if b >= 8 else 0)


def _item_table(core):
    bA, bB = core, 15 - core
    ko = [_koff(bA), _koff(bB)]
    qo = [0, 512]
    for t in range(bA):
        ko.append(_koff(t))
        qo.append(0)
    for t in range(bB):
        ko.append(_koff(t))
        qo.append(512)
    assert len(ko) == NITEMS
    vo = [128 * k for k in ko]
    return np.array([ko + vo + qo], dtype=np.int32)


def _masks():
    m = np.zeros((128, 2048), dtype=np.float32)
    qf = np.arange(512)[None, :]
    kp = np.arange(128)[:, None]
    for c in range(4):
        allow = (qf - kp) >= 128 * c
        m[:, 512 * c:512 * (c + 1)] = np.where(allow, 1.0, 0.0)
    return np.ascontiguousarray(m.astype(BF))


_NC_CACHE = {}


def kernel(L0_attn_x, L0_attn_Wq, L0_attn_Wk, L0_attn_Wv, L0_attn_Wo,
           L0_attn_lambda_q1, L0_attn_lambda_k1, L0_attn_lambda_q2,
           L0_attn_lambda_k2, L0_attn_ln_w,
           L0_ffn_wq_gate, L0_ffn_s_gate, L0_ffn_wq_up, L0_ffn_s_up,
           L0_ffn_wq_down, L0_ffn_s_down):
    x = np.ascontiguousarray(np.asarray(L0_attn_x, dtype=np.float32)
                             .reshape(8192, D))
    C = np.ascontiguousarray
    lam4 = np.concatenate([
        np.asarray(L0_attn_lambda_q1, np.float32).reshape(1, 64),
        np.asarray(L0_attn_lambda_k1, np.float32).reshape(1, 64),
        np.asarray(L0_attn_lambda_q2, np.float32).reshape(1, 64),
        np.asarray(L0_attn_lambda_k2, np.float32).reshape(1, 64)], axis=1)
    shared = {
        "wqT": C(np.asarray(L0_attn_Wq, np.float32).T.astype(BF)),
        "wkT": C(np.asarray(L0_attn_Wk, np.float32).T.astype(BF)),
        "wvT": C(np.asarray(L0_attn_Wv, np.float32).T.astype(BF)),
        "woT": C(np.asarray(L0_attn_Wo, np.float32).T.astype(BF)),
        "wg8": C(np.asarray(L0_ffn_wq_gate).reshape(NHC, 128, 16, 128)
                 .transpose(0, 3, 2, 1).reshape(NHC, 128, D).astype(np.uint8)),
        "wu8": C(np.asarray(L0_ffn_wq_up).reshape(NHC, 128, 16, 128)
                 .transpose(0, 3, 2, 1).reshape(NHC, 128, D).astype(np.uint8)),
        "wd8": C(np.asarray(L0_ffn_wq_down).T.astype(np.uint8)),
        "sgpp": C(np.pad(np.asarray(L0_ffn_s_gate, np.float32).reshape(NHC, 128).T, ((0, 0), (0, 4)))),
        "supp": C(np.pad(np.asarray(L0_ffn_s_up, np.float32).reshape(NHC, 128).T, ((0, 0), (0, 4)))),
        "sdb": C(np.broadcast_to(np.asarray(L0_ffn_s_down, np.float32)[None, :],
                                 (128, D))),
        "lnw": C(np.pad(np.asarray(L0_attn_ln_w, np.float32).reshape(128, 1), ((0, 0), (0, 7)))),
        "lam4": C(lam4),
        "ident_b": np.eye(128, dtype=BF),
        "ones_b": np.ones((128, 128), dtype=BF),
        "zrow_b": np.zeros((1, 512), dtype=BF),
        "onesrow_b": np.ones((1, 512), dtype=BF),
        "onescol_f": np.ones((1, 128), dtype=np.float32),
        "onescol128": np.ones((128, 1), dtype=np.float32),
        "cbias": np.broadcast_to(np.array(
            [EPS, -1.0, -LAMBDA_INIT, 0.0, 0.0, 0.0, 0.0, 0.0],
            dtype=np.float32)[None, :], (128, 8)).copy(),
        "masks": _masks(),
    }
    in_maps = []
    for c in range(NCORES):
        bA, bB = c, 15 - c
        m = dict(shared)
        m["x_own"] = C(np.concatenate(
            [x[512 * bA:512 * (bA + 1)], x[512 * bB:512 * (bB + 1)]], axis=0))
        m["itab"] = _item_table(c)
        in_maps.append(m)

    if "nc" not in _NC_CACHE:
        _NC_CACHE["nc"] = build_nc()
    nc = _NC_CACHE["nc"]
    res = run_bass_kernel_spmd(nc, in_maps, core_ids=list(range(NCORES)))

    out = np.empty((8192, D), dtype=np.float32)
    for c in range(NCORES):
        y = np.asarray(res.results[c]["y"], dtype=np.float32)
        out[512 * c:512 * (c + 1)] = y[0:512]
        out[512 * (15 - c):512 * (16 - c)] = y[512:1024]
    return out.reshape(1, 8192, D)


# version 23 (final)
# speedup vs baseline: 1.0213x; 1.0008x over previous
import math
from contextlib import ExitStack

import numpy as np
import ml_dtypes

import concourse.bass as bass
import concourse.mybir as mybir
from concourse.bass_utils import run_bass_kernel_spmd

F32 = mybir.dt.float32
BF16 = mybir.dt.bfloat16
I32 = mybir.dt.int32
U8 = mybir.dt.uint8
AF = mybir.ActivationFunctionType
ALU = mybir.AluOpType
AX = mybir.AxisListType

NCORES = 8
D = 2048
TOK = 1024
HID = 5632
NHC = 44
NQ = 4
HCQ = NHC // NQ
NH = 32
NDH = 16
NITEMS = 17
LAMBDA_INIT = 0.8 - 0.6 * math.exp(-0.3 * 12)
EPS = 1e-5
NEGBIG = -1e9
BF = ml_dtypes.bfloat16
DEBUG = False


class Prog:
    ENGINES = ("sync", "gpsimd", "vector", "scalar", "tensor")

    def __init__(self):
        self.ops = {e: [] for e in self.ENGINES}
        self.cnt = {}

    QUEUES = ("qS", "qG", "qA")

    def emit(self, engine, fn, waits=(), inc=None, inc_by=1, fifo=True):
        ev = None
        w = [x for x in waits if x]
        if inc is not None:
            if fifo and inc in self.QUEUES and self.cnt.get(inc, 0) > 0:
                w.append((inc, self.cnt[inc]))
            self.cnt[inc] = self.cnt.get(inc, 0) + inc_by
            ev = (inc, self.cnt[inc])
        self.ops[engine].append((fn, tuple(w), inc, inc_by))
        return ev

    def bump(self, sem):
        self.cnt[sem] = self.cnt.get(sem, 0) + 1
        return (sem, self.cnt[sem])

    def replay(self, engine, eng, sems):
        for fn, waits, inc, inc_by in self.ops[engine]:
            for (s, c) in waits:
                eng.wait_ge(sems[s], c)
            ins = fn(eng, sems)
            if inc is not None:
                ins.then_inc(sems[inc], inc_by)


def build_nc():
    nc = bass.Bass("TRN2", target_bir_lowering=False, debug=False)

    dx = nc.dram_tensor("x_own", [TOK, D], F32, kind="ExternalInput").ap()
    dwqT = nc.dram_tensor("wqT", [D, D], BF16, kind="ExternalInput").ap()
    dwkT = nc.dram_tensor("wkT", [D, 512], BF16, kind="ExternalInput").ap()
    dwvT = nc.dram_tensor("wvT", [D, 512], BF16, kind="ExternalInput").ap()
    dwoT = nc.dram_tensor("woT", [D, D], BF16, kind="ExternalInput").ap()
    dwg8 = nc.dram_tensor("wg8", [NHC, 128, D], U8, kind="ExternalInput").ap()
    dwu8 = nc.dram_tensor("wu8", [NHC, 128, D], U8, kind="ExternalInput").ap()
    dwd8 = nc.dram_tensor("wd8", [HID, D], U8, kind="ExternalInput").ap()
    dsg = nc.dram_tensor("sgpp", [128, 48], F32, kind="ExternalInput").ap()
    dsu = nc.dram_tensor("supp", [128, 48], F32, kind="ExternalInput").ap()
    dsd = nc.dram_tensor("sdb", [128, D], F32, kind="ExternalInput").ap()
    dlnw = nc.dram_tensor("lnw", [128, 8], F32, kind="ExternalInput").ap()
    dlam = nc.dram_tensor("lam4", [1, 256], F32, kind="ExternalInput").ap()
    dident_b = nc.dram_tensor("ident_b", [128, 128], BF16, kind="ExternalInput").ap()
    dones_b = nc.dram_tensor("ones_b", [128, 128], BF16, kind="ExternalInput").ap()
    dzrow_b = nc.dram_tensor("zrow_b", [1, 512], BF16, kind="ExternalInput").ap()
    donesrow_b = nc.dram_tensor("onesrow_b", [1, 512], BF16, kind="ExternalInput").ap()
    donescol_f = nc.dram_tensor("onescol_f", [1, 128], F32, kind="ExternalInput").ap()
    donescol128 = nc.dram_tensor("onescol128", [128, 1], F32, kind="ExternalInput").ap()
    dmasks = nc.dram_tensor("masks", [128, 2048], BF16, kind="ExternalInput").ap()
    dcbias = nc.dram_tensor("cbias", [128, 8], F32, kind="ExternalInput").ap()
    ditab = nc.dram_tensor("itab", [1, 3 * NITEMS], I32, kind="ExternalInput").ap()
    dy = nc.dram_tensor("y", [TOK, D], F32, kind="ExternalOutput").ap()
    if DEBUG:
        dbg_xn = nc.dram_tensor("dbg_xn", [128, 16 * TOK], BF16, kind="ExternalOutput").ap()
        dbg_qfm = nc.dram_tensor("dbg_qfm", [128, 16 * TOK], BF16, kind="ExternalOutput").ap()
        dbg_kall = nc.dram_tensor("dbg_kall", [NCORES * 512, TOK], BF16, kind="ExternalOutput").ap()
        dbg_vall = nc.dram_tensor("dbg_vall", [NCORES * 4 * TOK, 128], BF16, kind="ExternalOutput").ap()
        dbg_acc = nc.dram_tensor("dbg_acc", [128, 8 * D], BF16, kind="ExternalOutput").ap()
        dbg_negm = nc.dram_tensor("dbg_negm", [128, 18], F32, kind="ExternalOutput").ap()
        dbg_ofm = nc.dram_tensor("dbg_ofm", [128, NDH * TOK], BF16, kind="ExternalOutput").ap()
        dbg_l = nc.dram_tensor("dbg_l", [2, 1024], F32, kind="ExternalOutput").ap()
        dbg_h0 = nc.dram_tensor("dbg_h0", [128, HCQ * TOK], BF16, kind="ExternalOutput").ap()
        dbg_acc0 = nc.dram_tensor("dbg_acc0", [128, 8 * D], BF16, kind="ExternalOutput").ap()
        dbg_kv = nc.dram_tensor("dbg_kv", [128, 2048], BF16, kind="ExternalOutput").ap()
        dbg_kv16 = nc.dram_tensor("dbg_kv16", [128, 2048], BF16, kind="ExternalOutput").ap()
        dbg_wo = nc.dram_tensor("dbg_wo", [128, 16 * D], BF16, kind="ExternalOutput").ap()
        dbg_accpre = nc.dram_tensor("dbg_accpre", [128, 8 * D], BF16, kind="ExternalOutput").ap()
        dbg_accw = nc.dram_tensor("dbg_accw", [128, 8 * D], BF16, kind="ExternalOutput").ap()
        dbg_o0 = nc.dram_tensor("dbg_o0", [128, 2048], BF16, kind="ExternalOutput").ap()
        dbg_l0 = nc.dram_tensor("dbg_l0", [2, 1024], F32, kind="ExternalOutput").ap()
        dbg_gate = nc.dram_tensor("dbg_gate", [128, 512], F32, kind="ExternalOutput").ap()
        dbg_up = nc.dram_tensor("dbg_up", [128, 512], F32, kind="ExternalOutput").ap()
        dbg_sil = nc.dram_tensor("dbg_sil", [128, 512], BF16, kind="ExternalOutput").ap()
        dbg_wgd = nc.dram_tensor("dbg_wgd", [128, D], BF16, kind="ExternalOutput").ap()

    kst_d = nc.dram_tensor("k_stage", [512, TOK], BF16, kind="Internal")
    vst_d = nc.dram_tensor("v_stage", [4 * TOK, 128], BF16, kind="Internal")
    knst_d = nc.dram_tensor("kn_stage", [1, 8], F32, kind="Internal")
    kall_d = nc.dram_tensor("k_all", [NCORES * 512, TOK], BF16, kind="Internal")
    vall_d = nc.dram_tensor("v_all", [NCORES * 4 * TOK, 128], BF16, kind="Internal")
    knall_d = nc.dram_tensor("kn_all", [1, 8], F32, kind="Internal")
    vcur_d = nc.dram_tensor("v_cur", [8 * TOK + 16, 128], BF16, kind="Internal")

    off = [16384]

    def sb(name, shape, dtype, at=None):
        nbytes = shape[1] * {F32: 4, BF16: 2, I32: 4, U8: 1}[dtype]
        if at is None:
            at = off[0]
            off[0] = at + ((nbytes + 31) // 32) * 32
        return nc.alloc_sbuf_tensor_at(name, shape, dtype, offset=at)

    ident_b = sb("ident_b", [128, 128], BF16)
    ones_b = sb("ones_b", [128, 128], BF16)
    zrow_b = sb("zrow_b", [1, 512], BF16)
    onesrow_b = sb("onesrow_b", [1, 512], BF16)
    onescol_f = sb("onescol_f", [1, 128], F32)
    onescol128 = sb("onescol128", [128, 1], F32)
    cbias = sb("cbias", [128, 8], F32)
    sdb = sb("sdb", [128, D], F32)
    sgpp = sb("sgpp", [128, 48], F32)
    supp = sb("supp", [128, 48], F32)
    lnw = sb("lnw", [128, 8], F32)
    lnw_s = sb("lnw_s", [128, 8], F32)
    lam4 = sb("lam4", [1, 256], F32)
    lscr = sb("lscr", [1, 128], F32)
    stats = sb("stats", [128, 16], F32)
    nmq = sb("nmq", [1, 32], F32)
    nmk = sb("nmk", [1, 8], F32)
    mrow = sb("mrow", [1, 32], F32)
    bcrow = sb("bcrow", [1, 18], F32)
    negmpp = sb("negmpp", [128, 24], F32)
    itab = sb("itab", [1, 3 * NITEMS], I32)
    rpp = sb("rpp", [128, 16], F32)
    rbn = sb("rbn", [128, 8], F32)
    ssq8 = sb("ssq8", [128, 8], F32)
    rst8 = sb("rst8", [128, 8], F32)
    acc = sb("acc", [128, 8 * D], BF16)
    C0 = off[0]
    ofm = sb("ofm", [128, NDH * TOK], BF16)
    wdr2 = sb("wdr2", [128, 512], BF16, at=C0 + 24576)
    wdr3 = sb("wdr3", [128, 512], BF16, at=C0 + 25600)
    wdd2 = sb("wdd2", [128, 512], BF16, at=C0 + 26624)
    wdd3 = sb("wdd3", [128, 512], BF16, at=C0 + 27648)
    wdr4 = sb("wdr4", [128, 512], BF16, at=C0 + 28672)
    wdr5 = sb("wdr5", [128, 512], BF16, at=C0 + 29696)
    wdd4 = sb("wdd4", [128, 512], BF16, at=C0 + 30720)
    wdd5 = sb("wdd5", [128, 512], BF16, at=C0 + 31744)
    D0 = off[0]
    xnfm = sb("xnfm", [128, 16 * TOK], BF16)
    ksb = sb("ksb", [64, 8704], BF16, at=D0)
    vsb = sb("vsb", [128, 8192], BF16, at=D0 + 16384)
    E0 = off[0]
    qfm = sb("qfm", [128, 16 * TOK], BF16)
    wo_all = sb("wo_all", [128, 16 * D], BF16, at=D0)
    assert E0 - D0 == 32 * 1024
    F0 = off[0]
    hfm = sb("hfm", [128, HCQ * TOK], BF16)
    pt = sb("pt", [128, 4096], BF16, at=F0)
    o1s = sb("o1s", [128, 1024], BF16, at=F0 + 8192)
    o2s = sb("o2s", [128, 1024], BF16, at=F0 + 10240)
    kstat = sb("kstat", [128, 1024], BF16, at=F0 + 12288)
    vstat = sb("vstat", [128, 1024], BF16, at=F0 + 14336)
    onb = sb("onb", [128, 1024], BF16, at=F0 + 16384)
    vstat2 = sb("vstat2", [128, 1024], BF16, at=F0 + 18432)
    kfm = sb("kfm", [128, 4096], BF16, at=F0)
    assert 22 * 1024 >= 20480
    G0 = off[0]
    xs0 = sb("xs0", [128, D], F32, at=G0)
    xs1 = sb("xs1", [128, D], F32, at=G0 + 8192)
    la_sb = sb("la_sb", [1, 1024], F32, at=G0)
    lb_sb = sb("lb_sb", [1, 1024], F32, at=G0 + 4096)
    ra_sb = sb("ra_sb", [1, 1024], F32, at=G0 + 8192)
    rb_sb = sb("rb_sb", [1, 1024], F32, at=G0 + 12288)
    wvsb = sb("wvsb", [128, 8192], BF16, at=G0 + 16384)
    out0 = sb("out0", [128, D], F32, at=G0 + 16384)
    out1 = sb("out1", [128, D], F32, at=G0 + 24576)
    masks = sb("masks", [128, 2048], BF16, at=G0 + 16384)
    odb = sb("odb", [128, 1024], F32, at=G0 + 24576)
    scr1 = sb("scr1", [128, 1024], F32, at=G0 + 28672)
    wqs0 = sb("wqs0", [128, D], BF16, at=G0 + 32768)
    wqs1 = sb("wqs1", [128, D], BF16, at=G0 + 36864)
    wgr0 = sb("wgr0", [128, D], BF16, at=G0)
    wgr1 = sb("wgr1", [128, D], BF16, at=G0 + 4096)
    wur0 = sb("wur0", [128, D], BF16, at=G0 + 8192)
    wur1 = sb("wur1", [128, D], BF16, at=G0 + 12288)
    wgd0 = sb("wgd0", [128, D], BF16, at=G0 + 16384)
    wgd1 = sb("wgd1", [128, D], BF16, at=G0 + 20480)
    wud0 = sb("wud0", [128, D], BF16, at=G0 + 24576)
    wud1 = sb("wud1", [128, D], BF16, at=G0 + 28672)
    wdr0 = sb("wdr0", [128, 512], BF16, at=G0 + 32768)
    wdr1 = sb("wdr1", [128, 512], BF16, at=G0 + 33792)
    wdd0 = sb("wdd0", [128, 512], BF16, at=G0 + 34816)
    wdd1 = sb("wdd1", [128, 512], BF16, at=G0 + 35840)
    ds_scr = sb("ds_scr", [128, 512], F32, at=G0 + 36864)
    sil_scr = sb("sil_scr", [128, 1024], BF16, at=G0 + 38912)
    G_end = G0 + 40960
    xc = sb("xc", [128, D], F32, at=F0)
    sqs = sb("sqs", [128, 2048], F32, at=F0 + 8192)
    sqsb = sb("sqsb", [128, 2048], BF16, at=F0 + 8192)
    total = max(G_end, off[0])
    assert total <= 229312, f"SBUF over budget: {total}"

    xcb = xc.bitcast(BF16)
    vtmb = sqs.bitcast(BF16)

    PST = nc.alloc_psum_tensor("PST", [128, 2048], F32)
    PCt = nc.alloc_psum_tensor("PCt", [128, 1024], F32)
    PDt = nc.alloc_psum_tensor("PDt", [128, 1024], F32)
    PSTb = PST.bitcast(BF16)
    PCb = PCt.bitcast(BF16)
    PDb = PDt.bitcast(BF16)

    class PView:
        def __init__(self, t, off, width):
            self.t, self.off, self.width = t, off, width

        def __getitem__(self, idx):
            p, f = idx
            a = 0 if f.start is None else f.start
            b = self.width if f.stop is None else f.stop
            return self.t[p, a + self.off:b + self.off]

    PA = PView(PST, 0, 1024)
    PB = PView(PST, 1024, 1024)
    PC = PView(PCt, 0, 1024)
    PD = PView(PDt, 0, 1024)
    PAb = PView(PSTb, 0, 2048)
    PBb = PView(PSTb, 2048, 2048)

    sem_names = ["qS", "qG", "qA", "sPE", "sDVE", "sACT", "cc"]
    p = Prog()

    def dma(q, out, in_, waits=(), fifo=True):
        eng = {"qS": "sync", "qG": "gpsimd", "qA": "scalar"}[q]
        return p.emit(eng, lambda e, s, o=out, i=in_: e.dma_start(out=o, in_=i),
                      waits=waits, inc=q, inc_by=16, fifo=fifo)

    def op(engine, fn, waits=(), ev=False):
        sem = {"vector": "sDVE", "scalar": "sACT", "tensor": "sPE"}[engine]
        return p.emit(engine, fn, waits=waits, inc=(sem if ev else None))

    ev_ident = dma("qA", ident_b[:, :], dident_b)
    ev_cbias = dma("qA", cbias[:, :], dcbias)
    for (dst, src) in [(ones_b, dones_b), (zrow_b, dzrow_b),
                       (onesrow_b, donesrow_b), (onescol_f, donescol_f),
                       (onescol128, donescol128),
                       (sdb, dsd), (sgpp, dsg), (supp, dsu),
                       (lnw, dlnw), (itab, ditab)]:
        dma("qA", dst[:, :], src)
    ev_consts = dma("qA", lam4[0:1, :], dlam)

    xs = [xs0, xs1]
    ev_xn_act = [None] * 8
    ev_trans = [None] * 8
    ev_xn_evac = [None] * 8
    for tc in range(8):
        sc = 8 * (tc % 2)
        t = xs[tc % 2]
        ev_xld = dma("qS", t[:, :], dx[128 * tc:128 * (tc + 1), :],
                     waits=([ev_xn_act[tc - 2]] if tc >= 2 else []))
        e1 = op("vector", lambda e, s, t=t, sc=sc: e.reduce_sum(
            stats[:, sc:sc + 1], t[:, :], AX.X, negate=True),
            waits=[ev_xld] + ([ev_xn_act[tc - 2]] if tc >= 2 else []), ev=True)
        e2 = op("vector", lambda e, s, t=t, sc=sc: e.scalar_tensor_tensor(
            sqs[:, 0:D], t[:, :], 1.0, t[:, :], ALU.bypass, ALU.mult,
            accum_out=stats[:, sc + 2:sc + 3]), ev=True)
        e3 = op("scalar", lambda e, s, sc=sc: e.activation(
            stats[:, sc + 1:sc + 2], stats[:, sc:sc + 1], AF.Copy,
            scale=1.0 / D), waits=[e1], ev=True)
        e4 = op("scalar", lambda e, s, sc=sc: e.activation(
            stats[:, sc + 3:sc + 4], stats[:, sc + 2:sc + 3], AF.Copy,
            scale=1.0 / D), waits=[e2], ev=True)
        e5 = op("vector", lambda e, s, sc=sc: e.scalar_tensor_tensor(
            stats[:, sc + 4:sc + 5], stats[:, sc + 1:sc + 2], 1.0,
            stats[:, sc + 1:sc + 2], ALU.bypass, ALU.mult),
            waits=[e3], ev=True)
        e6 = op("vector", lambda e, s, sc=sc: e.tensor_sub(
            stats[:, sc + 5:sc + 6], stats[:, sc + 3:sc + 4],
            stats[:, sc + 4:sc + 5]), waits=[e4, e5], ev=True)
        e7 = op("scalar", lambda e, s, sc=sc: e.activation(
            stats[:, sc + 6:sc + 7], stats[:, sc + 5:sc + 6], AF.Sqrt,
            bias=cbias[:, 0:1]),
            waits=[e6] + ([ev_cbias] if tc == 0 else []), ev=True)
        e8 = op("vector", lambda e, s, sc=sc: e.reciprocal(
            stats[:, sc + 7:sc + 8], stats[:, sc + 6:sc + 7]),
            waits=[e7], ev=True)
        e9 = op("vector", lambda e, s, sc=sc: e.scalar_tensor_tensor(
            stats[:, sc + 4:sc + 5], stats[:, sc + 1:sc + 2], 1.0,
            stats[:, sc + 7:sc + 8], ALU.bypass, ALU.mult),
            waits=[e8], ev=True)
        ev_xn_act[tc] = op("vector", lambda e, s, t=t, sc=sc: e.tensor_scalar(
            xcb[:, 0:D], t[:, :], stats[:, sc + 7:sc + 8],
            stats[:, sc + 4:sc + 5], ALU.mult, ALU.add),
            waits=[e9] + ([ev_trans[tc - 1]] if tc >= 1 else []), ev=True)
        for i in range(16):
            ev_trans[tc] = op("tensor", lambda e, s, i=i: e.transpose(
                PAb[:, 128 * i:128 * (i + 1)], xcb[:, 128 * i:128 * (i + 1)],
                ident_b[:, :]),
                waits=([ev_ident, ev_xn_act[tc]] if i == 0 else []) +
                      ([ev_xn_evac[tc - 1]] if (i == 0 and tc >= 1) else []),
                ev=(i == 15))
        ev_xn_evac[tc] = op("vector", lambda e, s, tc=tc: e.tensor_copy(
            xnfm[:, :].rearrange("p (i t) -> p i t", i=16)[:, :, 128 * tc:128 * (tc + 1)],
            PAb[:, :].rearrange("p (i t) -> p i t", i=16)),
            waits=[ev_trans[tc]], ev=True)
    last_xn_evac = ev_xn_evac[7]

    if DEBUG:
        dma("qA", dbg_xn, xnfm[:, :], waits=[last_xn_evac])

    wqs = [wqs0, wqs1]
    ev_qev = [None] * 16
    for oc in range(16):
        ev_wq = dma("qG", wqs[oc % 2][:, :].rearrange("p (i o) -> p i o", i=16),
                    dwqT[:, 128 * oc:128 * (oc + 1)]
                    .rearrange("(i p) o -> p i o", p=128),
                    waits=[ev_qev[oc - 2]] if oc >= 2 else [])
        for th in range(2):
            mm_ev = None
            for ic in range(16):
                mm_ev = op("tensor", lambda e, s, oc=oc, th=th, ic=ic: e.matmul(
                    PB[:, 512 * th:512 * (th + 1)],
                    wqs[oc % 2][:, 128 * ic:128 * (ic + 1)],
                    xnfm[:, 1024 * ic + 512 * th:1024 * ic + 512 * (th + 1)],
                    start=(ic == 0), stop=(ic == 15), skip_group_check=True),
                    waits=([ev_wq, last_xn_evac] if (th == 0 and ic == 0) else []) +
                          ([ev_qev[oc - 1]] if (oc >= 1 and th == 0 and ic == 0) else []),
                    ev=(ic == 15))
            ev_qev[oc] = op("scalar", lambda e, s, oc=oc, th=th: e.activation(
                qfm[:, 1024 * oc + 512 * th:1024 * oc + 512 * (th + 1)],
                PB[:, 512 * th:512 * (th + 1)], AF.Copy, scale=0.125),
                waits=[mm_ev], ev=True)
    ev_kev = [None] * 4
    ev_kst = [None] * 4
    for oc in range(4):
        ev_wk = dma("qG", wqs[oc % 2][:, :].rearrange("p (i o) -> p i o", i=16),
                    dwkT[:, 128 * oc:128 * (oc + 1)]
                    .rearrange("(i p) o -> p i o", p=128),
                    waits=[ev_qev[14 + oc]] if oc < 2 else [ev_kev[oc - 2]])
        for th in range(2):
            mm_ev = None
            for ic in range(16):
                mm_ev = op("tensor", lambda e, s, oc=oc, th=th, ic=ic: e.matmul(
                    PB[:, 512 * th:512 * (th + 1)],
                    wqs[oc % 2][:, 128 * ic:128 * (ic + 1)],
                    xnfm[:, 1024 * ic + 512 * th:1024 * ic + 512 * (th + 1)],
                    start=(ic == 0), stop=(ic == 15), skip_group_check=True),
                    waits=([ev_wk] if (th == 0 and ic == 0) else []) +
                          ([ev_qev[15] if oc == 0 else ev_kev[oc - 1]]
                           if (th == 0 and ic == 0) else []),
                    ev=(ic == 15))
            ev_kev[oc] = op("scalar", lambda e, s, oc=oc, th=th: e.activation(
                kfm[:, 1024 * oc + 512 * th:1024 * oc + 512 * (th + 1)],
                PB[:, 512 * th:512 * (th + 1)], AF.Copy), waits=[mm_ev], ev=True)
        ev_kst[oc] = dma("qA", kst_d[128 * oc:128 * (oc + 1), :],
                         kfm[:, 1024 * oc:1024 * (oc + 1)], waits=[ev_kev[oc]])
    ev_wv = dma("qG", wvsb[:, :].rearrange("p (i v) -> p i v", i=16),
                dwvT.rearrange("(i p) v -> p i v", p=128),
                waits=[ev_qev[15]])
    ev_vst = {}
    ev_vev = [None] * 8
    for tc in range(8):
        mm_ev = None
        for ic in range(16):
            mm_ev = op("tensor", lambda e, s, tc=tc, ic=ic: e.matmul(
                PC[:, 0:512],
                xnfm[:, 1024 * ic + 128 * tc:1024 * ic + 128 * (tc + 1)],
                wvsb[:, 512 * ic:512 * (ic + 1)],
                start=(ic == 0), stop=(ic == 15), skip_group_check=True),
                waits=([ev_wv] if (tc == 0 and ic == 0) else []) +
                      ([ev_vev[tc - 1]] if (tc >= 1 and ic == 0) else []),
                ev=(ic == 15))
        ev_vev[tc] = op("vector", lambda e, s, tc=tc: e.tensor_copy(
            vtmb[:, 512 * (tc % 2):512 * (tc % 2 + 1)], PC[:, 0:512]),
            waits=[mm_ev] + ([ev_vst[(tc - 2, 3)]] if tc >= 2 else []), ev=True)
        for vh in range(4):
            ev_vst[(tc, vh)] = dma(
                "qA", vst_d[1024 * vh + 128 * tc:1024 * vh + 128 * (tc + 1), :],
                vtmb[:, 512 * (tc % 2) + 128 * vh:512 * (tc % 2) + 128 * (vh + 1)],
                waits=[ev_vev[tc]] if vh == 0 else [])
    ev_vst_last = ev_vst[(7, 3)]


    ev_knorm = None
    for oc in range(4):
        esq = op("scalar", lambda e, s, oc=oc: e.activation(
            sqsb[:, 0:1024], kfm[:, 1024 * oc:1024 * (oc + 1)], AF.Square),
            waits=([ev_kev[oc], ev_vst_last] if oc == 0 else [ev_kev[oc]]),
            ev=True)
        for half in range(2):
            kh = 2 * oc + half
            op("tensor", lambda e, s, half=half: e.matmul(
                PD[0:1, 0:512], ones_b[64 * half:64 * half + 64, 0:1],
                sqsb[64 * half:64 * half + 64, 0:512],
                start=True, stop=True, skip_group_check=True),
                waits=[esq] + ([ev_knorm] if ev_knorm else []))
            m2 = op("tensor", lambda e, s, half=half: e.matmul(
                PD[0:1, 512:1024], ones_b[64 * half:64 * half + 64, 0:1],
                sqsb[64 * half:64 * half + 64, 512:1024],
                start=True, stop=True, skip_group_check=True), ev=True)
            ev_knorm = op("vector", lambda e, s, kh=kh: e.reduce_max(
                nmk[0:1, kh:kh + 1], PD[0:1, 0:1024], AX.X),
                waits=[m2], ev=True)
    ev_knst = dma("qA", knst_d.ap(), nmk[0:1, :], waits=[ev_knorm])
    ev_qnorm = None
    for oc in range(16):
        esq = op("scalar", lambda e, s, oc=oc: e.activation(
            sqsb[:, 0:1024], qfm[:, 1024 * oc:1024 * (oc + 1)], AF.Square),
            waits=([ev_knorm] if oc == 0 else []), ev=True)
        for half in range(2):
            h = 2 * oc + half
            op("tensor", lambda e, s, half=half: e.matmul(
                PD[0:1, 0:512], ones_b[64 * half:64 * half + 64, 0:1],
                sqsb[64 * half:64 * half + 64, 0:512],
                start=True, stop=True, skip_group_check=True),
                waits=[esq, ev_qnorm if ev_qnorm else ev_knorm])
            m2 = op("tensor", lambda e, s, half=half: e.matmul(
                PD[0:1, 512:1024], ones_b[64 * half:64 * half + 64, 0:1],
                sqsb[64 * half:64 * half + 64, 512:1024],
                start=True, stop=True, skip_group_check=True), ev=True)
            ev_qnorm = op("vector", lambda e, s, h=h: e.reduce_max(
                nmq[0:1, h:h + 1], PD[0:1, 0:1024], AX.X),
                waits=[m2], ev=True)

    rg = [list(range(NCORES))]
    p.emit("gpsimd", lambda e, s: e.collective_compute(
        "AllGather", ALU.bypass, replica_groups=rg,
        ins=[kst_d.ap().opt()], outs=[kall_d.ap().opt()]),
        waits=[ev_kst[3], ev_vst_last], inc="cc")
    p.emit("gpsimd", lambda e, s: e.collective_compute(
        "AllGather", ALU.bypass, replica_groups=rg,
        ins=[vst_d.ap().opt()], outs=[vall_d.ap().opt()]),
        waits=[ev_vst_last], inc="cc")
    p.emit("gpsimd", lambda e, s: e.collective_compute(
        "AllReduce", ALU.max, replica_groups=rg,
        ins=[knst_d.ap().opt()], outs=[knall_d.ap().opt()]),
        waits=[ev_knst], inc="cc")

    if DEBUG:
        dma("qA", dbg_qfm, qfm[:, :], waits=[ev_qev[15]])
        dma("qA", dbg_kall, kall_d.ap(), waits=[("cc", 3)])
        dma("qA", dbg_vall, vall_d.ap(), waits=[("cc", 3)])

    def emit_mlambda():
        ev_knall = dma("qA", nmk[0:1, :], knall_d.ap(), waits=[("cc", 3)])
        e_sq1 = op("scalar", lambda e, s: e.activation(
            nmk[0:1, :], nmk[0:1, :], AF.Sqrt), waits=[ev_knall], ev=True)
        e_sq2 = op("scalar", lambda e, s: e.activation(
            nmq[0:1, :], nmq[0:1, :], AF.Sqrt), waits=[ev_qnorm], ev=True)
        e_m = None
        for h in range(32):
            e_m = op("vector", lambda e, s, h=h: e.tensor_mul(
                mrow[0:1, h:h + 1], nmq[0:1, h:h + 1],
                nmk[0:1, h // 4:h // 4 + 1]),
                waits=([e_sq1, e_sq2] if h == 0 else []), ev=(h == 31))
        e_mp = op("vector", lambda e, s: e.tensor_tensor(
            bcrow[0:1, 0:16],
            mrow[0:1, :].rearrange("a (h two) -> a h two", two=2)[:, :, 0],
            mrow[0:1, :].rearrange("a (h two) -> a h two", two=2)[:, :, 1],
            ALU.max), waits=[e_m], ev=True)
        e_neg = op("vector", lambda e, s: e.tensor_scalar(
            bcrow[0:1, 0:16], bcrow[0:1, 0:16], -1.0, -1.0,
            ALU.mult, ALU.add), waits=[e_mp], ev=True)
        e_l1 = op("vector", lambda e, s: e.tensor_mul(
            lscr[0:1, 0:64], lam4[0:1, 0:64], lam4[0:1, 64:128]),
            waits=[ev_consts], ev=True)
        e_s1 = op("vector", lambda e, s: e.reduce_sum(
            lscr[0:1, 64:65], lscr[0:1, 0:64], AX.X), waits=[e_l1], ev=True)
        e_e1 = op("scalar", lambda e, s: e.activation(
            lscr[0:1, 66:67], lscr[0:1, 64:65], AF.Exp), waits=[e_s1], ev=True)
        e_l2 = op("vector", lambda e, s: e.tensor_mul(
            lscr[0:1, 0:64], lam4[0:1, 128:192], lam4[0:1, 192:256]),
            waits=[e_e1], ev=True)
        e_s2 = op("vector", lambda e, s: e.reduce_sum(
            lscr[0:1, 65:66], lscr[0:1, 0:64], AX.X), waits=[e_l2], ev=True)
        e_e2 = op("scalar", lambda e, s: e.activation(
            lscr[0:1, 68:69], lscr[0:1, 65:66], AF.Exp), waits=[e_s2], ev=True)
        e_lf = op("vector", lambda e, s: e.tensor_sub(
            lscr[0:1, 70:71], lscr[0:1, 66:67], lscr[0:1, 68:69]),
            waits=[e_e2], ev=True)
        e_lf2 = op("vector", lambda e, s: e.tensor_scalar(
            bcrow[0:1, 16:17], lscr[0:1, 70:71], -1.0, -LAMBDA_INIT,
            ALU.mult, ALU.add), waits=[e_lf], ev=True)
        return e_neg, e_lf2

    def emit_down_prefetch(q):
        evs = {}
        for j in range(6):
            jp = (6, 7, 8, 9, 10, 5)[j]
            wd_w = [] if q == 0 else [ev_ddq[(q - 1, 3, jp)]]
            evs[j] = dma("qG", wdr[j][:, :],
                         dwd8[128 * (HCQ * q + j):128 * (HCQ * q + j + 1),
                              0:512], waits=wd_w)
        return evs

    down_pre = {}
    gw_pre = {}
    mlam_evs = None
    wgr = [wgr0, wgr1]
    wur = [wur0, wur1]
    wgd = [wgd0, wgd1]
    wud = [wud0, wud1]
    wdr = [wdr0, wdr1, wdr2, wdr3, wdr4, wdr5]
    wdd = [wdd0, wdd1, wdd2, wdd3, wdd4, wdd5]
    ev_mmj_prev = None
    ev_h = [None] * NHC
    ev_gdq = [None] * NHC
    ev_udq = [None] * NHC
    ev_ddq = {}
    ev_acc = {}
    last_xn_read = None
    ev_es_first = True
    for q in range(NQ):
        if q == 0:
            down_pre[0] = emit_down_prefetch(0)
        if q == 1:
            mlam_evs = emit_mlambda()
        for j in range(HCQ):
            H = HCQ * q + j
            pp = H % 2
            if (H) in gw_pre:
                ev_wg, ev_wu = gw_pre[H]
            else:
                wg_w = []
                if H == 0:
                    wg_w = [last_xn_evac, ev_vst_last]
                elif H >= 2:
                    wg_w = [ev_udq[H - 2]]
                ev_wg = dma("qG", wgr[pp][:, :], dwg8[H, :, :], waits=wg_w)
                ev_wu = dma("qG", wur[pp][:, :], dwu8[H, :, :])
            ev_gdq[H] = op("vector", lambda e, s, pp=pp: e.tensor_scalar(
                wgd[pp][:, :], wgr[pp][:, :], -128.0, None, ALU.add),
                waits=[ev_wg] + ([ev_h[H - 2]] if H >= 2 else []), ev=True)
            ev_udq[H] = op("vector", lambda e, s, pp=pp: e.tensor_scalar(
                wud[pp][:, :], wur[pp][:, :], -128.0, None, ALU.add),
                waits=[ev_wu], ev=True)
            for th in range(2):
                mw = []
                if th == 0:
                    mw = [ev_gdq[H], ev_udq[H]]
                    if j == 0:
                        mw.append(ev_qnorm if q == 0 else ev_acc[(q - 1, 3, 7)])
                        if DEBUG and q == 1:
                            mw.append(dbg_ffn_ev[0])
                    if H >= 1:
                        mw.append(ev_h[H - 1])
                mmg = mmu = None
                for ic in range(16):
                    mmg = op("tensor", lambda e, s, pp=pp, th=th, ic=ic: e.matmul(
                        PA[:, 512 * th:512 * (th + 1)],
                        wgd[pp][:, 128 * ic:128 * (ic + 1)],
                        xnfm[:, 1024 * ic + 512 * th:1024 * ic + 512 * (th + 1)],
                        start=(ic == 0), stop=(ic == 15), skip_group_check=True),
                        waits=(mw if ic == 0 else []), ev=(ic == 15))
                for ic in range(16):
                    mmu = op("tensor", lambda e, s, pp=pp, th=th, ic=ic: e.matmul(
                        PB[:, 512 * th:512 * (th + 1)],
                        wud[pp][:, 128 * ic:128 * (ic + 1)],
                        xnfm[:, 1024 * ic + 512 * th:1024 * ic + 512 * (th + 1)],
                        start=(ic == 0), stop=(ic == 15), skip_group_check=True),
                        ev=(ic == 15))
                if H == NHC - 1 and th == 1:
                    last_xn_read = mmu
                es_w = [mmg]
                if DEBUG and H == 0 and th == 0:
                    edw = op("vector", lambda e, s: e.tensor_copy(
                        ds_scr[:, :], PA[:, 0:512]), waits=[mmg], ev=True)
                    dma("qA", dbg_wgd, wgd0[:, :], waits=[ev_gdq[0]])
                    dbg_g_ev = dma("qA", dbg_gate, ds_scr[:, :], waits=[edw])
                    es_w.append(dbg_g_ev)
                    edw2 = op("vector", lambda e, s: e.tensor_copy(
                        ds_scr[:, :], PB[:, 0:512]), waits=[mmu, dbg_g_ev], ev=True)
                    dbg_u_ev = dma("qA", dbg_up, ds_scr[:, :], waits=[edw2])
                    es_w.append(dbg_u_ev)
                if ev_es_first:
                    es_w.append(ev_qev[15])
                    ev_es_first = False
                es = op("scalar", lambda e, s, pp=pp, th=th, H=H: e.activation(
                    sil_scr[:, 512 * th:512 * (th + 1)],
                    PA[:, 512 * th:512 * (th + 1)], AF.Silu,
                    scale=sgpp[:, H:H + 1], bias=cbias[:, 3:4]),
                    waits=es_w, ev=True)
                hw = [mmu, es]
                if DEBUG and H == 0 and th == 0:
                    hw.append(dma("qA", dbg_sil, sil_scr[:, 0:512], waits=[es]))
                ev_h[H] = op("vector",
                             lambda e, s, th=th, H=H, j=j: e.scalar_tensor_tensor(
                    hfm[:, 1024 * j + 512 * th:1024 * j + 512 * (th + 1)],
                    PB[:, 512 * th:512 * (th + 1)], supp[:, H:H + 1],
                    sil_scr[:, 512 * th:512 * (th + 1)],
                    ALU.mult, ALU.mult), waits=hw, ev=True)
        if q + 1 < NQ:
            for Hn in (HCQ * (q + 1), HCQ * (q + 1) + 1):
                gw_pre[Hn] = (dma("qG", wgr[Hn % 2][:, :], dwg8[Hn, :, :],
                                  waits=[ev_udq[Hn - 2]]),
                              dma("qG", wur[Hn % 2][:, :], dwu8[Hn, :, :]))
        ps_slices = [(PA, 0), (PA, 512), (PB, 0), (PB, 512),
                     (PC, 0), (PC, 512), (PD, 0), (PD, 512)]
        for dd in range(4):
            mm_last = [None] * 8
            ev_mmj = {}
            for j in range(HCQ):
                H = HCQ * q + j
                pp = j % 6
                jp = (6, 7, 8, 9, 10, 5)[pp]
                if dd == 0 and j < 6:
                    evd = down_pre[q][j]
                else:
                    if j >= 6:
                        wd_w = [ev_ddq[(q, dd, j - 6)]]
                    else:
                        wd_w = [ev_ddq[(q, dd - 1, jp)]]
                    evd = dma("qG", wdr[pp][:, :],
                              dwd8[128 * H:128 * (H + 1),
                                   512 * dd:512 * (dd + 1)],
                              waits=wd_w)
                dq_w = [evd]
                if j >= 6:
                    dq_w.append(ev_mmj[j - 6])
                elif ev_mmj_prev is not None:
                    dq_w.append(ev_mmj_prev[jp])
                ev_ddq[(q, dd, j)] = op(
                    "vector", lambda e, s, pp=pp: e.tensor_scalar(
                        wdd[pp][:, :], wdr[pp][:, :], -128.0, None, ALU.add),
                    waits=dq_w, ev=True)
                for tc in range(8):
                    ps, o = ps_slices[tc]
                    dmw = []
                    if tc == 0:
                        dmw = [ev_ddq[(q, dd, j)]]
                    if j == 0 and tc == 0 and dd == 0:
                        dmw += [ev_h[HCQ * q + HCQ - 1]]
                        if q == 0:
                            dmw += [ev_qnorm, ev_vev[7]]
                    if j == 0:
                        if dd >= 1:
                            dmw += [ev_acc[(q, dd - 1, tc)]]
                        elif q >= 1:
                            dmw += [ev_acc[(q - 1, 3, tc)]]
                    mm_last[tc] = op("tensor",
                                     lambda e, s, ps=ps, o=o, pp=pp, j=j, tc=tc: e.matmul(
                        ps[:, o:o + 512],
                        hfm[:, 1024 * j + 128 * tc:1024 * j + 128 * (tc + 1)],
                        wdd[pp][:, :],
                        start=(j == 0), stop=(j == HCQ - 1),
                        skip_group_check=True),
                        waits=dmw, ev=(j == HCQ - 1 or tc == 7))
                ev_mmj[j] = mm_last[7]
            ev_mmj_prev = ev_mmj
            for tc in range(8):
                ps, o = ps_slices[tc]
                if q == 0:
                    ev_acc[(q, dd, tc)] = op(
                        "vector", lambda e, s, ps=ps, o=o, tc=tc, dd=dd: e.tensor_mul(
                            acc[:, D * tc + 512 * dd:D * tc + 512 * (dd + 1)],
                            ps[:, o:o + 512], sdb[:, 512 * dd:512 * (dd + 1)]),
                        waits=[mm_last[tc]], ev=True)
                else:
                    e1d = op("vector", lambda e, s, ps=ps, o=o, dd=dd: e.tensor_mul(
                        ds_scr[:, :], ps[:, o:o + 512],
                        sdb[:, 512 * dd:512 * (dd + 1)]),
                        waits=[mm_last[tc]], ev=True)
                    ev_acc[(q, dd, tc)] = op(
                        "vector", lambda e, s, tc=tc, dd=dd: e.tensor_add(
                            acc[:, D * tc + 512 * dd:D * tc + 512 * (dd + 1)],
                            acc[:, D * tc + 512 * dd:D * tc + 512 * (dd + 1)],
                            ds_scr[:, :]), waits=[e1d], ev=True)
        if q + 1 < NQ:
            down_pre[q + 1] = emit_down_prefetch(q + 1)
        if DEBUG and q == 0:
            dbg_ffn_ev = [None]
            e1dmp = dma("qA", dbg_h0, hfm[:, :], waits=[ev_h[HCQ - 1]])
            dbg_ffn_ev[0] = dma("qA", dbg_acc0, acc[:, :],
                                waits=[ev_acc[(0, 3, 7)]])
    ffn_last = ev_acc[(NQ - 1, 3, 7)]
    ev_masks = dma("qA", masks[:, :], dmasks, waits=[ev_h[NHC - 1]])

    e_neg, e_lf2 = mlam_evs
    e_bc = op("tensor", lambda e, s: e.matmul(
        PD[:, 0:18], onescol_f[0:1, :], bcrow[0:1, 0:18],
        start=True, stop=True, skip_group_check=True),
        waits=[e_neg, e_lf2, ffn_last], ev=True)
    ev_negmpp = op("vector", lambda e, s: e.tensor_copy(
        negmpp[:, 0:18], PD[:, 0:18]), waits=[e_bc], ev=True)
    ev_lnws = op("scalar", lambda e, s: e.activation(
        lnw_s[:, 0:1], lnw[:, 0:1], AF.Copy, scale=(1.0 - LAMBDA_INIT)),
        waits=[ev_consts], ev=True)

    if DEBUG:
        dma("qA", dbg_acc, acc[:, :], waits=[ffn_last])
        dma("qA", dbg_negm, negmpp[:, 0:18], waits=[ev_negmpp])

    PV = PC
    LL = PD

    def mk_frozen2(name, colbase, mx):
        st = {"r": None, "rvs": [None] * NITEMS}

        def get(e, i):
            if st["r"] is None:
                st["r"] = e.alloc_register(name)
            if st["rvs"][i] is None:
                e.reg_load(st["r"], itab[0:1, colbase + i:colbase + i + 1])
                st["rvs"][i] = bass.RuntimeValue(
                    st["r"], min_val=0, max_val=mx + i)
            return st["rvs"][i]
        return get

    ve_rv = mk_frozen2("koff", 0, 7680)
    sy_rv = mk_frozen2("voff", NITEMS, (8 * TOK - 512) * 128)
    pe_state = {"r": [None, None]}

    def pe_flag(e, i):
        if pe_state["r"][0] is None:
            pe_state["r"][0] = e.alloc_register("qflag0")
            pe_state["r"][1] = e.alloc_register("qflag1")
        r = pe_state["r"][i % 2]
        e.reg_load(r, itab[0:1, 2 * NITEMS + i:2 * NITEMS + i + 1])
        return r

    def pe_flag_cached(e, j):
        return pe_state["r"][j % 2]

    dbg_kv_ev = [None]
    ev_kcache = ev_vcache = None
    ev_kv = {}
    ev_vv = {}
    ev_stq = {}
    ev_stqA = {}
    ev_exp = {}
    ev_expA = {}
    ev_msk = {}
    ev_pv = {}
    ev_evO = {}
    ev_epi_done = None
    ev_epi_ps = None
    ev_epi_act = None
    ev_epi_tr = None
    er1_ev = None
    for h in range(NH):
        d = h // 2
        kh = h // 4
        vh = h // 8
        isA = (h % 2 == 0)
        rb = 64 * (h % 2)
        if h % 4 == 0:
            kc_w = ([last_xn_read, ("cc", 3)] if h == 0
                    else [ev_kv[(h - 1, NITEMS - 1)]])
            for r in range(NCORES):
                ev_kcache = dma("qS", ksb[:, 1024 * r:1024 * (r + 1)],
                                kall_d.ap()[512 * r + 64 * kh:512 * r + 64 * (kh + 1), :],
                                waits=(kc_w if r == 0 else []))
        if h % 8 == 0:
            vc_w = [("cc", 3)] if h == 0 else [ev_vv[(h - 1, NITEMS - 1)]]
            for r in range(NCORES):
                ev_vcache = dma("qG",
                                vcur_d.ap()[1024 * r:1024 * (r + 1), :],
                                vall_d.ap()[4096 * r + 1024 * vh:4096 * r + 1024 * (vh + 1), :],
                                waits=(vc_w if r == 0 else []))
        if h == 0:
            stw0 = [ffn_last]
            pvw0 = [ev_negmpp]
        else:
            stw0 = [ev_exp[(h - 1, NITEMS - 1)]]
            pvw0 = [ev_evO[(h - 1, "o")], ev_evO[(h - 1, "l")]]
        if ev_epi_done is not None:
            pvw0.append(ev_epi_done)

        def pvl_body(e, s, ev, ssl, vbuf, vo, qa, st):
            for c in range(4):
                e.matmul(PV[:, qa:qa + 512],
                         vbuf[:, vo + 128 * c:vo + 128 * (c + 1)],
                         pt[:, ssl + 512 * c:ssl + 512 * (c + 1)],
                         start=(st and c == 0), stop=False,
                         skip_group_check=True)
            for c in range(4):
                mm = e.matmul(LL[0:64, qa:qa + 512], ones_b[:, 0:64],
                              pt[:, ssl + 512 * c:ssl + 512 * (c + 1)],
                              start=(st and c == 0), stop=False,
                              skip_group_check=True)
            mm.then_inc(s[ev[0]], 1)

        def pvl(e, s, h, j, ev):
            ssl = 2048 * (j % 2)
            vbuf = vstat if (j % 4) < 2 else vstat2
            vo = 512 * (j % 2)
            if j < 2:
                pvl_body(e, s, ev, ssl, vbuf, vo, 512 * j, True)
                return
            rf = pe_flag_cached(e, j)
            for cond in (0, 1):
                qa = 512 * cond
                cm = e.If_eq(rf, 0) if cond == 0 else e.Else()
                with cm:
                    pvl_body(e, s, ev, ssl, vbuf, vo, qa, False)
            e.end_ifs()

        for i in range(NITEMS):
            sl = i % 2
            dwk = []
            if i == 0:
                dwk = [ev_kcache, ev_consts]
            if i < 2 and h > 0:
                dwk = dwk + [ev_stq[(h - 1, NITEMS - 2 + i)]]
            if i >= 2:
                dwk = [ev_stq[(h, i - 2)]]

            def kcopy(e, s, i=i, sl=sl, rb=rb):
                v = ve_rv(e, i)
                return e.tensor_copy(kstat[rb:rb + 64, 512 * sl:512 * (sl + 1)],
                                     ksb[:, bass.ds(v, 512)])
            if DEBUG and h in (0, 16) and i == 2:
                tgt = dbg_kv if h == 0 else dbg_kv16
                dma("qA", tgt[:, 0:1024], kstat[:, :],
                    waits=[ev_kv[(h, 1)], ev_vv[(h, 1)]])
                dbg_kv_ev[0] = dma("qA", tgt[:, 1024:2048], vstat[:, :])
                dwk = dwk + [dbg_kv_ev[0]]
            ev_kv[(h, i)] = op("vector", kcopy, waits=dwk, ev=True)

            if 0 <= i - 1 < 2:
                ev_msk[(h, i - 1)] = op(
                    "vector", lambda e, s, psl=(i - 1) % 2: e.tensor_mul(
                        pt[:, 2048 * psl:2048 * (psl + 1)],
                        pt[:, 2048 * psl:2048 * (psl + 1)], masks[:, 0:2048]),
                    waits=[ev_exp[(h, i - 1)]] +
                          ([ev_masks] if (h == 0 and i == 1) else []), ev=True)

            if i == 1 and h % 2 == 1:
                w1 = [ev_evO[(h - 1, "l")]]
                if ev_epi_tr is not None:
                    w1.append(ev_epi_tr)
                er1_ev = op("vector", lambda e, s: e.reciprocal(
                    ra_sb[0:1, :], la_sb[0:1, :]), waits=w1, ev=True)

            dwv = []
            if i == 0 and h == 0:
                dwv = [ev_consts]
            if i == 0 and h % 8 == 0:
                dwv = dwv + [ev_vcache]
            if i < 4 and h > 0:
                dwv = dwv + [ev_pv[(h - 1, (16, 13, 14, 15)[i])]]
            if i < 4 and h == 0:
                dwv = dwv + [ffn_last]
            if i >= 4:
                dwv = [ev_pv[(h, i - 4)]]
            if DEBUG and h in (0, 16) and i == 2:
                dwv = dwv + [dbg_kv_ev[0]]

            def vcopy(e, s, i=i, sl=sl):
                v = sy_rv(e, i)
                vbuf = vstat if (i % 4) < 2 else vstat2
                src = bass.AP(vcur_d, v, [[128, 128], [16384, 4], [1, 128]])
                return e.dma_start(
                    out=vbuf[:, 512 * sl:512 * (sl + 1)]
                    .rearrange("p (c v) -> p c v", c=4), in_=src)
            ev_vv[(h, i)] = p.emit("gpsimd", vcopy, waits=dwv, inc="qG",
                                   inc_by=16)

            stw = [ev_kv[(h, i)]]
            stwB = []
            if i == 0:
                stw += stw0
            if i >= 1:
                stw.append(ev_expA[(h, i - 1)])
                stwB.append(ev_exp[(h, i - 1)])
            ev_stqA[(h, i)] = p.bump("sPE")

            def stmmA(e, s, h=h, i=i, sl=sl, rb=rb, ev=ev_stqA[(h, i)]):
                base = 1024 * (h // 2)
                tp = (rb, 0) if rb else None
                rf = pe_flag(e, i)
                for cond in (0, 1):
                    qa = base + 512 * cond
                    cm = e.If_eq(rf, 0) if cond == 0 else e.Else()
                    with cm:
                        rhs = qfm[rb:rb + 64, qa:qa + 512]
                        for c in range(2):
                            mm = e.matmul(
                                PST[:, 512 * c:512 * (c + 1)],
                                kstat[rb:rb + 64,
                                      512 * sl + 128 * c:512 * sl + 128 * (c + 1)],
                                rhs, start=True, stop=True,
                                skip_group_check=True, tile_position=tp)
                        mm.then_inc(s[ev[0]], 1)
                e.end_ifs()
            p.emit("tensor", stmmA, waits=stw)
            ev_stq[(h, i)] = p.bump("sPE")

            def stmmB(e, s, h=h, i=i, sl=sl, rb=rb, ev=ev_stq[(h, i)]):
                base = 1024 * (h // 2)
                tp = (rb, 0) if rb else None
                rf = pe_flag_cached(e, i)
                for cond in (0, 1):
                    qa = base + 512 * cond
                    cm = e.If_eq(rf, 0) if cond == 0 else e.Else()
                    with cm:
                        rhs = qfm[rb:rb + 64, qa:qa + 512]
                        for c in range(2, 4):
                            mm = e.matmul(
                                PST[:, 512 * c:512 * (c + 1)],
                                kstat[rb:rb + 64,
                                      512 * sl + 128 * c:512 * sl + 128 * (c + 1)],
                                rhs, start=True, stop=True,
                                skip_group_check=True, tile_position=tp)
                        mm.then_inc(s[ev[0]], 1)
                e.end_ifs()
            p.emit("tensor", stmmB, waits=stwB)

            if i >= 1:
                ev_pv[(h, i - 1)] = p.bump("sPE")
                p.emit("tensor",
                       lambda e, s, h=h, j=i - 1, ev=ev_pv[(h, i - 1)]: pvl(
                           e, s, h, j, ev),
                       waits=[ev_vv[(h, i - 1)]] +
                             ([ev_msk[(h, i - 1)]] + pvw0
                              if i - 1 < 2 else []))

            ew = [ev_stqA[(h, i)]]
            if i >= 2:
                ew.append(ev_pv[(h, i - 2)])
            elif h > 0:
                ew.append(ev_pv[(h - 1, NITEMS - 2 + i)])
            ev_expA[(h, i)] = op("scalar", lambda e, s, sl=sl, d=d: e.activation(
                pt[:, 2048 * sl:2048 * sl + 1024], PST[:, 0:1024],
                AF.Exp, bias=negmpp[:, d:d + 1]), waits=ew, ev=True)
            ev_exp[(h, i)] = op("scalar", lambda e, s, sl=sl, d=d: e.activation(
                pt[:, 2048 * sl + 1024:2048 * sl + 2048], PST[:, 1024:2048],
                AF.Exp, bias=negmpp[:, d:d + 1]),
                waits=[ev_stq[(h, i)]], ev=True)
        ev_pv[(h, NITEMS - 1)] = p.bump("sPE")
        p.emit("tensor",
               lambda e, s, h=h, j=NITEMS - 1, ev=ev_pv[(h, NITEMS - 1)]: pvl(
                   e, s, h, j, ev),
               waits=[ev_exp[(h, NITEMS - 1)], ev_vv[(h, NITEMS - 1)]])

        odst = o1s if isA else o2s
        ldst = la_sb if isA else lb_sb
        evo_w = [ev_pv[(h, NITEMS - 1)]]
        if h == 0:
            evo_w.append(ffn_last)
        ev_evO[(h, "o")] = op("vector", lambda e, s, odst=odst: e.tensor_copy(
            odst[:, :], PV[:, 0:1024]), waits=evo_w, ev=True)
        ev_evO[(h, "l")] = op("vector", lambda e, s, ldst=ldst: e.tensor_copy(
            ldst[0:1, :], LL[0:1, 0:1024]), ev=True)
        if DEBUG and h == 1:
            dma("qA", dbg_l0[0:1, :], la_sb[0:1, :], waits=[ev_evO[(h, "l")]])
            dma("qA", dbg_l0[1:2, :], lb_sb[0:1, :])
            dma("qA", dbg_o0[:, 0:1024], o1s[:, :])
            dma("qA", dbg_o0[:, 1024:2048], o2s[:, :])

        if not isA:
            er_w = [ev_evO[(h, "l")]]
            if ev_epi_tr is not None:
                er_w.append(ev_epi_tr)
            er2a = op("vector", lambda e, s: e.reciprocal(
                rb_sb[0:1, 0:512], lb_sb[0:1, 0:512]), waits=er_w, ev=True)
            er2b = op("vector", lambda e, s: e.reciprocal(
                rb_sb[0:1, 512:1024], lb_sb[0:1, 512:1024]), ev=True)
            to_ev = None
            for qq in range(8):
                to_ev = op("tensor", lambda e, s, qq=qq: e.transpose(
                    PCb[:, 128 * qq:128 * (qq + 1)],
                    o1s[:, 128 * qq:128 * (qq + 1)], ident_b[:, :]),
                    waits=([ev_evO[(h - 1, "o")], ev_evO[(h, "o")]]
                           if qq == 0 else []))
            for qq in range(8):
                to_ev = op("tensor", lambda e, s, qq=qq: e.transpose(
                    PCb[:, 1024 + 128 * qq:1024 + 128 * (qq + 1)],
                    o2s[:, 128 * qq:128 * (qq + 1)], ident_b[:, :]),
                    ev=(qq == 7))
            tr_ev = None
            for qq in range(8):
                tr_ev = op("tensor", lambda e, s, qq=qq: e.transpose(
                    PD[:, qq:qq + 1], ra_sb[0:1, 128 * qq:128 * (qq + 1)],
                    onescol_f[0:1, 0:1]),
                    waits=([er1_ev, ev_evO[(h, "l")]] if qq == 0 else []))
            for qq in range(8):
                tr_ev = op("tensor", lambda e, s, qq=qq: e.transpose(
                    PD[:, 8 + qq:8 + qq + 1],
                    rb_sb[0:1, 128 * qq:128 * (qq + 1)], onescol_f[0:1, 0:1]),
                    waits=([er2a] if qq == 0 else ([er2b] if qq == 4 else [])),
                    ev=(qq == 7))
            ev_epi_tr = tr_ev
            e_rpp = op("vector", lambda e, s: e.tensor_copy(
                rpp[:, :], PD[:, 0:16]), waits=[tr_ev], ev=True)
            e_rcp = e_rpp
            e_rbn = op("vector", lambda e, s: e.tensor_scalar(
                rbn[:, :], rpp[:, 8:16], negmpp[:, 16:17], None, ALU.mult),
                ev=True)
            e_oda = [None] * 8
            for qq in range(8):
                e_oda[qq] = op("vector", lambda e, s, qq=qq: e.tensor_scalar(
                    scr1[:, 128 * qq:128 * (qq + 1)],
                    PCb[:, 128 * qq:128 * (qq + 1)], rpp[:, qq:qq + 1], None,
                    ALU.mult),
                    waits=([to_ev, e_rcp] if qq == 0 else []), ev=True)
            e_od = [None] * 8
            for qq in range(8):
                e_od[qq] = op("vector", lambda e, s, qq=qq: e.scalar_tensor_tensor(
                    odb[:, 128 * qq:128 * (qq + 1)],
                    PCb[:, 1024 + 128 * qq:1024 + 128 * (qq + 1)], rbn[:, qq:qq + 1],
                    scr1[:, 128 * qq:128 * (qq + 1)], ALU.mult, ALU.add),
                    waits=[e_oda[qq]] + ([e_rbn] if qq == 0 else []), ev=True)
            e_ssq = None
            for qq in range(8):
                e_ssq = op("vector", lambda e, s, qq=qq: e.scalar_tensor_tensor(
                    scr1[:, 128 * qq:128 * (qq + 1)],
                    odb[:, 128 * qq:128 * (qq + 1)], 1.0,
                    odb[:, 128 * qq:128 * (qq + 1)], ALU.bypass, ALU.mult,
                    accum_out=ssq8[:, qq:qq + 1]),
                    waits=[e_od[qq]], ev=(qq == 7))
            e_std = op("scalar", lambda e, s: e.activation(
                ssq8[:, :], ssq8[:, :], AF.Sqrt, scale=1.0 / 128,
                bias=cbias[:, 0:1]), waits=[e_ssq], ev=True)
            e_rst = op("vector", lambda e, s: e.reciprocal(
                rst8[:, :], ssq8[:, :]), waits=[e_std], ev=True)
            e_on = None
            for qq in range(8):
                e_on = op("vector", lambda e, s, qq=qq: e.tensor_scalar(
                    onb[:, 128 * qq:128 * (qq + 1)],
                    odb[:, 128 * qq:128 * (qq + 1)], rst8[:, qq:qq + 1], None,
                    ALU.mult),
                    waits=([e_rst] if qq == 0 else []), ev=(qq == 7))
            ev_epi_act = e_on
            tb_ev = None
            for qq in range(8):
                tb_ev = op("tensor", lambda e, s, qq=qq: e.transpose(
                    PDb[:, 1024 + 128 * qq:1024 + 128 * (qq + 1)],
                    onb[:, 128 * qq:128 * (qq + 1)], ident_b[:, :]),
                    waits=([e_on] if qq == 0 else []), ev=(qq == 7))
            ev_epi_ps = tb_ev
            ev_epi_done = op("vector", lambda e, s, d=d: e.tensor_scalar(
                ofm[:, 1024 * d:1024 * (d + 1)], PDb[:, 1024:2048],
                lnw_s[:, 0:1], None, ALU.mult),
                waits=[tb_ev] + ([ev_lnws] if d == 0 else []), ev=True)

    attn_done = ev_epi_done
    attn_kv_done = ev_kv[(NH - 1, NITEMS - 1)]
    if DEBUG:
        dma("qA", dbg_accpre, acc[:, :], waits=[attn_done])
        dma("qA", dbg_ofm, ofm[:, :], waits=[attn_done])
        dma("qA", dbg_l[0:1, :], la_sb[0:1, :], waits=[attn_done])
        dma("qA", dbg_l[1:2, :], lb_sb[0:1, :], waits=[attn_done])

    ev_stq_last = ev_stq[(NH - 1, NITEMS - 1)]
    ev_wo_q = {}
    for ic in range(16):
        q3 = ("qG", "qS", "qA")[ic % 3]
        w = [ev_stq_last, attn_kv_done] if ic < 3 else []
        ev_wo_q[q3] = dma(q3, wo_all[:, 2048 * ic:2048 * (ic + 1)],
                          dwoT[128 * ic:128 * (ic + 1), :], waits=w,
                          fifo=False)
    ev_wo_gate = list(ev_wo_q.values())
    ev_acc2 = {}
    oc_bank = {0: (PA, 0), 1: (PB, 0), 2: (PC, 0), 3: (PC, 512)}
    oc_order = (2, 3, 0, 1)
    for tc in range(8):
        for j_oc, oc in enumerate(oc_order):
            ps, o = oc_bank[oc]
            mm = None
            for ic in range(16):
                mw = []
                if ic == 0:
                    if tc == 0 and j_oc == 0:
                        mw = ev_wo_gate + [attn_done]
                    if tc >= 1:
                        mw.append(ev_acc2[(tc - 1, oc)])
                mm = op("tensor", lambda e, s, ps=ps, o=o, ic=ic, tc=tc, oc=oc: e.matmul(
                    ps[:, o:o + 512],
                    ofm[:, 1024 * ic + 128 * tc:1024 * ic + 128 * (tc + 1)],
                    wo_all[:, 2048 * ic + 512 * oc:2048 * ic + 512 * (oc + 1)],
                    start=(ic == 0), stop=(ic == 15), skip_group_check=True),
                    waits=mw, ev=(ic == 15))
            ev_acc2[(tc, oc)] = op(
                "vector", lambda e, s, ps=ps, o=o, tc=tc, oc=oc: e.tensor_add(
                    acc[:, D * tc + 512 * oc:D * tc + 512 * (oc + 1)],
                    acc[:, D * tc + 512 * oc:D * tc + 512 * (oc + 1)],
                    ps[:, o:o + 512]), waits=[mm], ev=True)

    if DEBUG:
        dma("qA", dbg_wo, wo_all[:, :], waits=ev_wo_gate)
        dma("qA", dbg_accw, acc[:, :], waits=[ev_acc2[(7, 3)]])

    outb = [out0, out1]
    ev_store = [None] * 8
    ev_add = [None] * 8
    for tc in range(8):
        w = [ev_acc2[(tc, oc)] for oc in range(4)]
        exld = dma("qA", xs[tc % 2][:, :], dx[128 * tc:128 * (tc + 1), :],
                   waits=w + ([ev_add[tc - 2]] if tc >= 2 else []))
        ev_add[tc] = op("vector", lambda e, s, tc=tc: e.tensor_add(
            outb[tc % 2][:, :], acc[:, D * tc:D * (tc + 1)], xs[tc % 2][:, :]),
            waits=[exld] + ([ev_store[tc - 2]] if tc >= 2 else []), ev=True)
        ev_store[tc] = dma("qS", dy[128 * tc:128 * (tc + 1), :],
                           outb[tc % 2][:, :], waits=[ev_add[tc]])
    final_qA = p.cnt["qA"]
    final_qS = p.cnt["qS"]
    p.emit("scalar", lambda e, s: e.nop(),
           waits=[("qA", final_qA), ("qS", final_qS)])

    with ExitStack() as ctx:
        sems = {n: ctx.enter_context(nc.semaphore(n)) for n in sem_names}
        with nc.Block() as block:
            @block.sync
            def _(eng):
                p.replay("sync", eng, sems)

            @block.gpsimd
            def _(eng):
                p.replay("gpsimd", eng, sems)

            @block.vector
            def _(eng):
                p.replay("vector", eng, sems)

            @block.scalar
            def _(eng):
                p.replay("scalar", eng, sems)

            @block.tensor
            def _(eng):
                p.replay("tensor", eng, sems)

    return nc


def _koff(b):
    r = b if b < 8 else 15 - b
    return 1024 * r + (512 if b >= 8 else 0)


def _item_table(core):
    bA, bB = core, 15 - core
    ko = [_koff(bA), _koff(bB)]
    qo = [0, 512]
    for t in range(bA):
        ko.append(_koff(t))
        qo.append(0)
    for t in range(bB):
        ko.append(_koff(t))
        qo.append(512)
    assert len(ko) == NITEMS
    vo = [128 * k for k in ko]
    return np.array([ko + vo + qo], dtype=np.int32)


def _masks():
    m = np.zeros((128, 2048), dtype=np.float32)
    qf = np.arange(512)[None, :]
    kp = np.arange(128)[:, None]
    for c in range(4):
        allow = (qf - kp) >= 128 * c
        m[:, 512 * c:512 * (c + 1)] = np.where(allow, 1.0, 0.0)
    return np.ascontiguousarray(m.astype(BF))


_NC_CACHE = {}


def kernel(L0_attn_x, L0_attn_Wq, L0_attn_Wk, L0_attn_Wv, L0_attn_Wo,
           L0_attn_lambda_q1, L0_attn_lambda_k1, L0_attn_lambda_q2,
           L0_attn_lambda_k2, L0_attn_ln_w,
           L0_ffn_wq_gate, L0_ffn_s_gate, L0_ffn_wq_up, L0_ffn_s_up,
           L0_ffn_wq_down, L0_ffn_s_down):
    x = np.ascontiguousarray(np.asarray(L0_attn_x, dtype=np.float32)
                             .reshape(8192, D))
    C = np.ascontiguousarray
    lam4 = np.concatenate([
        np.asarray(L0_attn_lambda_q1, np.float32).reshape(1, 64),
        np.asarray(L0_attn_lambda_k1, np.float32).reshape(1, 64),
        np.asarray(L0_attn_lambda_q2, np.float32).reshape(1, 64),
        np.asarray(L0_attn_lambda_k2, np.float32).reshape(1, 64)], axis=1)
    shared = {
        "wqT": C(np.asarray(L0_attn_Wq, np.float32).T.astype(BF)),
        "wkT": C(np.asarray(L0_attn_Wk, np.float32).T.astype(BF)),
        "wvT": C(np.asarray(L0_attn_Wv, np.float32).T.astype(BF)),
        "woT": C(np.asarray(L0_attn_Wo, np.float32).T.astype(BF)),
        "wg8": C(np.asarray(L0_ffn_wq_gate).reshape(NHC, 128, 16, 128)
                 .transpose(0, 3, 2, 1).reshape(NHC, 128, D).astype(np.uint8)),
        "wu8": C(np.asarray(L0_ffn_wq_up).reshape(NHC, 128, 16, 128)
                 .transpose(0, 3, 2, 1).reshape(NHC, 128, D).astype(np.uint8)),
        "wd8": C(np.asarray(L0_ffn_wq_down).T.astype(np.uint8)),
        "sgpp": C(np.pad(np.asarray(L0_ffn_s_gate, np.float32).reshape(NHC, 128).T, ((0, 0), (0, 4)))),
        "supp": C(np.pad(np.asarray(L0_ffn_s_up, np.float32).reshape(NHC, 128).T, ((0, 0), (0, 4)))),
        "sdb": C(np.broadcast_to(np.asarray(L0_ffn_s_down, np.float32)[None, :],
                                 (128, D))),
        "lnw": C(np.pad(np.asarray(L0_attn_ln_w, np.float32).reshape(128, 1), ((0, 0), (0, 7)))),
        "lam4": C(lam4),
        "ident_b": np.eye(128, dtype=BF),
        "ones_b": np.ones((128, 128), dtype=BF),
        "zrow_b": np.zeros((1, 512), dtype=BF),
        "onesrow_b": np.ones((1, 512), dtype=BF),
        "onescol_f": np.ones((1, 128), dtype=np.float32),
        "onescol128": np.ones((128, 1), dtype=np.float32),
        "cbias": np.broadcast_to(np.array(
            [EPS, -1.0, -LAMBDA_INIT, 0.0, 0.0, 0.0, 0.0, 0.0],
            dtype=np.float32)[None, :], (128, 8)).copy(),
        "masks": _masks(),
    }
    in_maps = []
    for c in range(NCORES):
        bA, bB = c, 15 - c
        m = dict(shared)
        m["x_own"] = C(np.concatenate(
            [x[512 * bA:512 * (bA + 1)], x[512 * bB:512 * (bB + 1)]], axis=0))
        m["itab"] = _item_table(c)
        in_maps.append(m)

    if "nc" not in _NC_CACHE:
        _NC_CACHE["nc"] = build_nc()
    nc = _NC_CACHE["nc"]
    res = run_bass_kernel_spmd(nc, in_maps, core_ids=list(range(NCORES)))

    out = np.empty((8192, D), dtype=np.float32)
    for c in range(NCORES):
        y = np.asarray(res.results[c]["y"], dtype=np.float32)
        out[512 * c:512 * (c + 1)] = y[0:512]
        out[512 * (15 - c):512 * (16 - c)] = y[512:1024]
    return out.reshape(1, 8192, D)


# version 24
# speedup vs baseline: 1.0255x; 1.0041x over previous
import math
from contextlib import ExitStack

import numpy as np
import ml_dtypes

import concourse.bass as bass
import concourse.mybir as mybir
from concourse.bass_utils import run_bass_kernel_spmd

F32 = mybir.dt.float32
BF16 = mybir.dt.bfloat16
I32 = mybir.dt.int32
U8 = mybir.dt.uint8
AF = mybir.ActivationFunctionType
ALU = mybir.AluOpType
AX = mybir.AxisListType

NCORES = 8
D = 2048
TOK = 1024
HID = 5632
NHC = 44
NQ = 4
HCQ = NHC // NQ
NH = 32
NDH = 16
NITEMS = 17
LAMBDA_INIT = 0.8 - 0.6 * math.exp(-0.3 * 12)
EPS = 1e-5
NEGBIG = -1e9
BF = ml_dtypes.bfloat16
DEBUG = False


class Prog:
    ENGINES = ("sync", "gpsimd", "vector", "scalar", "tensor")

    def __init__(self):
        self.ops = {e: [] for e in self.ENGINES}
        self.cnt = {}

    QUEUES = ("qS", "qG", "qA")

    def emit(self, engine, fn, waits=(), inc=None, inc_by=1, fifo=True):
        ev = None
        w = [x for x in waits if x]
        if inc is not None:
            if fifo and inc in self.QUEUES and self.cnt.get(inc, 0) > 0:
                w.append((inc, self.cnt[inc]))
            self.cnt[inc] = self.cnt.get(inc, 0) + inc_by
            ev = (inc, self.cnt[inc])
        self.ops[engine].append((fn, tuple(w), inc, inc_by))
        return ev

    def bump(self, sem):
        self.cnt[sem] = self.cnt.get(sem, 0) + 1
        return (sem, self.cnt[sem])

    def replay(self, engine, eng, sems):
        for fn, waits, inc, inc_by in self.ops[engine]:
            for (s, c) in waits:
                eng.wait_ge(sems[s], c)
            ins = fn(eng, sems)
            if inc is not None:
                ins.then_inc(sems[inc], inc_by)


def build_nc():
    nc = bass.Bass("TRN2", target_bir_lowering=False, debug=False)

    dx = nc.dram_tensor("x_own", [TOK, D], F32, kind="ExternalInput").ap()
    dwqT = nc.dram_tensor("wqT", [D, D], BF16, kind="ExternalInput").ap()
    dwkT = nc.dram_tensor("wkT", [D, 512], BF16, kind="ExternalInput").ap()
    dwvT = nc.dram_tensor("wvT", [D, 512], BF16, kind="ExternalInput").ap()
    dwoT = nc.dram_tensor("woT", [D, D], BF16, kind="ExternalInput").ap()
    dwg8 = nc.dram_tensor("wg8", [NHC, 128, D], U8, kind="ExternalInput").ap()
    dwu8 = nc.dram_tensor("wu8", [NHC, 128, D], U8, kind="ExternalInput").ap()
    dwd8 = nc.dram_tensor("wd8", [HID, D], U8, kind="ExternalInput").ap()
    dsg = nc.dram_tensor("sgpp", [128, 48], F32, kind="ExternalInput").ap()
    dsu = nc.dram_tensor("supp", [128, 48], F32, kind="ExternalInput").ap()
    dsd = nc.dram_tensor("sdb", [128, D], F32, kind="ExternalInput").ap()
    dlnw = nc.dram_tensor("lnw", [128, 8], F32, kind="ExternalInput").ap()
    dlam = nc.dram_tensor("lam4", [1, 256], F32, kind="ExternalInput").ap()
    dident_b = nc.dram_tensor("ident_b", [128, 128], BF16, kind="ExternalInput").ap()
    dones_b = nc.dram_tensor("ones_b", [128, 128], BF16, kind="ExternalInput").ap()
    dzrow_b = nc.dram_tensor("zrow_b", [1, 512], BF16, kind="ExternalInput").ap()
    donesrow_b = nc.dram_tensor("onesrow_b", [1, 512], BF16, kind="ExternalInput").ap()
    donescol_f = nc.dram_tensor("onescol_f", [1, 128], F32, kind="ExternalInput").ap()
    donescol128 = nc.dram_tensor("onescol128", [128, 1], F32, kind="ExternalInput").ap()
    dmasks = nc.dram_tensor("masks", [128, 2048], BF16, kind="ExternalInput").ap()
    dcbias = nc.dram_tensor("cbias", [128, 8], F32, kind="ExternalInput").ap()
    ditab = nc.dram_tensor("itab", [1, 3 * NITEMS], I32, kind="ExternalInput").ap()
    dy = nc.dram_tensor("y", [TOK, D], F32, kind="ExternalOutput").ap()
    if DEBUG:
        dbg_xn = nc.dram_tensor("dbg_xn", [128, 16 * TOK], BF16, kind="ExternalOutput").ap()
        dbg_qfm = nc.dram_tensor("dbg_qfm", [128, 16 * TOK], BF16, kind="ExternalOutput").ap()
        dbg_kall = nc.dram_tensor("dbg_kall", [NCORES * 512, TOK], BF16, kind="ExternalOutput").ap()
        dbg_vall = nc.dram_tensor("dbg_vall", [NCORES * 4 * TOK, 128], BF16, kind="ExternalOutput").ap()
        dbg_acc = nc.dram_tensor("dbg_acc", [128, 8 * D], BF16, kind="ExternalOutput").ap()
        dbg_negm = nc.dram_tensor("dbg_negm", [128, 18], F32, kind="ExternalOutput").ap()
        dbg_ofm = nc.dram_tensor("dbg_ofm", [128, NDH * TOK], BF16, kind="ExternalOutput").ap()
        dbg_l = nc.dram_tensor("dbg_l", [2, 1024], F32, kind="ExternalOutput").ap()
        dbg_h0 = nc.dram_tensor("dbg_h0", [128, HCQ * TOK], BF16, kind="ExternalOutput").ap()
        dbg_acc0 = nc.dram_tensor("dbg_acc0", [128, 8 * D], BF16, kind="ExternalOutput").ap()
        dbg_kv = nc.dram_tensor("dbg_kv", [128, 2048], BF16, kind="ExternalOutput").ap()
        dbg_kv16 = nc.dram_tensor("dbg_kv16", [128, 2048], BF16, kind="ExternalOutput").ap()
        dbg_wo = nc.dram_tensor("dbg_wo", [128, 16 * D], BF16, kind="ExternalOutput").ap()
        dbg_accpre = nc.dram_tensor("dbg_accpre", [128, 8 * D], BF16, kind="ExternalOutput").ap()
        dbg_accw = nc.dram_tensor("dbg_accw", [128, 8 * D], BF16, kind="ExternalOutput").ap()
        dbg_o0 = nc.dram_tensor("dbg_o0", [128, 2048], BF16, kind="ExternalOutput").ap()
        dbg_l0 = nc.dram_tensor("dbg_l0", [2, 1024], F32, kind="ExternalOutput").ap()
        dbg_gate = nc.dram_tensor("dbg_gate", [128, 512], F32, kind="ExternalOutput").ap()
        dbg_up = nc.dram_tensor("dbg_up", [128, 512], F32, kind="ExternalOutput").ap()
        dbg_sil = nc.dram_tensor("dbg_sil", [128, 512], BF16, kind="ExternalOutput").ap()
        dbg_wgd = nc.dram_tensor("dbg_wgd", [128, D], BF16, kind="ExternalOutput").ap()

    kst_d = nc.dram_tensor("k_stage", [512, TOK], BF16, kind="Internal")
    vst_d = nc.dram_tensor("v_stage", [4 * TOK, 128], BF16, kind="Internal")
    knst_d = nc.dram_tensor("kn_stage", [1, 8], F32, kind="Internal")
    kall_d = nc.dram_tensor("k_all", [NCORES * 512, TOK], BF16, kind="Internal")
    vall_d = nc.dram_tensor("v_all", [NCORES * 4 * TOK, 128], BF16, kind="Internal")
    knall_d = nc.dram_tensor("kn_all", [1, 8], F32, kind="Internal")
    vcur_d = nc.dram_tensor("v_cur", [8 * TOK + 16, 128], BF16, kind="Internal")

    off = [16384]

    def sb(name, shape, dtype, at=None):
        nbytes = shape[1] * {F32: 4, BF16: 2, I32: 4, U8: 1}[dtype]
        if at is None:
            at = off[0]
            off[0] = at + ((nbytes + 31) // 32) * 32
        return nc.alloc_sbuf_tensor_at(name, shape, dtype, offset=at)

    ident_b = sb("ident_b", [128, 128], BF16)
    ones_b = sb("ones_b", [128, 128], BF16)
    zrow_b = sb("zrow_b", [1, 512], BF16)
    onesrow_b = sb("onesrow_b", [1, 512], BF16)
    onescol_f = sb("onescol_f", [1, 128], F32)
    onescol128 = sb("onescol128", [128, 1], F32)
    cbias = sb("cbias", [128, 8], F32)
    sdb = sb("sdb", [128, D], F32)
    sgpp = sb("sgpp", [128, 48], F32)
    supp = sb("supp", [128, 48], F32)
    lnw = sb("lnw", [128, 8], F32)
    lnw_s = sb("lnw_s", [128, 8], F32)
    lam4 = sb("lam4", [1, 256], F32)
    lscr = sb("lscr", [1, 128], F32)
    stats = sb("stats", [128, 16], F32)
    nmq = sb("nmq", [1, 32], F32)
    nmk = sb("nmk", [1, 8], F32)
    mrow = sb("mrow", [1, 32], F32)
    bcrow = sb("bcrow", [1, 18], F32)
    negmpp = sb("negmpp", [128, 24], F32)
    itab = sb("itab", [1, 3 * NITEMS], I32)
    rpp = sb("rpp", [128, 16], F32)
    rbn = sb("rbn", [128, 8], F32)
    ssq8 = sb("ssq8", [128, 8], F32)
    rst8 = sb("rst8", [128, 8], F32)
    acc = sb("acc", [128, 8 * D], BF16)
    C0 = off[0]
    ofm = sb("ofm", [128, NDH * TOK], BF16)
    wdr2 = sb("wdr2", [128, 512], BF16, at=C0 + 24576)
    wdr3 = sb("wdr3", [128, 512], BF16, at=C0 + 25600)
    wdd2 = sb("wdd2", [128, 512], BF16, at=C0 + 26624)
    wdd3 = sb("wdd3", [128, 512], BF16, at=C0 + 27648)
    wdr4 = sb("wdr4", [128, 512], BF16, at=C0 + 28672)
    wdr5 = sb("wdr5", [128, 512], BF16, at=C0 + 29696)
    wdd4 = sb("wdd4", [128, 512], BF16, at=C0 + 30720)
    wdd5 = sb("wdd5", [128, 512], BF16, at=C0 + 31744)
    D0 = off[0]
    xnfm = sb("xnfm", [128, 16 * TOK], BF16)
    ksb = sb("ksb", [64, 8704], BF16, at=D0)
    vsb = sb("vsb", [128, 8192], BF16, at=D0 + 16384)
    E0 = off[0]
    qfm = sb("qfm", [128, 16 * TOK], BF16)
    wo_all = sb("wo_all", [128, 16 * D], BF16, at=D0)
    assert E0 - D0 == 32 * 1024
    F0 = off[0]
    hfm = sb("hfm", [128, HCQ * TOK], BF16)
    pt = sb("pt", [128, 4096], BF16, at=F0)
    o1s = sb("o1s", [128, 1024], BF16, at=F0 + 8192)
    o2s = sb("o2s", [128, 1024], BF16, at=F0 + 10240)
    kstat = sb("kstat", [128, 1024], BF16, at=F0 + 12288)
    vstat = sb("vstat", [128, 1024], BF16, at=F0 + 14336)
    onb = sb("onb", [128, 1024], BF16, at=F0 + 16384)
    vstat2 = sb("vstat2", [128, 1024], BF16, at=F0 + 18432)
    kfm = sb("kfm", [128, 4096], BF16, at=F0)
    assert 22 * 1024 >= 20480
    G0 = off[0]
    xs0 = sb("xs0", [128, D], F32, at=G0)
    xs1 = sb("xs1", [128, D], F32, at=G0 + 8192)
    la_sb = sb("la_sb", [1, 1024], F32, at=G0)
    lb_sb = sb("lb_sb", [1, 1024], F32, at=G0 + 4096)
    ra_sb = sb("ra_sb", [1, 1024], F32, at=G0 + 8192)
    rb_sb = sb("rb_sb", [1, 1024], F32, at=G0 + 12288)
    wvsb = sb("wvsb", [128, 8192], BF16, at=G0 + 16384)
    out0 = sb("out0", [128, D], F32, at=G0 + 16384)
    out1 = sb("out1", [128, D], F32, at=G0 + 24576)
    masks = sb("masks", [128, 2048], BF16, at=G0 + 16384)
    odb = sb("odb", [128, 1024], F32, at=G0 + 24576)
    scr1 = sb("scr1", [128, 1024], F32, at=G0 + 28672)
    wqs0 = sb("wqs0", [128, D], BF16, at=G0 + 32768)
    wqs1 = sb("wqs1", [128, D], BF16, at=G0 + 36864)
    wgr0 = sb("wgr0", [128, D], BF16, at=G0)
    wgr1 = sb("wgr1", [128, D], BF16, at=G0 + 4096)
    wur0 = sb("wur0", [128, D], BF16, at=G0 + 8192)
    wur1 = sb("wur1", [128, D], BF16, at=G0 + 12288)
    wgd0 = sb("wgd0", [128, D], BF16, at=G0 + 16384)
    wgd1 = sb("wgd1", [128, D], BF16, at=G0 + 20480)
    wud0 = sb("wud0", [128, D], BF16, at=G0 + 24576)
    wud1 = sb("wud1", [128, D], BF16, at=G0 + 28672)
    wdr0 = sb("wdr0", [128, 512], BF16, at=G0 + 32768)
    wdr1 = sb("wdr1", [128, 512], BF16, at=G0 + 33792)
    wdd0 = sb("wdd0", [128, 512], BF16, at=G0 + 34816)
    wdd1 = sb("wdd1", [128, 512], BF16, at=G0 + 35840)
    ds_scr = sb("ds_scr", [128, 512], F32, at=G0 + 36864)
    sil_scr = sb("sil_scr", [128, 1024], BF16, at=G0 + 38912)
    G_end = G0 + 40960
    xc = sb("xc", [128, D], F32, at=F0)
    sqs = sb("sqs", [128, 2048], F32, at=F0 + 8192)
    sqsb = sb("sqsb", [128, 2048], BF16, at=F0 + 8192)
    total = max(G_end, off[0])
    assert total <= 229312, f"SBUF over budget: {total}"

    xcb = xc.bitcast(BF16)
    vtmb = sqs.bitcast(BF16)

    PST = nc.alloc_psum_tensor("PST", [128, 2048], F32)
    PCt = nc.alloc_psum_tensor("PCt", [128, 1024], F32)
    PDt = nc.alloc_psum_tensor("PDt", [128, 1024], F32)
    PSTb = PST.bitcast(BF16)
    PCb = PCt.bitcast(BF16)
    PDb = PDt.bitcast(BF16)

    class PView:
        def __init__(self, t, off, width):
            self.t, self.off, self.width = t, off, width

        def __getitem__(self, idx):
            p, f = idx
            a = 0 if f.start is None else f.start
            b = self.width if f.stop is None else f.stop
            return self.t[p, a + self.off:b + self.off]

    PA = PView(PST, 0, 1024)
    PB = PView(PST, 1024, 1024)
    PC = PView(PCt, 0, 1024)
    PD = PView(PDt, 0, 1024)
    PAb = PView(PSTb, 0, 2048)
    PBb = PView(PSTb, 2048, 2048)

    sem_names = ["qS", "qG", "qA", "sPE", "sDVE", "sACT", "cc"]
    p = Prog()

    def dma(q, out, in_, waits=(), fifo=True):
        eng = {"qS": "sync", "qG": "gpsimd", "qA": "scalar"}[q]
        return p.emit(eng, lambda e, s, o=out, i=in_: e.dma_start(out=o, in_=i),
                      waits=waits, inc=q, inc_by=16, fifo=fifo)

    def op(engine, fn, waits=(), ev=False):
        sem = {"vector": "sDVE", "scalar": "sACT", "tensor": "sPE"}[engine]
        return p.emit(engine, fn, waits=waits, inc=(sem if ev else None))

    ev_ident = dma("qA", ident_b[:, :], dident_b)
    ev_cbias = dma("qA", cbias[:, :], dcbias)
    for (dst, src) in [(ones_b, dones_b), (zrow_b, dzrow_b),
                       (onesrow_b, donesrow_b), (onescol_f, donescol_f),
                       (onescol128, donescol128),
                       (sdb, dsd), (sgpp, dsg), (supp, dsu),
                       (lnw, dlnw), (itab, ditab)]:
        dma("qA", dst[:, :], src)
    ev_consts = dma("qA", lam4[0:1, :], dlam)

    xs = [xs0, xs1]
    ev_xn_act = [None] * 8
    ev_trans = [None] * 8
    ev_xn_evac = [None] * 8
    for tc in range(8):
        sc = 8 * (tc % 2)
        t = xs[tc % 2]
        ev_xld = dma("qS", t[:, :], dx[128 * tc:128 * (tc + 1), :],
                     waits=([ev_xn_act[tc - 2]] if tc >= 2 else []))
        e1 = op("vector", lambda e, s, t=t, sc=sc: e.reduce_sum(
            stats[:, sc:sc + 1], t[:, :], AX.X, negate=True),
            waits=[ev_xld] + ([ev_xn_act[tc - 2]] if tc >= 2 else []), ev=True)
        e2 = op("vector", lambda e, s, t=t, sc=sc: e.scalar_tensor_tensor(
            sqs[:, 0:D], t[:, :], 1.0, t[:, :], ALU.bypass, ALU.mult,
            accum_out=stats[:, sc + 2:sc + 3]), ev=True)
        e3 = op("scalar", lambda e, s, sc=sc: e.activation(
            stats[:, sc + 1:sc + 2], stats[:, sc:sc + 1], AF.Copy,
            scale=1.0 / D), waits=[e1], ev=True)
        e4 = op("scalar", lambda e, s, sc=sc: e.activation(
            stats[:, sc + 3:sc + 4], stats[:, sc + 2:sc + 3], AF.Copy,
            scale=1.0 / D), waits=[e2], ev=True)
        e5 = op("vector", lambda e, s, sc=sc: e.scalar_tensor_tensor(
            stats[:, sc + 4:sc + 5], stats[:, sc + 1:sc + 2], 1.0,
            stats[:, sc + 1:sc + 2], ALU.bypass, ALU.mult),
            waits=[e3], ev=True)
        e6 = op("vector", lambda e, s, sc=sc: e.tensor_sub(
            stats[:, sc + 5:sc + 6], stats[:, sc + 3:sc + 4],
            stats[:, sc + 4:sc + 5]), waits=[e4, e5], ev=True)
        e7 = op("scalar", lambda e, s, sc=sc: e.activation(
            stats[:, sc + 6:sc + 7], stats[:, sc + 5:sc + 6], AF.Sqrt,
            bias=cbias[:, 0:1]),
            waits=[e6] + ([ev_cbias] if tc == 0 else []), ev=True)
        e8 = op("vector", lambda e, s, sc=sc: e.reciprocal(
            stats[:, sc + 7:sc + 8], stats[:, sc + 6:sc + 7]),
            waits=[e7], ev=True)
        e9 = op("vector", lambda e, s, sc=sc: e.scalar_tensor_tensor(
            stats[:, sc + 4:sc + 5], stats[:, sc + 1:sc + 2], 1.0,
            stats[:, sc + 7:sc + 8], ALU.bypass, ALU.mult),
            waits=[e8], ev=True)
        ev_xn_act[tc] = op("vector", lambda e, s, t=t, sc=sc: e.tensor_scalar(
            xcb[:, 0:D], t[:, :], stats[:, sc + 7:sc + 8],
            stats[:, sc + 4:sc + 5], ALU.mult, ALU.add),
            waits=[e9] + ([ev_trans[tc - 1]] if tc >= 1 else []), ev=True)
        for i in range(16):
            ev_trans[tc] = op("tensor", lambda e, s, i=i: e.transpose(
                PAb[:, 128 * i:128 * (i + 1)], xcb[:, 128 * i:128 * (i + 1)],
                ident_b[:, :]),
                waits=([ev_ident, ev_xn_act[tc]] if i == 0 else []) +
                      ([ev_xn_evac[tc - 1]] if (i == 0 and tc >= 1) else []),
                ev=(i == 15))
        ev_xn_evac[tc] = op("vector", lambda e, s, tc=tc: e.tensor_copy(
            xnfm[:, :].rearrange("p (i t) -> p i t", i=16)[:, :, 128 * tc:128 * (tc + 1)],
            PAb[:, :].rearrange("p (i t) -> p i t", i=16)),
            waits=[ev_trans[tc]], ev=True)
    last_xn_evac = ev_xn_evac[7]

    if DEBUG:
        dma("qA", dbg_xn, xnfm[:, :], waits=[last_xn_evac])

    wqs = [wqs0, wqs1]
    ev_qev = [None] * 16
    for oc in range(16):
        ev_wq = dma("qG", wqs[oc % 2][:, :].rearrange("p (i o) -> p i o", i=16),
                    dwqT[:, 128 * oc:128 * (oc + 1)]
                    .rearrange("(i p) o -> p i o", p=128),
                    waits=[ev_qev[oc - 2]] if oc >= 2 else [])
        for th in range(2):
            mm_ev = None
            for ic in range(16):
                mm_ev = op("tensor", lambda e, s, oc=oc, th=th, ic=ic: e.matmul(
                    PB[:, 512 * th:512 * (th + 1)],
                    wqs[oc % 2][:, 128 * ic:128 * (ic + 1)],
                    xnfm[:, 1024 * ic + 512 * th:1024 * ic + 512 * (th + 1)],
                    start=(ic == 0), stop=(ic == 15), skip_group_check=True),
                    waits=([ev_wq, last_xn_evac] if (th == 0 and ic == 0) else []) +
                          ([ev_qev[oc - 1]] if (oc >= 1 and th == 0 and ic == 0) else []),
                    ev=(ic == 15))
            ev_qev[oc] = op("scalar", lambda e, s, oc=oc, th=th: e.activation(
                qfm[:, 1024 * oc + 512 * th:1024 * oc + 512 * (th + 1)],
                PB[:, 512 * th:512 * (th + 1)], AF.Copy, scale=0.125),
                waits=[mm_ev], ev=True)
    ev_kev = [None] * 4
    ev_kst = [None] * 4
    for oc in range(4):
        ev_wk = dma("qG", wqs[oc % 2][:, :].rearrange("p (i o) -> p i o", i=16),
                    dwkT[:, 128 * oc:128 * (oc + 1)]
                    .rearrange("(i p) o -> p i o", p=128),
                    waits=[ev_qev[14 + oc]] if oc < 2 else [ev_kev[oc - 2]])
        for th in range(2):
            mm_ev = None
            for ic in range(16):
                mm_ev = op("tensor", lambda e, s, oc=oc, th=th, ic=ic: e.matmul(
                    PB[:, 512 * th:512 * (th + 1)],
                    wqs[oc % 2][:, 128 * ic:128 * (ic + 1)],
                    xnfm[:, 1024 * ic + 512 * th:1024 * ic + 512 * (th + 1)],
                    start=(ic == 0), stop=(ic == 15), skip_group_check=True),
                    waits=([ev_wk] if (th == 0 and ic == 0) else []) +
                          ([ev_qev[15] if oc == 0 else ev_kev[oc - 1]]
                           if (th == 0 and ic == 0) else []),
                    ev=(ic == 15))
            ev_kev[oc] = op("scalar", lambda e, s, oc=oc, th=th: e.activation(
                kfm[:, 1024 * oc + 512 * th:1024 * oc + 512 * (th + 1)],
                PB[:, 512 * th:512 * (th + 1)], AF.Copy), waits=[mm_ev], ev=True)
        ev_kst[oc] = dma("qA", kst_d[128 * oc:128 * (oc + 1), :],
                         kfm[:, 1024 * oc:1024 * (oc + 1)], waits=[ev_kev[oc]])
    ev_wv = dma("qG", wvsb[:, :].rearrange("p (i v) -> p i v", i=16),
                dwvT.rearrange("(i p) v -> p i v", p=128),
                waits=[ev_qev[15]])
    ev_vst = {}
    ev_vev = [None] * 8
    for tc in range(8):
        mm_ev = None
        for ic in range(16):
            mm_ev = op("tensor", lambda e, s, tc=tc, ic=ic: e.matmul(
                PC[:, 0:512],
                xnfm[:, 1024 * ic + 128 * tc:1024 * ic + 128 * (tc + 1)],
                wvsb[:, 512 * ic:512 * (ic + 1)],
                start=(ic == 0), stop=(ic == 15), skip_group_check=True),
                waits=([ev_wv] if (tc == 0 and ic == 0) else []) +
                      ([ev_vev[tc - 1]] if (tc >= 1 and ic == 0) else []),
                ev=(ic == 15))
        ev_vev[tc] = op("vector", lambda e, s, tc=tc: e.tensor_copy(
            vtmb[:, 512 * (tc % 2):512 * (tc % 2 + 1)], PC[:, 0:512]),
            waits=[mm_ev] + ([ev_vst[(tc - 2, 3)]] if tc >= 2 else []), ev=True)
        for vh in range(4):
            ev_vst[(tc, vh)] = dma(
                "qA", vst_d[1024 * vh + 128 * tc:1024 * vh + 128 * (tc + 1), :],
                vtmb[:, 512 * (tc % 2) + 128 * vh:512 * (tc % 2) + 128 * (vh + 1)],
                waits=[ev_vev[tc]] if vh == 0 else [])
    ev_vst_last = ev_vst[(7, 3)]


    ev_knorm = None
    rm_hist = []
    for oc in range(4):
        esq = op("scalar", lambda e, s, oc=oc: e.activation(
            sqsb[:, 0:1024], kfm[:, 1024 * oc:1024 * (oc + 1)], AF.Square),
            waits=([ev_kev[oc], ev_vst_last] if oc == 0 else [ev_kev[oc]]),
            ev=True)
        for half in range(2):
            kh = 2 * oc + half
            NB = (PD, PC)[len(rm_hist) % 2]
            w1 = [esq] + ([rm_hist[-2]] if len(rm_hist) >= 2 else [])
            if len(rm_hist) == 1:
                w1.append(ev_vev[7])
            op("tensor", lambda e, s, half=half, NB=NB: e.matmul(
                NB[0:1, 0:512], ones_b[64 * half:64 * half + 64, 0:1],
                sqsb[64 * half:64 * half + 64, 0:512],
                start=True, stop=True, skip_group_check=True), waits=w1)
            m2 = op("tensor", lambda e, s, half=half, NB=NB: e.matmul(
                NB[0:1, 512:1024], ones_b[64 * half:64 * half + 64, 0:1],
                sqsb[64 * half:64 * half + 64, 512:1024],
                start=True, stop=True, skip_group_check=True), ev=True)
            ev_knorm = op("vector", lambda e, s, kh=kh, NB=NB: e.reduce_max(
                nmk[0:1, kh:kh + 1], NB[0:1, 0:1024], AX.X),
                waits=[m2], ev=True)
            rm_hist.append(ev_knorm)
    ev_knst = dma("qA", knst_d.ap(), nmk[0:1, :], waits=[ev_knorm])
    ev_qnorm = None
    for oc in range(16):
        esq = op("scalar", lambda e, s, oc=oc: e.activation(
            sqsb[:, 0:1024], qfm[:, 1024 * oc:1024 * (oc + 1)], AF.Square),
            waits=([ev_knorm] if oc == 0 else []), ev=True)
        for half in range(2):
            h = 2 * oc + half
            NB = (PD, PC)[len(rm_hist) % 2]
            w1 = [esq] + ([rm_hist[-2]] if len(rm_hist) >= 2 else [])
            op("tensor", lambda e, s, half=half, NB=NB: e.matmul(
                NB[0:1, 0:512], ones_b[64 * half:64 * half + 64, 0:1],
                sqsb[64 * half:64 * half + 64, 0:512],
                start=True, stop=True, skip_group_check=True), waits=w1)
            m2 = op("tensor", lambda e, s, half=half, NB=NB: e.matmul(
                NB[0:1, 512:1024], ones_b[64 * half:64 * half + 64, 0:1],
                sqsb[64 * half:64 * half + 64, 512:1024],
                start=True, stop=True, skip_group_check=True), ev=True)
            ev_qnorm = op("vector", lambda e, s, h=h, NB=NB: e.reduce_max(
                nmq[0:1, h:h + 1], NB[0:1, 0:1024], AX.X),
                waits=[m2], ev=True)
            rm_hist.append(ev_qnorm)

    rg = [list(range(NCORES))]
    p.emit("gpsimd", lambda e, s: e.collective_compute(
        "AllGather", ALU.bypass, replica_groups=rg,
        ins=[kst_d.ap().opt()], outs=[kall_d.ap().opt()]),
        waits=[ev_kst[3], ev_vst_last], inc="cc")
    p.emit("gpsimd", lambda e, s: e.collective_compute(
        "AllGather", ALU.bypass, replica_groups=rg,
        ins=[vst_d.ap().opt()], outs=[vall_d.ap().opt()]),
        waits=[ev_vst_last], inc="cc")
    p.emit("gpsimd", lambda e, s: e.collective_compute(
        "AllReduce", ALU.max, replica_groups=rg,
        ins=[knst_d.ap().opt()], outs=[knall_d.ap().opt()]),
        waits=[ev_knst], inc="cc")

    if DEBUG:
        dma("qA", dbg_qfm, qfm[:, :], waits=[ev_qev[15]])
        dma("qA", dbg_kall, kall_d.ap(), waits=[("cc", 3)])
        dma("qA", dbg_vall, vall_d.ap(), waits=[("cc", 3)])

    def emit_mlambda():
        ev_knall = dma("qA", nmk[0:1, :], knall_d.ap(), waits=[("cc", 3)])
        e_sq1 = op("scalar", lambda e, s: e.activation(
            nmk[0:1, :], nmk[0:1, :], AF.Sqrt), waits=[ev_knall], ev=True)
        e_sq2 = op("scalar", lambda e, s: e.activation(
            nmq[0:1, :], nmq[0:1, :], AF.Sqrt), waits=[ev_qnorm], ev=True)
        e_m = None
        for h in range(32):
            e_m = op("vector", lambda e, s, h=h: e.tensor_mul(
                mrow[0:1, h:h + 1], nmq[0:1, h:h + 1],
                nmk[0:1, h // 4:h // 4 + 1]),
                waits=([e_sq1, e_sq2] if h == 0 else []), ev=(h == 31))
        e_mp = op("vector", lambda e, s: e.tensor_tensor(
            bcrow[0:1, 0:16],
            mrow[0:1, :].rearrange("a (h two) -> a h two", two=2)[:, :, 0],
            mrow[0:1, :].rearrange("a (h two) -> a h two", two=2)[:, :, 1],
            ALU.max), waits=[e_m], ev=True)
        e_neg = op("vector", lambda e, s: e.tensor_scalar(
            bcrow[0:1, 0:16], bcrow[0:1, 0:16], -1.0, -1.0,
            ALU.mult, ALU.add), waits=[e_mp], ev=True)
        e_l1 = op("vector", lambda e, s: e.tensor_mul(
            lscr[0:1, 0:64], lam4[0:1, 0:64], lam4[0:1, 64:128]),
            waits=[ev_consts], ev=True)
        e_s1 = op("vector", lambda e, s: e.reduce_sum(
            lscr[0:1, 64:65], lscr[0:1, 0:64], AX.X), waits=[e_l1], ev=True)
        e_e1 = op("scalar", lambda e, s: e.activation(
            lscr[0:1, 66:67], lscr[0:1, 64:65], AF.Exp), waits=[e_s1], ev=True)
        e_l2 = op("vector", lambda e, s: e.tensor_mul(
            lscr[0:1, 0:64], lam4[0:1, 128:192], lam4[0:1, 192:256]),
            waits=[e_e1], ev=True)
        e_s2 = op("vector", lambda e, s: e.reduce_sum(
            lscr[0:1, 65:66], lscr[0:1, 0:64], AX.X), waits=[e_l2], ev=True)
        e_e2 = op("scalar", lambda e, s: e.activation(
            lscr[0:1, 68:69], lscr[0:1, 65:66], AF.Exp), waits=[e_s2], ev=True)
        e_lf = op("vector", lambda e, s: e.tensor_sub(
            lscr[0:1, 70:71], lscr[0:1, 66:67], lscr[0:1, 68:69]),
            waits=[e_e2], ev=True)
        e_lf2 = op("vector", lambda e, s: e.tensor_scalar(
            bcrow[0:1, 16:17], lscr[0:1, 70:71], -1.0, -LAMBDA_INIT,
            ALU.mult, ALU.add), waits=[e_lf], ev=True)
        return e_neg, e_lf2

    def emit_down_prefetch(q):
        evs = {}
        for j in range(6):
            jp = (6, 7, 8, 9, 10, 5)[j]
            wd_w = [] if q == 0 else [ev_ddq[(q - 1, 3, jp)]]
            evs[j] = dma("qG", wdr[j][:, :],
                         dwd8[128 * (HCQ * q + j):128 * (HCQ * q + j + 1),
                              0:512], waits=wd_w)
        return evs

    down_pre = {}
    gw_pre = {}
    mlam_evs = None
    wgr = [wgr0, wgr1]
    wur = [wur0, wur1]
    wgd = [wgd0, wgd1]
    wud = [wud0, wud1]
    wdr = [wdr0, wdr1, wdr2, wdr3, wdr4, wdr5]
    wdd = [wdd0, wdd1, wdd2, wdd3, wdd4, wdd5]
    ev_mmj_prev = None
    ev_h = [None] * NHC
    ev_gdq = [None] * NHC
    ev_udq = [None] * NHC
    ev_ddq = {}
    ev_acc = {}
    last_xn_read = None
    ev_es_first = True
    for q in range(NQ):
        if q == 0:
            down_pre[0] = emit_down_prefetch(0)
        if q == 1:
            mlam_evs = emit_mlambda()
        for j in range(HCQ):
            H = HCQ * q + j
            pp = H % 2
            if (H) in gw_pre:
                ev_wg, ev_wu = gw_pre[H]
            else:
                wg_w = []
                if H == 0:
                    wg_w = [last_xn_evac, ev_vst_last]
                elif H >= 2:
                    wg_w = [ev_udq[H - 2]]
                ev_wg = dma("qG", wgr[pp][:, :], dwg8[H, :, :], waits=wg_w)
                ev_wu = dma("qG", wur[pp][:, :], dwu8[H, :, :])
            ev_gdq[H] = op("vector", lambda e, s, pp=pp: e.tensor_scalar(
                wgd[pp][:, :], wgr[pp][:, :], -128.0, None, ALU.add),
                waits=[ev_wg] + ([ev_h[H - 2]] if H >= 2 else []), ev=True)
            ev_udq[H] = op("vector", lambda e, s, pp=pp: e.tensor_scalar(
                wud[pp][:, :], wur[pp][:, :], -128.0, None, ALU.add),
                waits=[ev_wu], ev=True)
            for th in range(2):
                mw = []
                if th == 0:
                    mw = [ev_gdq[H], ev_udq[H]]
                    if j == 0:
                        mw.append(ev_qnorm if q == 0 else ev_acc[(q - 1, 3, 7)])
                        if DEBUG and q == 1:
                            mw.append(dbg_ffn_ev[0])
                    if H >= 1:
                        mw.append(ev_h[H - 1])
                mmg = mmu = None
                for ic in range(16):
                    mmg = op("tensor", lambda e, s, pp=pp, th=th, ic=ic: e.matmul(
                        PA[:, 512 * th:512 * (th + 1)],
                        wgd[pp][:, 128 * ic:128 * (ic + 1)],
                        xnfm[:, 1024 * ic + 512 * th:1024 * ic + 512 * (th + 1)],
                        start=(ic == 0), stop=(ic == 15), skip_group_check=True),
                        waits=(mw if ic == 0 else []), ev=(ic == 15))
                for ic in range(16):
                    mmu = op("tensor", lambda e, s, pp=pp, th=th, ic=ic: e.matmul(
                        PB[:, 512 * th:512 * (th + 1)],
                        wud[pp][:, 128 * ic:128 * (ic + 1)],
                        xnfm[:, 1024 * ic + 512 * th:1024 * ic + 512 * (th + 1)],
                        start=(ic == 0), stop=(ic == 15), skip_group_check=True),
                        ev=(ic == 15))
                if H == NHC - 1 and th == 1:
                    last_xn_read = mmu
                es_w = [mmg]
                if DEBUG and H == 0 and th == 0:
                    edw = op("vector", lambda e, s: e.tensor_copy(
                        ds_scr[:, :], PA[:, 0:512]), waits=[mmg], ev=True)
                    dma("qA", dbg_wgd, wgd0[:, :], waits=[ev_gdq[0]])
                    dbg_g_ev = dma("qA", dbg_gate, ds_scr[:, :], waits=[edw])
                    es_w.append(dbg_g_ev)
                    edw2 = op("vector", lambda e, s: e.tensor_copy(
                        ds_scr[:, :], PB[:, 0:512]), waits=[mmu, dbg_g_ev], ev=True)
                    dbg_u_ev = dma("qA", dbg_up, ds_scr[:, :], waits=[edw2])
                    es_w.append(dbg_u_ev)
                if ev_es_first:
                    es_w.append(ev_qev[15])
                    ev_es_first = False
                es = op("scalar", lambda e, s, pp=pp, th=th, H=H: e.activation(
                    sil_scr[:, 512 * th:512 * (th + 1)],
                    PA[:, 512 * th:512 * (th + 1)], AF.Silu,
                    scale=sgpp[:, H:H + 1], bias=cbias[:, 3:4]),
                    waits=es_w, ev=True)
                hw = [mmu, es]
                if DEBUG and H == 0 and th == 0:
                    hw.append(dma("qA", dbg_sil, sil_scr[:, 0:512], waits=[es]))
                ev_h[H] = op("vector",
                             lambda e, s, th=th, H=H, j=j: e.scalar_tensor_tensor(
                    hfm[:, 1024 * j + 512 * th:1024 * j + 512 * (th + 1)],
                    PB[:, 512 * th:512 * (th + 1)], supp[:, H:H + 1],
                    sil_scr[:, 512 * th:512 * (th + 1)],
                    ALU.mult, ALU.mult), waits=hw, ev=True)
        if q + 1 < NQ:
            for Hn in (HCQ * (q + 1), HCQ * (q + 1) + 1):
                gw_pre[Hn] = (dma("qG", wgr[Hn % 2][:, :], dwg8[Hn, :, :],
                                  waits=[ev_udq[Hn - 2]]),
                              dma("qG", wur[Hn % 2][:, :], dwu8[Hn, :, :]))
        ps_slices = [(PA, 0), (PA, 512), (PB, 0), (PB, 512),
                     (PC, 0), (PC, 512), (PD, 0), (PD, 512)]
        for dd in range(4):
            mm_last = [None] * 8
            ev_mmj = {}
            for j in range(HCQ):
                H = HCQ * q + j
                pp = j % 6
                jp = (6, 7, 8, 9, 10, 5)[pp]
                if dd == 0 and j < 6:
                    evd = down_pre[q][j]
                else:
                    if j >= 6:
                        wd_w = [ev_ddq[(q, dd, j - 6)]]
                    else:
                        wd_w = [ev_ddq[(q, dd - 1, jp)]]
                    evd = dma("qG", wdr[pp][:, :],
                              dwd8[128 * H:128 * (H + 1),
                                   512 * dd:512 * (dd + 1)],
                              waits=wd_w)
                dq_w = [evd]
                if j >= 6:
                    dq_w.append(ev_mmj[j - 6])
                elif ev_mmj_prev is not None:
                    dq_w.append(ev_mmj_prev[jp])
                ev_ddq[(q, dd, j)] = op(
                    "vector", lambda e, s, pp=pp: e.tensor_scalar(
                        wdd[pp][:, :], wdr[pp][:, :], -128.0, None, ALU.add),
                    waits=dq_w, ev=True)
                for tc in range(8):
                    ps, o = ps_slices[tc]
                    dmw = []
                    if tc == 0:
                        dmw = [ev_ddq[(q, dd, j)]]
                    if j == 0 and tc == 0 and dd == 0:
                        dmw += [ev_h[HCQ * q + HCQ - 1]]
                        if q == 0:
                            dmw += [ev_qnorm, ev_vev[7]]
                    if j == 0:
                        if dd >= 1:
                            dmw += [ev_acc[(q, dd - 1, tc)]]
                        elif q >= 1:
                            dmw += [ev_acc[(q - 1, 3, tc)]]
                    mm_last[tc] = op("tensor",
                                     lambda e, s, ps=ps, o=o, pp=pp, j=j, tc=tc: e.matmul(
                        ps[:, o:o + 512],
                        hfm[:, 1024 * j + 128 * tc:1024 * j + 128 * (tc + 1)],
                        wdd[pp][:, :],
                        start=(j == 0), stop=(j == HCQ - 1),
                        skip_group_check=True),
                        waits=dmw, ev=(j == HCQ - 1 or tc == 7))
                ev_mmj[j] = mm_last[7]
            ev_mmj_prev = ev_mmj
            for tc in range(8):
                ps, o = ps_slices[tc]
                if q == 0:
                    ev_acc[(q, dd, tc)] = op(
                        "vector", lambda e, s, ps=ps, o=o, tc=tc, dd=dd: e.tensor_mul(
                            acc[:, D * tc + 512 * dd:D * tc + 512 * (dd + 1)],
                            ps[:, o:o + 512], sdb[:, 512 * dd:512 * (dd + 1)]),
                        waits=[mm_last[tc]], ev=True)
                else:
                    e1d = op("vector", lambda e, s, ps=ps, o=o, dd=dd: e.tensor_mul(
                        ds_scr[:, :], ps[:, o:o + 512],
                        sdb[:, 512 * dd:512 * (dd + 1)]),
                        waits=[mm_last[tc]], ev=True)
                    ev_acc[(q, dd, tc)] = op(
                        "vector", lambda e, s, tc=tc, dd=dd: e.tensor_add(
                            acc[:, D * tc + 512 * dd:D * tc + 512 * (dd + 1)],
                            acc[:, D * tc + 512 * dd:D * tc + 512 * (dd + 1)],
                            ds_scr[:, :]), waits=[e1d], ev=True)
        if q + 1 < NQ:
            down_pre[q + 1] = emit_down_prefetch(q + 1)
        if DEBUG and q == 0:
            dbg_ffn_ev = [None]
            e1dmp = dma("qA", dbg_h0, hfm[:, :], waits=[ev_h[HCQ - 1]])
            dbg_ffn_ev[0] = dma("qA", dbg_acc0, acc[:, :],
                                waits=[ev_acc[(0, 3, 7)]])
    ffn_last = ev_acc[(NQ - 1, 3, 7)]
    ev_masks = dma("qA", masks[:, :], dmasks, waits=[ev_h[NHC - 1]])

    e_neg, e_lf2 = mlam_evs
    e_bc = op("tensor", lambda e, s: e.matmul(
        PD[:, 0:18], onescol_f[0:1, :], bcrow[0:1, 0:18],
        start=True, stop=True, skip_group_check=True),
        waits=[e_neg, e_lf2, ffn_last], ev=True)
    ev_negmpp = op("vector", lambda e, s: e.tensor_copy(
        negmpp[:, 0:18], PD[:, 0:18]), waits=[e_bc], ev=True)
    ev_lnws = op("scalar", lambda e, s: e.activation(
        lnw_s[:, 0:1], lnw[:, 0:1], AF.Copy, scale=(1.0 - LAMBDA_INIT)),
        waits=[ev_consts], ev=True)

    if DEBUG:
        dma("qA", dbg_acc, acc[:, :], waits=[ffn_last])
        dma("qA", dbg_negm, negmpp[:, 0:18], waits=[ev_negmpp])

    PV = PC
    LL = PD

    def mk_frozen2(name, colbase, mx):
        st = {"r": None, "rvs": [None] * NITEMS}

        def get(e, i):
            if st["r"] is None:
                st["r"] = e.alloc_register(name)
            if st["rvs"][i] is None:
                e.reg_load(st["r"], itab[0:1, colbase + i:colbase + i + 1])
                st["rvs"][i] = bass.RuntimeValue(
                    st["r"], min_val=0, max_val=mx + i)
            return st["rvs"][i]
        return get

    ve_rv = mk_frozen2("koff", 0, 7680)
    sy_rv = mk_frozen2("voff", NITEMS, (8 * TOK - 512) * 128)
    pe_state = {"r": [None, None]}

    def pe_flag(e, i):
        if pe_state["r"][0] is None:
            pe_state["r"][0] = e.alloc_register("qflag0")
            pe_state["r"][1] = e.alloc_register("qflag1")
        r = pe_state["r"][i % 2]
        e.reg_load(r, itab[0:1, 2 * NITEMS + i:2 * NITEMS + i + 1])
        return r

    def pe_flag_cached(e, j):
        return pe_state["r"][j % 2]

    dbg_kv_ev = [None]
    ev_kcache = ev_vcache = None
    ev_kv = {}
    ev_vv = {}
    ev_stq = {}
    ev_stqA = {}
    ev_exp = {}
    ev_expA = {}
    ev_msk = {}
    ev_pv = {}
    ev_evO = {}
    ev_epi_done = None
    ev_epi_ps = None
    ev_epi_act = None
    ev_epi_tr = None
    er1_ev = None
    for h in range(NH):
        d = h // 2
        kh = h // 4
        vh = h // 8
        isA = (h % 2 == 0)
        rb = 64 * (h % 2)
        if h % 4 == 0:
            kc_w = ([last_xn_read, ("cc", 3)] if h == 0
                    else [ev_kv[(h - 1, NITEMS - 1)]])
            for r in range(NCORES):
                ev_kcache = dma("qS", ksb[:, 1024 * r:1024 * (r + 1)],
                                kall_d.ap()[512 * r + 64 * kh:512 * r + 64 * (kh + 1), :],
                                waits=(kc_w if r == 0 else []))
        if h % 8 == 0:
            vc_w = [("cc", 3)] if h == 0 else [ev_vv[(h - 1, NITEMS - 1)]]
            for r in range(NCORES):
                ev_vcache = dma("qG",
                                vcur_d.ap()[1024 * r:1024 * (r + 1), :],
                                vall_d.ap()[4096 * r + 1024 * vh:4096 * r + 1024 * (vh + 1), :],
                                waits=(vc_w if r == 0 else []))
        if h == 0:
            stw0 = [ffn_last]
            pvw0 = [ev_negmpp]
        else:
            stw0 = [ev_exp[(h - 1, NITEMS - 1)]]
            pvw0 = [ev_evO[(h - 1, "o")], ev_evO[(h - 1, "l")]]
        if ev_epi_done is not None:
            pvw0.append(ev_epi_done)

        def pvl_body(e, s, ev, ssl, vbuf, vo, qa, st):
            for c in range(4):
                e.matmul(PV[:, qa:qa + 512],
                         vbuf[:, vo + 128 * c:vo + 128 * (c + 1)],
                         pt[:, ssl + 512 * c:ssl + 512 * (c + 1)],
                         start=(st and c == 0), stop=False,
                         skip_group_check=True)
            for c in range(4):
                mm = e.matmul(LL[0:64, qa:qa + 512], ones_b[:, 0:64],
                              pt[:, ssl + 512 * c:ssl + 512 * (c + 1)],
                              start=(st and c == 0), stop=False,
                              skip_group_check=True)
            mm.then_inc(s[ev[0]], 1)

        def pvl(e, s, h, j, ev):
            ssl = 2048 * (j % 2)
            vbuf = vstat if (j % 4) < 2 else vstat2
            vo = 512 * (j % 2)
            if j < 2:
                pvl_body(e, s, ev, ssl, vbuf, vo, 512 * j, True)
                return
            rf = pe_flag_cached(e, j)
            for cond in (0, 1):
                qa = 512 * cond
                cm = e.If_eq(rf, 0) if cond == 0 else e.Else()
                with cm:
                    pvl_body(e, s, ev, ssl, vbuf, vo, qa, False)
            e.end_ifs()

        for i in range(NITEMS):
            sl = i % 2
            dwk = []
            if i == 0:
                dwk = [ev_kcache, ev_consts]
            if i < 2 and h > 0:
                dwk = dwk + [ev_stq[(h - 1, NITEMS - 2 + i)]]
            if i >= 2:
                dwk = [ev_stq[(h, i - 2)]]

            def kcopy(e, s, i=i, sl=sl, rb=rb):
                v = ve_rv(e, i)
                return e.tensor_copy(kstat[rb:rb + 64, 512 * sl:512 * (sl + 1)],
                                     ksb[:, bass.ds(v, 512)])
            if DEBUG and h in (0, 16) and i == 2:
                tgt = dbg_kv if h == 0 else dbg_kv16
                dma("qA", tgt[:, 0:1024], kstat[:, :],
                    waits=[ev_kv[(h, 1)], ev_vv[(h, 1)]])
                dbg_kv_ev[0] = dma("qA", tgt[:, 1024:2048], vstat[:, :])
                dwk = dwk + [dbg_kv_ev[0]]
            ev_kv[(h, i)] = op("vector", kcopy, waits=dwk, ev=True)

            if 0 <= i - 1 < 2:
                ev_msk[(h, i - 1)] = op(
                    "vector", lambda e, s, psl=(i - 1) % 2: e.tensor_mul(
                        pt[:, 2048 * psl:2048 * (psl + 1)],
                        pt[:, 2048 * psl:2048 * (psl + 1)], masks[:, 0:2048]),
                    waits=[ev_exp[(h, i - 1)]] +
                          ([ev_masks] if (h == 0 and i == 1) else []), ev=True)

            if i == 1 and h % 2 == 1:
                w1 = [ev_evO[(h - 1, "l")]]
                if ev_epi_tr is not None:
                    w1.append(ev_epi_tr)
                er1_ev = op("vector", lambda e, s: e.reciprocal(
                    ra_sb[0:1, :], la_sb[0:1, :]), waits=w1, ev=True)

            dwv = []
            if i == 0 and h == 0:
                dwv = [ev_consts]
            if i == 0 and h % 8 == 0:
                dwv = dwv + [ev_vcache]
            if i < 4 and h > 0:
                dwv = dwv + [ev_pv[(h - 1, (16, 13, 14, 15)[i])]]
            if i < 4 and h == 0:
                dwv = dwv + [ffn_last]
            if i >= 4:
                dwv = [ev_pv[(h, i - 4)]]
            if DEBUG and h in (0, 16) and i == 2:
                dwv = dwv + [dbg_kv_ev[0]]

            def vcopy(e, s, i=i, sl=sl):
                v = sy_rv(e, i)
                vbuf = vstat if (i % 4) < 2 else vstat2
                src = bass.AP(vcur_d, v, [[128, 128], [16384, 4], [1, 128]])
                return e.dma_start(
                    out=vbuf[:, 512 * sl:512 * (sl + 1)]
                    .rearrange("p (c v) -> p c v", c=4), in_=src)
            ev_vv[(h, i)] = p.emit("gpsimd", vcopy, waits=dwv, inc="qG",
                                   inc_by=16)

            stw = [ev_kv[(h, i)]]
            stwB = []
            if i == 0:
                stw += stw0
            if i >= 1:
                stw.append(ev_expA[(h, i - 1)])
                stwB.append(ev_exp[(h, i - 1)])
            ev_stqA[(h, i)] = p.bump("sPE")

            def stmmA(e, s, h=h, i=i, sl=sl, rb=rb, ev=ev_stqA[(h, i)]):
                base = 1024 * (h // 2)
                tp = (rb, 0) if rb else None
                rf = pe_flag(e, i)
                for cond in (0, 1):
                    qa = base + 512 * cond
                    cm = e.If_eq(rf, 0) if cond == 0 else e.Else()
                    with cm:
                        rhs = qfm[rb:rb + 64, qa:qa + 512]
                        for c in range(2):
                            mm = e.matmul(
                                PST[:, 512 * c:512 * (c + 1)],
                                kstat[rb:rb + 64,
                                      512 * sl + 128 * c:512 * sl + 128 * (c + 1)],
                                rhs, start=True, stop=True,
                                skip_group_check=True, tile_position=tp)
                        mm.then_inc(s[ev[0]], 1)
                e.end_ifs()
            p.emit("tensor", stmmA, waits=stw)
            ev_stq[(h, i)] = p.bump("sPE")

            def stmmB(e, s, h=h, i=i, sl=sl, rb=rb, ev=ev_stq[(h, i)]):
                base = 1024 * (h // 2)
                tp = (rb, 0) if rb else None
                rf = pe_flag_cached(e, i)
                for cond in (0, 1):
                    qa = base + 512 * cond
                    cm = e.If_eq(rf, 0) if cond == 0 else e.Else()
                    with cm:
                        rhs = qfm[rb:rb + 64, qa:qa + 512]
                        for c in range(2, 4):
                            mm = e.matmul(
                                PST[:, 512 * c:512 * (c + 1)],
                                kstat[rb:rb + 64,
                                      512 * sl + 128 * c:512 * sl + 128 * (c + 1)],
                                rhs, start=True, stop=True,
                                skip_group_check=True, tile_position=tp)
                        mm.then_inc(s[ev[0]], 1)
                e.end_ifs()
            p.emit("tensor", stmmB, waits=stwB)

            if i >= 1:
                ev_pv[(h, i - 1)] = p.bump("sPE")
                p.emit("tensor",
                       lambda e, s, h=h, j=i - 1, ev=ev_pv[(h, i - 1)]: pvl(
                           e, s, h, j, ev),
                       waits=[ev_vv[(h, i - 1)]] +
                             ([ev_msk[(h, i - 1)]] + pvw0
                              if i - 1 < 2 else []))

            ew = [ev_stqA[(h, i)]]
            if i >= 2:
                ew.append(ev_pv[(h, i - 2)])
            elif h > 0:
                ew.append(ev_pv[(h - 1, NITEMS - 2 + i)])
            ev_expA[(h, i)] = op("scalar", lambda e, s, sl=sl, d=d: e.activation(
                pt[:, 2048 * sl:2048 * sl + 1024], PST[:, 0:1024],
                AF.Exp, bias=negmpp[:, d:d + 1]), waits=ew, ev=True)
            ev_exp[(h, i)] = op("scalar", lambda e, s, sl=sl, d=d: e.activation(
                pt[:, 2048 * sl + 1024:2048 * sl + 2048], PST[:, 1024:2048],
                AF.Exp, bias=negmpp[:, d:d + 1]),
                waits=[ev_stq[(h, i)]], ev=True)
        ev_pv[(h, NITEMS - 1)] = p.bump("sPE")
        p.emit("tensor",
               lambda e, s, h=h, j=NITEMS - 1, ev=ev_pv[(h, NITEMS - 1)]: pvl(
                   e, s, h, j, ev),
               waits=[ev_exp[(h, NITEMS - 1)], ev_vv[(h, NITEMS - 1)]])

        odst = o1s if isA else o2s
        ldst = la_sb if isA else lb_sb
        evo_w = [ev_pv[(h, NITEMS - 1)]]
        if h == 0:
            evo_w.append(ffn_last)
        ev_evO[(h, "o")] = op("vector", lambda e, s, odst=odst: e.tensor_copy(
            odst[:, :], PV[:, 0:1024]), waits=evo_w, ev=True)
        ev_evO[(h, "l")] = op("vector", lambda e, s, ldst=ldst: e.tensor_copy(
            ldst[0:1, :], LL[0:1, 0:1024]), ev=True)
        if DEBUG and h == 1:
            dma("qA", dbg_l0[0:1, :], la_sb[0:1, :], waits=[ev_evO[(h, "l")]])
            dma("qA", dbg_l0[1:2, :], lb_sb[0:1, :])
            dma("qA", dbg_o0[:, 0:1024], o1s[:, :])
            dma("qA", dbg_o0[:, 1024:2048], o2s[:, :])

        if not isA:
            er_w = [ev_evO[(h, "l")]]
            if ev_epi_tr is not None:
                er_w.append(ev_epi_tr)
            er2a = op("vector", lambda e, s: e.reciprocal(
                rb_sb[0:1, 0:512], lb_sb[0:1, 0:512]), waits=er_w, ev=True)
            er2b = op("vector", lambda e, s: e.reciprocal(
                rb_sb[0:1, 512:1024], lb_sb[0:1, 512:1024]), ev=True)
            to_ev = None
            for qq in range(8):
                to_ev = op("tensor", lambda e, s, qq=qq: e.transpose(
                    PCb[:, 128 * qq:128 * (qq + 1)],
                    o1s[:, 128 * qq:128 * (qq + 1)], ident_b[:, :]),
                    waits=([ev_evO[(h - 1, "o")], ev_evO[(h, "o")]]
                           if qq == 0 else []))
            for qq in range(8):
                to_ev = op("tensor", lambda e, s, qq=qq: e.transpose(
                    PCb[:, 1024 + 128 * qq:1024 + 128 * (qq + 1)],
                    o2s[:, 128 * qq:128 * (qq + 1)], ident_b[:, :]),
                    ev=(qq == 7))
            tr_ev = None
            for qq in range(8):
                tr_ev = op("tensor", lambda e, s, qq=qq: e.transpose(
                    PD[:, qq:qq + 1], ra_sb[0:1, 128 * qq:128 * (qq + 1)],
                    onescol_f[0:1, 0:1]),
                    waits=([er1_ev, ev_evO[(h, "l")]] if qq == 0 else []))
            for qq in range(8):
                tr_ev = op("tensor", lambda e, s, qq=qq: e.transpose(
                    PD[:, 8 + qq:8 + qq + 1],
                    rb_sb[0:1, 128 * qq:128 * (qq + 1)], onescol_f[0:1, 0:1]),
                    waits=([er2a] if qq == 0 else ([er2b] if qq == 4 else [])),
                    ev=(qq == 7))
            ev_epi_tr = tr_ev
            e_rpp = op("vector", lambda e, s: e.tensor_copy(
                rpp[:, :], PD[:, 0:16]), waits=[tr_ev], ev=True)
            e_rcp = e_rpp
            e_rbn = op("vector", lambda e, s: e.tensor_scalar(
                rbn[:, :], rpp[:, 8:16], negmpp[:, 16:17], None, ALU.mult),
                ev=True)
            e_oda = [None] * 8
            for qq in range(8):
                e_oda[qq] = op("vector", lambda e, s, qq=qq: e.tensor_scalar(
                    scr1[:, 128 * qq:128 * (qq + 1)],
                    PCb[:, 128 * qq:128 * (qq + 1)], rpp[:, qq:qq + 1], None,
                    ALU.mult),
                    waits=([to_ev, e_rcp] if qq == 0 else []), ev=True)
            e_od = [None] * 8
            for qq in range(8):
                e_od[qq] = op("vector", lambda e, s, qq=qq: e.scalar_tensor_tensor(
                    odb[:, 128 * qq:128 * (qq + 1)],
                    PCb[:, 1024 + 128 * qq:1024 + 128 * (qq + 1)], rbn[:, qq:qq + 1],
                    scr1[:, 128 * qq:128 * (qq + 1)], ALU.mult, ALU.add),
                    waits=[e_oda[qq]] + ([e_rbn] if qq == 0 else []), ev=True)
            e_ssq = None
            for qq in range(8):
                e_ssq = op("vector", lambda e, s, qq=qq: e.scalar_tensor_tensor(
                    scr1[:, 128 * qq:128 * (qq + 1)],
                    odb[:, 128 * qq:128 * (qq + 1)], 1.0,
                    odb[:, 128 * qq:128 * (qq + 1)], ALU.bypass, ALU.mult,
                    accum_out=ssq8[:, qq:qq + 1]),
                    waits=[e_od[qq]], ev=(qq == 7))
            e_std = op("scalar", lambda e, s: e.activation(
                ssq8[:, :], ssq8[:, :], AF.Sqrt, scale=1.0 / 128,
                bias=cbias[:, 0:1]), waits=[e_ssq], ev=True)
            e_rst = op("vector", lambda e, s: e.reciprocal(
                rst8[:, :], ssq8[:, :]), waits=[e_std], ev=True)
            e_on = None
            for qq in range(8):
                e_on = op("vector", lambda e, s, qq=qq: e.tensor_scalar(
                    onb[:, 128 * qq:128 * (qq + 1)],
                    odb[:, 128 * qq:128 * (qq + 1)], rst8[:, qq:qq + 1], None,
                    ALU.mult),
                    waits=([e_rst] if qq == 0 else []), ev=(qq == 7))
            ev_epi_act = e_on
            tb_ev = None
            for qq in range(8):
                tb_ev = op("tensor", lambda e, s, qq=qq: e.transpose(
                    PDb[:, 1024 + 128 * qq:1024 + 128 * (qq + 1)],
                    onb[:, 128 * qq:128 * (qq + 1)], ident_b[:, :]),
                    waits=([e_on] if qq == 0 else []), ev=(qq == 7))
            ev_epi_ps = tb_ev
            ev_epi_done = op("vector", lambda e, s, d=d: e.tensor_scalar(
                ofm[:, 1024 * d:1024 * (d + 1)], PDb[:, 1024:2048],
                lnw_s[:, 0:1], None, ALU.mult),
                waits=[tb_ev] + ([ev_lnws] if d == 0 else []), ev=True)

    attn_done = ev_epi_done
    attn_kv_done = ev_kv[(NH - 1, NITEMS - 1)]
    if DEBUG:
        dma("qA", dbg_accpre, acc[:, :], waits=[attn_done])
        dma("qA", dbg_ofm, ofm[:, :], waits=[attn_done])
        dma("qA", dbg_l[0:1, :], la_sb[0:1, :], waits=[attn_done])
        dma("qA", dbg_l[1:2, :], lb_sb[0:1, :], waits=[attn_done])

    ev_stq_last = ev_stq[(NH - 1, NITEMS - 1)]
    ev_wo_q = {}
    for ic in range(16):
        q3 = ("qG", "qS", "qA")[ic % 3]
        w = [ev_stq_last, attn_kv_done] if ic < 3 else []
        ev_wo_q[q3] = dma(q3, wo_all[:, 2048 * ic:2048 * (ic + 1)],
                          dwoT[128 * ic:128 * (ic + 1), :], waits=w,
                          fifo=False)
    ev_wo_gate = list(ev_wo_q.values())
    ev_acc2 = {}
    oc_bank = {0: (PA, 0), 1: (PB, 0), 2: (PC, 0), 3: (PC, 512)}
    oc_order = (2, 3, 0, 1)
    for tc in range(8):
        for j_oc, oc in enumerate(oc_order):
            ps, o = oc_bank[oc]
            mm = None
            for ic in range(16):
                mw = []
                if ic == 0:
                    if tc == 0 and j_oc == 0:
                        mw = ev_wo_gate + [attn_done]
                    if tc >= 1:
                        mw.append(ev_acc2[(tc - 1, oc)])
                mm = op("tensor", lambda e, s, ps=ps, o=o, ic=ic, tc=tc, oc=oc: e.matmul(
                    ps[:, o:o + 512],
                    ofm[:, 1024 * ic + 128 * tc:1024 * ic + 128 * (tc + 1)],
                    wo_all[:, 2048 * ic + 512 * oc:2048 * ic + 512 * (oc + 1)],
                    start=(ic == 0), stop=(ic == 15), skip_group_check=True),
                    waits=mw, ev=(ic == 15))
            ev_acc2[(tc, oc)] = op(
                "vector", lambda e, s, ps=ps, o=o, tc=tc, oc=oc: e.tensor_add(
                    acc[:, D * tc + 512 * oc:D * tc + 512 * (oc + 1)],
                    acc[:, D * tc + 512 * oc:D * tc + 512 * (oc + 1)],
                    ps[:, o:o + 512]), waits=[mm], ev=True)

    if DEBUG:
        dma("qA", dbg_wo, wo_all[:, :], waits=ev_wo_gate)
        dma("qA", dbg_accw, acc[:, :], waits=[ev_acc2[(7, 3)]])

    outb = [out0, out1]
    ev_store = [None] * 8
    ev_add = [None] * 8
    for tc in range(8):
        w = [ev_acc2[(tc, oc)] for oc in range(4)]
        exld = dma("qA", xs[tc % 2][:, :], dx[128 * tc:128 * (tc + 1), :],
                   waits=w + ([ev_add[tc - 2]] if tc >= 2 else []))
        ev_add[tc] = op("vector", lambda e, s, tc=tc: e.tensor_add(
            outb[tc % 2][:, :], acc[:, D * tc:D * (tc + 1)], xs[tc % 2][:, :]),
            waits=[exld] + ([ev_store[tc - 2]] if tc >= 2 else []), ev=True)
        ev_store[tc] = dma("qS", dy[128 * tc:128 * (tc + 1), :],
                           outb[tc % 2][:, :], waits=[ev_add[tc]])
    final_qA = p.cnt["qA"]
    final_qS = p.cnt["qS"]
    p.emit("scalar", lambda e, s: e.nop(),
           waits=[("qA", final_qA), ("qS", final_qS)])

    with ExitStack() as ctx:
        sems = {n: ctx.enter_context(nc.semaphore(n)) for n in sem_names}
        with nc.Block() as block:
            @block.sync
            def _(eng):
                p.replay("sync", eng, sems)

            @block.gpsimd
            def _(eng):
                p.replay("gpsimd", eng, sems)

            @block.vector
            def _(eng):
                p.replay("vector", eng, sems)

            @block.scalar
            def _(eng):
                p.replay("scalar", eng, sems)

            @block.tensor
            def _(eng):
                p.replay("tensor", eng, sems)

    return nc


def _koff(b):
    r = b if b < 8 else 15 - b
    return 1024 * r + (512 if b >= 8 else 0)


def _item_table(core):
    bA, bB = core, 15 - core
    ko = [_koff(bA), _koff(bB)]
    qo = [0, 512]
    for t in range(bA):
        ko.append(_koff(t))
        qo.append(0)
    for t in range(bB):
        ko.append(_koff(t))
        qo.append(512)
    assert len(ko) == NITEMS
    vo = [128 * k for k in ko]
    return np.array([ko + vo + qo], dtype=np.int32)


def _masks():
    m = np.zeros((128, 2048), dtype=np.float32)
    qf = np.arange(512)[None, :]
    kp = np.arange(128)[:, None]
    for c in range(4):
        allow = (qf - kp) >= 128 * c
        m[:, 512 * c:512 * (c + 1)] = np.where(allow, 1.0, 0.0)
    return np.ascontiguousarray(m.astype(BF))


_NC_CACHE = {}


def kernel(L0_attn_x, L0_attn_Wq, L0_attn_Wk, L0_attn_Wv, L0_attn_Wo,
           L0_attn_lambda_q1, L0_attn_lambda_k1, L0_attn_lambda_q2,
           L0_attn_lambda_k2, L0_attn_ln_w,
           L0_ffn_wq_gate, L0_ffn_s_gate, L0_ffn_wq_up, L0_ffn_s_up,
           L0_ffn_wq_down, L0_ffn_s_down):
    x = np.ascontiguousarray(np.asarray(L0_attn_x, dtype=np.float32)
                             .reshape(8192, D))
    C = np.ascontiguousarray
    lam4 = np.concatenate([
        np.asarray(L0_attn_lambda_q1, np.float32).reshape(1, 64),
        np.asarray(L0_attn_lambda_k1, np.float32).reshape(1, 64),
        np.asarray(L0_attn_lambda_q2, np.float32).reshape(1, 64),
        np.asarray(L0_attn_lambda_k2, np.float32).reshape(1, 64)], axis=1)
    shared = {
        "wqT": C(np.asarray(L0_attn_Wq, np.float32).T.astype(BF)),
        "wkT": C(np.asarray(L0_attn_Wk, np.float32).T.astype(BF)),
        "wvT": C(np.asarray(L0_attn_Wv, np.float32).T.astype(BF)),
        "woT": C(np.asarray(L0_attn_Wo, np.float32).T.astype(BF)),
        "wg8": C(np.asarray(L0_ffn_wq_gate).reshape(NHC, 128, 16, 128)
                 .transpose(0, 3, 2, 1).reshape(NHC, 128, D).astype(np.uint8)),
        "wu8": C(np.asarray(L0_ffn_wq_up).reshape(NHC, 128, 16, 128)
                 .transpose(0, 3, 2, 1).reshape(NHC, 128, D).astype(np.uint8)),
        "wd8": C(np.asarray(L0_ffn_wq_down).T.astype(np.uint8)),
        "sgpp": C(np.pad(np.asarray(L0_ffn_s_gate, np.float32).reshape(NHC, 128).T, ((0, 0), (0, 4)))),
        "supp": C(np.pad(np.asarray(L0_ffn_s_up, np.float32).reshape(NHC, 128).T, ((0, 0), (0, 4)))),
        "sdb": C(np.broadcast_to(np.asarray(L0_ffn_s_down, np.float32)[None, :],
                                 (128, D))),
        "lnw": C(np.pad(np.asarray(L0_attn_ln_w, np.float32).reshape(128, 1), ((0, 0), (0, 7)))),
        "lam4": C(lam4),
        "ident_b": np.eye(128, dtype=BF),
        "ones_b": np.ones((128, 128), dtype=BF),
        "zrow_b": np.zeros((1, 512), dtype=BF),
        "onesrow_b": np.ones((1, 512), dtype=BF),
        "onescol_f": np.ones((1, 128), dtype=np.float32),
        "onescol128": np.ones((128, 1), dtype=np.float32),
        "cbias": np.broadcast_to(np.array(
            [EPS, -1.0, -LAMBDA_INIT, 0.0, 0.0, 0.0, 0.0, 0.0],
            dtype=np.float32)[None, :], (128, 8)).copy(),
        "masks": _masks(),
    }
    in_maps = []
    for c in range(NCORES):
        bA, bB = c, 15 - c
        m = dict(shared)
        m["x_own"] = C(np.concatenate(
            [x[512 * bA:512 * (bA + 1)], x[512 * bB:512 * (bB + 1)]], axis=0))
        m["itab"] = _item_table(c)
        in_maps.append(m)

    if "nc" not in _NC_CACHE:
        _NC_CACHE["nc"] = build_nc()
    nc = _NC_CACHE["nc"]
    res = run_bass_kernel_spmd(nc, in_maps, core_ids=list(range(NCORES)))

    out = np.empty((8192, D), dtype=np.float32)
    for c in range(NCORES):
        y = np.asarray(res.results[c]["y"], dtype=np.float32)
        out[512 * c:512 * (c + 1)] = y[0:512]
        out[512 * (15 - c):512 * (16 - c)] = y[512:1024]
    return out.reshape(1, 8192, D)
